# version 07
# baseline (speedup 1.0000x reference)
import numpy as np
import ml_dtypes
import concourse.bass as bass
import concourse.mybir as mybir
from concourse.bass_utils import run_bass_kernel_spmd
from contextlib import ExitStack

F32 = mybir.dt.float32
BF16 = mybir.dt.bfloat16
I32 = mybir.dt.int32
AF = mybir.ActivationFunctionType
ALU = mybir.AluOpType
AX = mybir.AxisListType

BF = ml_dtypes.bfloat16

NCORES = 8
B, S, D = 2, 8192, 1024
T = B * S
TC = T // NCORES
NT = TC // 128
EPS = 1e-5

ENGS = ("sync", "scalar", "vector", "tensor", "gpsimd")
AUTODRAIN = True


class _Sem:
    __slots__ = ("h", "n", "name")

    def __init__(self, h, name):
        self.h = h
        self.n = 0
        self.name = name


class Em:

    def __init__(self, nc, stack):
        self.nc = nc
        self.stack = stack
        self.q = {e: [] for e in ENGS}
        self._semi = 0

    def sem(self, name=None):
        self._semi += 1
        name = name or f"s{self._semi}"
        if not hasattr(self, "_sems"):
            self._sems = {}
        if name in self._sems:
            return self._sems[name]
        h = self.stack.enter_context(self.nc.semaphore(name))
        s = _Sem(h, name)
        self._sems[name] = s
        return s

    def barrier(self):
        for e in ENGS:
            for s in self._sems.values():
                self.wait(e, s, s.n)

    def op(self, eng, method, *args, waits=(), inc=None, incv=1, **kw):
        wl = []
        for w in waits:
            if isinstance(w, tuple):
                wl.append((w[0], w[1]))
            else:
                wl.append((w, w.n))
        if inc is not None:
            inc.n += incv

        drain = (AUTODRAIN and eng in ("vector", "scalar", "gpsimd")
                 and method not in ("dma_start", "collective_compute"))

        def fn(e, method=method, args=args, kw=kw, wl=tuple(wl), inc=inc,
               incv=incv, drain=drain):
            for s, v in wl:
                if v > 0:
                    e.wait_ge(s.h, v)
            r = getattr(e, method)(*args, **kw)
            if inc is not None:
                if method == "collective_compute":
                    r.then_inc(inc.h)
                else:
                    r.then_inc(inc.h, incv)
            if drain:
                e.drain()

        self.q[eng].append(fn)
        return inc.n if inc is not None else None

    def wait(self, eng, sem, val=None):
        v = sem.n if val is None else val
        if v > 0:
            self.q[eng].append(lambda e, s=sem, v=v: e.wait_ge(s.h, v))

    def raw(self, eng, fn):
        self.q[eng].append(fn)

    def dma(self, eng, out, in_, sem, waits=()):
        return self.op(eng, "dma_start", waits=waits, inc=sem, incv=16,
                       out=out, in_=in_)

    def finalize(self, block):
        for e in ENGS:
            if self.q[e]:
                getattr(block, e)(
                    lambda eng, ops=tuple(self.q[e]): [f(eng) for f in ops])


def emit_ln_tile(em, x_ap, zbf_ap, scratch, cols, s_act, s_dve, eps_col,
                 gain_ap=None, bias_ap=None, zf32_ap=None, out_waits=(),
                 in_waits=()):
    sum_c = cols[:, 0:1]
    ssq_c = cols[:, 1:2]
    mean_c = cols[:, 2:3]
    var_c = cols[:, 3:4]
    rstd_c = cols[:, 4:5]
    em.op("scalar", "activation", scratch, x_ap, AF.Identity,
          accum_out=sum_c, waits=tuple(in_waits))
    em.op("scalar", "activation", scratch, x_ap, AF.Square,
          accum_out=ssq_c, inc=s_act)
    em.op("vector", "tensor_scalar_mul", mean_c, sum_c, 1.0 / D,
          waits=(s_act,))
    em.op("vector", "tensor_scalar", var_c, ssq_c, 1.0 / D, None, ALU.mult)
    em.op("vector", "scalar_tensor_tensor", var_c, mean_c, mean_c, var_c,
          ALU.mult, ALU.subtract, inc=s_dve)
    em.op("scalar", "activation", var_c, var_c, AF.Ln, scale=-1.0,
          bias=eps_col, waits=(s_dve,))
    em.op("scalar", "activation", rstd_c, var_c, AF.Exp, scale=-0.5,
          inc=s_act)
    if gain_ap is None:
        first = zbf_ap if zbf_ap is not None else zf32_ap
        em.op("vector", "tensor_scalar", first, x_ap, mean_c, rstd_c,
              ALU.subtract, ALU.mult, waits=(s_act,) + tuple(out_waits),
              inc=s_dve)
        if zbf_ap is not None and zf32_ap is not None:
            em.op("vector", "tensor_scalar", zf32_ap, x_ap, mean_c, rstd_c,
                  ALU.subtract, ALU.mult, inc=s_dve)
    else:
        assert zf32_ap is not None
        em.op("vector", "tensor_scalar", zf32_ap, x_ap, mean_c, rstd_c,
              ALU.subtract, ALU.mult, waits=(s_act,) + tuple(out_waits))
        em.op("vector", "tensor_mul", zf32_ap, zf32_ap, gain_ap)
        if bias_ap is not None:
            em.op("vector", "tensor_add", zf32_ap, zf32_ap, bias_ap)
        em.op("vector", "tensor_copy", zbf_ap, zf32_ap, inc=s_dve)


def emit_transpose8(em, zbf_view, zT_view, pst, ident, s_in, s_pe, s_ev,
                    nchunk=8, evac_eng="vector"):
    for c in range(nchunk):
        em.op("tensor", "transpose", pst[:, 0:128],
              zbf_view[:, c * 128:(c + 1) * 128],
              ident, waits=((s_in, s_in.n), (s_ev, s_ev.n)), inc=s_pe)
        em.op(evac_eng, "tensor_copy" if evac_eng == "vector" else "copy",
              zT_view[:, c * 128:(c + 1) * 128], pst[:, 0:128],
              waits=(s_pe,), inc=s_ev)


_PROGS = {}


def _get_prog(name, builder):
    if name not in _PROGS:
        _PROGS[name] = builder()
    return _PROGS[name]


def _launch(name, builder, in_maps):
    nc = _get_prog(name, builder)
    res = run_bass_kernel_spmd(nc, in_maps, core_ids=list(range(NCORES)))
    return res.results


IDENT_BF = np.eye(128, dtype=BF)



class _FX:

    def __init__(self):
        self.nc = bass.Bass("TRN2", target_bir_lowering=False)
        self._gs = ExitStack()
        self._gs.__enter__()
        self.em = Em(self.nc, self._gs)
        self.prefix = ""
        self.io = {}
        self.alias = {}

    def get(self, name, shape, dtype, kind):
        key = self.alias.get(name, name)
        if key not in self.io:
            if kind == "Internal":
                self.io[key] = self.nc.dram_tensor(key, shape, dtype).ap()
            else:
                self.io[key] = self.nc.dram_tensor(key, shape, dtype,
                                                   kind=kind).ap()
        return self.io[key]

    def close(self):
        with self.nc.Block() as block:
            self.em.finalize(block)
        self._gs.__exit__(None, None, None)
        return self.nc


def _fx_open(fx):
    if fx is None:
        return bass.Bass("TRN2", target_bir_lowering=False), None, "", True
    return fx.nc, fx.em, fx.prefix, False


def _fio(fx, nc, name, shape, dtype, kind):
    if fx is None:
        return nc.dram_tensor(name, shape, dtype, kind=kind).ap()
    return fx.get(name, shape, dtype, kind)


def build_mbuild(fx=None):
    nc, _em0, pre, _own = _fx_open(fx)
    w1_d = _fio(fx, nc, "w1", [D, 4 * D], F32, "ExternalInput")
    w2_d = _fio(fx, nc, "w2", [4 * D, D], F32, "ExternalInput")
    id_d = _fio(fx, nc, "ident", [128, 128], BF16, "ExternalInput")
    m_d = _fio(fx, nc, "m", [D, D], F32, "ExternalOutput")
    NK = 4 * D // 128

    with ExitStack() as stack:
        em = _em0 if _em0 is not None else Em(nc, stack)
        sb = lambda nm, sh, dt=F32: stack.enter_context(nc.sbuf_tensor(pre + nm, sh, dt))
        ps = lambda nm, sh, dt=F32: stack.enter_context(nc.psum_tensor(pre + nm, sh, dt))

        w1_sb = sb("w1_sb", [128, 8 * 4 * D], BF16)
        w1t_sb = sb("w1t_sb", [128, NK * 128], BF16)
        w2_sb = sb("w2_sb", [128, NK * D], BF16)
        id_sb = sb("id_sb", [128, 128], BF16)
        o_sb = sb("o_sb", [128, D])
        pst = ps("pst", [128, 1024], BF16)
        po = [ps(f"po{j}", [128, 512]) for j in range(2)]

        ld = em.sem("ld")
        ldg = em.sem("ldg")
        s_pe = em.sem("pe")
        s_ev = em.sem("ev")
        s_mm = em.sem("mm")
        s_st = em.sem("st")

        em.dma("gpsimd", w1_sb[:, :].rearrange("p (c n) -> p c n", c=8),
               w1_d.rearrange("(c p) n -> p c n", p=128), ldg)
        em.dma("gpsimd", w2_sb[:, :].rearrange("p (k n) -> p k n", k=NK),
               w2_d.rearrange("(k p) n -> p k n", p=128), ldg)
        em.dma("sync", id_sb[:, :], id_d, ld)
        ldv0, ldgv0 = ld.n, ldg.n

        for i in range(8):
            for k in range(NK):
                em.op("tensor", "transpose", pst[:, 0:128],
                      w1_sb[:, :].rearrange("p (c n) -> p c n", c=8)[:, i, 128 * k:128 * k + 128],
                      id_sb[:, :],
                      waits=((ld, ldv0), (ldg, ldgv0), (s_ev, s_ev.n)),
                      inc=s_pe)
                em.op("vector", "tensor_copy", w1t_sb[:, 128 * k:128 * k + 128],
                      pst[:, 0:128], waits=(s_pe,), inc=s_ev)
            for j in range(2):
                for k in range(NK):
                    em.op("tensor", "matmul", po[j][:, :],
                          w1t_sb[:, 128 * k:128 * k + 128],
                          w2_sb[:, :].rearrange("p (k n) -> p k n", k=NK)[:, k, 512 * j:512 * j + 512],
                          start=(k == 0), stop=(k == NK - 1),
                          waits=((s_ev, s_ev.n), (s_st, s_st.n)) if k == 0 else (),
                          inc=(s_mm if k == NK - 1 else None))
                em.op("vector", "tensor_copy", o_sb[:, 512 * j:512 * j + 512],
                      po[j][:, :], waits=((s_mm, s_mm.n), (s_st, s_st.n)), inc=s_ev)
            em.wait("sync", s_ev)
            em.dma("sync", m_d[128 * i:128 * i + 128, :], o_sb[:, :], s_st)
        em.wait("gpsimd", s_st, s_st.n)

        if _own:
            with nc.Block() as block:
                em.finalize(block)
    return nc


def build_msum(fx=None):
    nc, _em0, pre, _own = _fx_open(fx)
    mall_d = _fio(fx, nc, "mall", [NCORES, D, D], F32, "ExternalInput")
    mh_d = _fio(fx, nc, "mh", [D, D], BF16, "ExternalOutput")

    with ExitStack() as stack:
        em = _em0 if _em0 is not None else Em(nc, stack)
        acc = stack.enter_context(nc.sbuf_tensor("acc", [128, 8 * D], F32))
        cur = stack.enter_context(nc.sbuf_tensor("cur", [128, 8 * D], F32))
        obf = stack.enter_context(nc.sbuf_tensor("obf", [128, 8 * D], BF16))
        ld = em.sem("ld")
        s_dve = em.sem("dve")
        s_st = em.sem("st")

        accv = acc[:, :].rearrange("p (c n) -> p c n", c=8)
        curv = cur[:, :].rearrange("p (c n) -> p c n", c=8)
        em.dma("sync", accv, mall_d[0].rearrange("(c p) n -> p c n", p=128), ld)
        for i in range(1, NCORES):
            em.dma("sync", curv, mall_d[i].rearrange("(c p) n -> p c n", p=128),
                   ld, waits=((s_dve, s_dve.n),))
            em.op("vector", "tensor_add", acc[:, :], acc[:, :], cur[:, :],
                  waits=((ld, ld.n),), inc=s_dve)
        em.op("vector", "tensor_scalar_mul", obf[:, :], acc[:, :], 0.5,
              inc=s_dve)
        em.wait("gpsimd", s_dve)
        em.dma("gpsimd", mh_d.rearrange("(c p) n -> p c n", p=128),
               obf[:, :].rearrange("p (c n) -> p c n", c=8), s_st)
        em.wait("gpsimd", s_st, s_st.n)

        if _own:
            with nc.Block() as block:
                em.finalize(block)
    return nc


def build_ffn0(fx=None):
    nc, _em0, pre, _own = _fx_open(fx)
    x_d = _fio(fx, nc, "x", [TC, D], F32, "ExternalInput")
    mh_d = _fio(fx, nc, "mh", [D, D], BF16, "ExternalInput")
    id_d = _fio(fx, nc, "ident", [128, 128], BF16, "ExternalInput")
    xo_d = _fio(fx, nc, "xo", [TC, D], F32, "ExternalOutput")

    with ExitStack() as stack:
        em = _em0 if _em0 is not None else Em(nc, stack)
        sb = lambda nm, sh, dt=F32: stack.enter_context(nc.sbuf_tensor(pre + nm, sh, dt))
        ps = lambda nm, sh, dt=F32: stack.enter_context(nc.psum_tensor(pre + nm, sh, dt))

        x_sb = sb("x_sb", [128, NT * D])
        m_sb = sb("m_sb", [128, 8 * D], BF16)
        id_sb = sb("id_sb", [128, 128], BF16)
        zbf = [sb(f"zbf{i}", [128, D], BF16) for i in range(2)]
        zT = [sb(f"zT{i}", [128, D], BF16) for i in range(2)]
        scr = [sb(f"scr{i}", [128, D]) for i in range(2)]
        cols = [sb(f"cols{i}", [128, 8]) for i in range(2)]
        pst = ps("pst", [128, 1024], BF16)
        po = [ps(f"po{j}", [128, 512]) for j in range(2)]

        kc = sb("kc", [128, 8])

        ld = em.sem("ld")
        s_act = em.sem("act")
        s_dve = em.sem("dve")
        s_pe = em.sem("pe")
        s_ev = em.sem("ev")
        s_mm = em.sem("mm")
        s_res = em.sem("res")
        s_st = em.sem("st")

        xv = x_sb[:, :].rearrange("p (t d) -> p t d", t=NT)
        mv = m_sb[:, :].rearrange("p (c d) -> p c d", c=8)

        ldg = em.sem("ldg")
        em.dma("sync", xv, x_d.rearrange("(t p) d -> p t d", p=128), ld)
        em.dma("sync", id_sb[:, :], id_d, ld)
        em.dma("gpsimd", mv, mh_d.rearrange("(c p) d -> p c d", p=128), ldg)
        ldv = ld.n
        ldgv = ldg.n
        em.op("vector", "memset", kc[:, :], 0.0)
        em.op("vector", "memset", kc[:, 0:1], EPS, inc=s_dve)
        em.wait("scalar", s_dve, s_dve.n)
        em.wait("tensor", ld, ldv)
        em.wait("tensor", ldg, ldgv)

        for t in range(NT):
            i = t % 2
            em.wait("scalar", ld, ldv)
            em.wait("scalar", ldg, ldgv)
            war = ((s_pe, 8 * (t - 1)),) if t >= 2 else ()
            emit_ln_tile(em, xv[:, t], zbf[i][:, :], scr[i][:, :],
                         cols[i][:, :], s_act, s_dve, kc[:, 0:1],
                         out_waits=war)
            emit_transpose8(em, zbf[i][:, :], zT[i][:, :], pst, id_sb[:, :],
                            s_dve, s_pe, s_ev)
            for j in range(2):
                for c in range(8):
                    em.op("tensor", "matmul", po[j][:, :],
                          zT[i][:, 128 * c:128 * c + 128],
                          mv[:, c, 512 * j:512 * j + 512],
                          start=(c == 0), stop=(c == 7),
                          waits=((ld, ldv), (ldg, ldgv), (s_ev, s_ev.n),
                                 (s_res, s_res.n)) if c == 0 else (),
                          inc=(s_mm if c == 7 else None))
                em.op("vector", "tensor_add", xv[:, t, 512 * j:512 * j + 512],
                      xv[:, t, 512 * j:512 * j + 512], po[j][:, :],
                      waits=(s_mm,), inc=s_res)
        em.wait("sync", s_res)
        em.dma("sync", xo_d.rearrange("(t p) d -> p t d", p=128), xv, s_st)
        em.wait("gpsimd", s_st, s_st.n)

        if _own:
            with nc.Block() as block:
                em.finalize(block)
    return nc


def build_l2ffn(fx=None):
    nc, _em0, pre, _own = _fx_open(fx)
    x_d = _fio(fx, nc, "x", [TC, D], F32, "ExternalInput")
    g_d = _fio(fx, nc, "gamma", [1, D], F32, "ExternalInput")
    b_d = _fio(fx, nc, "beta", [1, D], F32, "ExternalInput")
    q1_d = _fio(fx, nc, "q1", [D, 4 * D], I32, "ExternalInput")
    s1_d = _fio(fx, nc, "s1", [8, 4 * D], F32, "ExternalInput")
    b1_d = _fio(fx, nc, "b1", [128, 32], F32, "ExternalInput")
    q2_d = _fio(fx, nc, "q2", [4 * D, D], I32, "ExternalInput")
    s2_d = _fio(fx, nc, "s2", [32, D], F32, "ExternalInput")
    b2_d = _fio(fx, nc, "b2", [1, D], F32, "ExternalInput")
    id_d = _fio(fx, nc, "ident", [128, 128], BF16, "ExternalInput")
    xo_d = _fio(fx, nc, "xo", [TC, D], F32, "ExternalOutput")

    HNT = NT // 2

    with ExitStack() as stack:
        em = _em0 if _em0 is not None else Em(nc, stack)
        sb = lambda nm, sh, dt=F32: stack.enter_context(nc.sbuf_tensor(pre + nm, sh, dt))
        ps = lambda nm, sh, dt=F32: stack.enter_context(nc.psum_tensor(pre + nm, sh, dt))

        w1 = sb("w1", [128, 8 * 4 * D], BF16)
        w2 = sb("w2", [128, 32 * D], BF16)
        hT = sb("hT", [128, HNT * D], BF16)
        yT = sb("yT", [128, 32 * 128], BF16)
        xt = [sb(f"xt{i}", [128, D]) for i in range(2)]
        uz = sb("uz", [128, D])
        zscr = sb("zscr", [128, D])
        zbf = sb("zbf", [128, D], BF16)
        stats = sb("stats", [128, 8 * NT])
        cols2 = sb("cols2", [128, 8])
        gbc = sb("gbc", [128, D])
        bbc = sb("bbc", [128, D])
        srow = [sb(f"srow{i}", [1, 2048]) for i in range(2)]
        b1c = sb("b1c", [128, 32])
        b2row = sb("b2row", [1, D], BF16)
        ones_f = sb("ones_f", [1, 128])
        ones_b = sb("ones_b", [1, 128], BF16)
        id_sb = sb("id_sb", [128, 128], BF16)
        kc = sb("kc", [128, 8])

        scr = sb("scr", [128, D], BF16)
        pst = ps("pst", [128, 1024], BF16)
        pbc = ps("pbc", [128, 512])
        pb1 = [ps(f"pb1{i}", [128, 512]) for i in range(2)]
        pb2 = [ps(f"pb2{i}", [128, 512]) for i in range(2)]

        ld = em.sem("ld")
        lds = em.sem("lds")
        ldx = [em.sem("ldx0"), em.sem("ldx1")]
        s_act = em.sem("act")
        s_dve = em.sem("dve")
        s_pe = em.sem("pe")
        s_ev = em.sem("ev")
        s_mm = em.sem("mm")
        s_stp = [em.sem("st0"), em.sem("st1")]
        s_upd = em.sem("upd")

        w1v = w1[:, :].rearrange("p (c n) -> p c n", c=8)
        w2v = w2[:, :].rearrange("p (c n) -> p c n", c=32)
        hTv = hT[:, :].rearrange("p (t n) -> p t n", t=HNT)
        xv_t = x_d.rearrange("(t p) d -> t p d", p=128)
        xov_t = xo_d.rearrange("(t p) d -> t p d", p=128)

        em.op("vector", "memset", kc[:, :], 0.0)
        em.op("vector", "memset", kc[:, 0:1], EPS)
        em.op("vector", "memset", ones_f[:, :], 1.0)
        em.op("vector", "memset", ones_b[:, :], 1.0, inc=s_dve)
        em.dma("sync", srow[0][0:1, 0:D], g_d, lds)
        em.dma("sync", srow[1][0:1, 0:D], b_d, lds)
        em.dma("sync", b1c[:, :], b1_d, ld)
        em.dma("sync", id_sb[:, :], id_d, ld)
        ldg = em.sem("ldg")
        em.dma("gpsimd", b2row[:, :], b2_d, ldg)
        em.dma("gpsimd", w1v, q1_d.rearrange("(c p) n -> p c n", p=128), ldg)
        em.dma("gpsimd", w2v, q2_d.rearrange("(c p) n -> p c n", p=128), ldg)
        ldv = ld.n
        ldgv = ldg.n

        em.wait("tensor", ld, ldv)
        for dst, buf in ((gbc, srow[0]), (bbc, srow[1])):
            for j in range(2):
                em.op("tensor", "matmul", pbc[:, :], ones_f[0:1, :],
                      buf[0:1, 512 * j:512 * j + 512],
                      start=True, stop=True,
                      waits=((lds, lds.n), (s_dve, s_dve.n), (s_ev, s_ev.n)),
                      inc=s_mm)
                em.op("vector", "tensor_copy", dst[:, 512 * j:512 * j + 512],
                      pbc[:, :], waits=(s_mm,), inc=s_ev)
        gb_mm = s_mm.n

        pieces = []
        for c in range(8):
            for h in range(2):
                pieces.append(("w1", c, h))
        for c in range(32):
            pieces.append(("w2", c, 0))
        for idx, (wname, c, h) in enumerate(pieces):
            buf = srow[idx % 2]
            ncols = 2048 if wname == "w1" else 1024
            if wname == "w1":
                src_ap = s1_d[c:c + 1, 2048 * h:2048 * h + 2048]
            else:
                src_ap = s2_d[c:c + 1, :]
            em.dma("sync", buf[0:1, 0:ncols], src_ap, lds,
                   waits=((s_mm, s_mm.n),))
            for j in range(ncols // 512):
                em.op("tensor", "matmul", pbc[:, :], ones_f[0:1, :],
                      buf[0:1, 512 * j:512 * j + 512],
                      start=True, stop=True,
                      waits=((lds, lds.n), (ldg, ldgv), (s_ev, s_ev.n)),
                      inc=s_mm)
                wv = w1v if wname == "w1" else w2v
                off = (2048 * h if wname == "w1" else 0) + 512 * j
                em.op("vector", "scalar_tensor_tensor",
                      wv[:, c, off:off + 512], wv[:, c, off:off + 512],
                      -128.0, pbc[:, :], ALU.add, ALU.mult,
                      waits=(s_mm,), inc=s_ev)
        deq_ev = s_ev.n

        for half in range(2):
            t0 = half * HNT
            st_half = (s_stp[0].n, s_stp[1].n)
            ht_ev = []
            for tt in range(HNT):
                t = t0 + tt
                i = t % 2
                em.dma("sync", xt[i][:, :], xv_t[t], ldx[i],
                       waits=((s_dve, s_dve.n), (s_stp[0], st_half[0]),
                              (s_stp[1], st_half[1])))
                em.wait("scalar", ldx[i], ldx[i].n)
                st_cols = stats[:, 8 * t:8 * t + 8]
                emit_ln_tile(em, xt[i][:, :], None, scr[:, :], st_cols,
                             s_act, s_dve, kc[:, 0:1], zf32_ap=zscr[:, :])
                emit_ln_tile(em, zscr[:, :], zbf[:, :], scr[:, :],
                             cols2[:, :], s_act, s_dve, kc[:, 0:1],
                             gain_ap=gbc[:, :], bias_ap=bbc[:, :],
                             zf32_ap=uz[:, :], out_waits=((s_pe, s_pe.n),),
                             in_waits=((s_dve, s_dve.n),))
                emit_transpose8(em, zbf[:, :], hTv[:, tt], pst, id_sb[:, :],
                                s_dve, s_pe, s_ev)
                ht_ev.append(s_ev.n)
            phaseA_dve = s_dve.n

            sact_pack = getattr(build_l2ffn, "_dummy", None) or []
            sact_pack = []
            for tt in range(HNT):
                t = t0 + tt
                i = t % 2
                em.dma("sync", xt[i][:, :], xv_t[t], ldx[i],
                       waits=((s_upd, s_upd.n), (s_stp[i], s_stp[i].n))
                             if tt >= 2
                             else ((s_dve, phaseA_dve),
                                   (s_stp[0], st_half[0]),
                                   (s_stp[1], st_half[1])))
                ldxv = ldx[i].n
                for p in range(8):
                    bk = p % 2
                    gp = len(sact_pack)
                    for u in range(4):
                        nb = 4 * p + u
                        for c in range(8):
                            first = (u == 0 and c == 0)
                            w = ()
                            if first:
                                w = ((s_ev, max(deq_ev, ht_ev[tt])),)
                                if gp >= 2:
                                    w = w + ((s_act, sact_pack[gp - 2]),)
                            em.op("tensor", "matmul",
                                  pb1[bk][:, 128 * u:128 * u + 128],
                                  w1v[:, c, 128 * nb:128 * nb + 128],
                                  hTv[:, tt, 128 * c:128 * c + 128],
                                  start=first, stop=(u == 3 and c == 7),
                                  waits=w,
                                  inc=(s_mm if (u == 3 and c == 7) else None))
                    for u in range(4):
                        nb = 4 * p + u
                        em.op("scalar", "activation",
                              yT[:, 128 * nb:128 * nb + 128],
                              pb1[bk][:, 128 * u:128 * u + 128], AF.Silu,
                              bias=b1c[:, nb:nb + 1],
                              waits=(s_mm,) if u == 0 else (), inc=s_act)
                    sact_pack.append(s_act.n)
                silu_done = s_act.n
                for j in range(2):
                    for c in range(32):
                        em.op("tensor", "matmul", pb2[j][:, :],
                              yT[:, 128 * c:128 * c + 128],
                              w2v[:, c, 512 * j:512 * j + 512],
                              start=(c == 0), stop=False,
                              waits=((s_act, silu_done), (s_upd, s_upd.n))
                                    if c == 0 else ())
                    em.op("tensor", "matmul", pb2[j][:, :], ones_b[0:1, :],
                          b2row[0:1, 512 * j:512 * j + 512],
                          start=False, stop=True, inc=s_mm)
                fc2_done = s_mm.n
                em.op("vector", "tensor_scalar", uz[:, :], xt[i][:, :],
                      stats[:, 8 * t + 2:8 * t + 3],
                      stats[:, 8 * t + 4:8 * t + 5],
                      ALU.subtract, ALU.mult,
                      waits=((ldx[i], ldxv), (s_mm, fc2_done)))
                for j in range(2):
                    em.op("vector", "scalar_tensor_tensor",
                          xt[i][:, 512 * j:512 * j + 512], pb2[j][:, :],
                          0.25, xt[i][:, 512 * j:512 * j + 512],
                          ALU.mult, ALU.add)
                em.op("vector", "scalar_tensor_tensor", xt[i][:, :],
                      uz[:, :], 0.5, xt[i][:, :], ALU.mult, ALU.add,
                      inc=s_upd)
                em.dma("sync", xov_t[t], xt[i][:, :], s_stp[i],
                       waits=((s_upd, s_upd.n),))
        em.wait("gpsimd", s_stp[0], s_stp[0].n)
        em.wait("gpsimd", s_stp[1], s_stp[1].n)

        if _own:
            with nc.Block() as block:
                em.finalize(block)
    return nc


def step_l2ffn(x_parts, ins):
    b1c = np.ascontiguousarray(ins["L2_ffn_fc1_bias"].reshape(32, 128).T)
    common = {
        "gamma": ins["L2_ffn_ln_gamma"].reshape(1, D),
        "beta": ins["L2_ffn_ln_beta"].reshape(1, D),
        "q1": ins["L2_ffn_fc1_q"],
        "s1": ins["L2_ffn_fc1_scales"],
        "b1": b1c,
        "q2": ins["L2_ffn_fc2_q"],
        "s2": ins["L2_ffn_fc2_scales"],
        "b2": ins["L2_ffn_fc2_bias"].reshape(1, D),
        "ident": IDENT_BF,
    }
    res = _launch("l2ffn", build_l2ffn,
                  [dict(common, x=x_parts[c]) for c in range(NCORES)])
    return [np.asarray(res[c]["xo"]) for c in range(NCORES)]


def build_l3ffn(fx=None):
    nc, _em0, pre, _own = _fx_open(fx)
    x_d = _fio(fx, nc, "x", [TC, D], F32, "ExternalInput")
    rw_d = _fio(fx, nc, "rw", [D, 8], F32, "ExternalInput")
    rb_d = _fio(fx, nc, "rb", [1, 8], F32, "ExternalInput")
    we_d = _fio(fx, nc, "we", [8, D, D], F32, "ExternalInput")
    be_d = _fio(fx, nc, "be", [1, 8 * D], F32, "ExternalInput")
    c8_d = _fio(fx, nc, "cols8m", [128, 8], F32, "ExternalInput")
    idf_d = _fio(fx, nc, "identf", [128, 128], F32, "ExternalInput")
    xo_d = _fio(fx, nc, "xo", [TC, D], F32, "ExternalOutput")

    with ExitStack() as stack:
        em = _em0 if _em0 is not None else Em(nc, stack)
        sb = lambda nm, sh, dt=F32: stack.enter_context(nc.sbuf_tensor(pre + nm, sh, dt))
        ps = lambda nm, sh, dt=F32: stack.enter_context(nc.psum_tensor(pre + nm, sh, dt))

        wexp = sb("wexp", [128, 64 * D], BF16)
        rw_sb = sb("rw_sb", [128, 8 * 8])
        rb_sb = sb("rb_sb", [1, 8])
        be_sb = sb("be_sb", [1, 8 * D], BF16)
        c8_sb = sb("c8_sb", [128, 8])
        idf_sb = sb("idf_sb", [128, 128])
        xt = [sb(f"xt{i}", [128, D]) for i in range(2)]
        zf = sb("zf", [128, D])
        zTf = sb("zTf", [128, D])
        zTb = sb("zTb", [128, D], BF16)
        acc = sb("acc", [128, D])
        scr = sb("scr", [128, D], BF16)
        cols = sb("cols", [128, 8])
        lg = sb("lg", [128, 16])
        eq = sb("eq", [128, 8])
        coefs = sb("coefs", [128, 8])
        ones_f = sb("ones_f", [1, 128])
        ones_b = sb("ones_b", [1, 128], BF16)
        kc = sb("kc", [128, 8])

        pstf = ps("pstf", [128, 512])
        prt = ps("prt", [128, 512])
        pex = [ps(f"pex{i}", [128, 512]) for i in range(4)]

        ld = em.sem("ld")
        ldg = em.sem("ldg")
        ldx = [em.sem("ldx0"), em.sem("ldx1")]
        s_act = em.sem("act")
        s_dve = em.sem("dve")
        s_pe = em.sem("pe")
        s_ev = em.sem("ev")
        s_mm = em.sem("mm")
        s_stp = [em.sem("st0"), em.sem("st1")]

        wv = wexp[:, :].rearrange("p (e c n) -> p e c n", e=8, c=8)
        rwv = rw_sb[:, :].rearrange("p (c e) -> p c e", c=8)
        xv_t = x_d.rearrange("(t p) d -> t p d", p=128)
        xov_t = xo_d.rearrange("(t p) d -> t p d", p=128)

        em.op("vector", "memset", kc[:, :], 0.0)
        em.op("vector", "memset", kc[:, 0:1], EPS)
        em.op("vector", "memset", ones_f[:, :], 1.0)
        em.op("vector", "memset", ones_b[:, :], 1.0, inc=s_dve)
        em.dma("sync", rw_sb[:, :].rearrange("p (c e) -> p c e", c=8),
               rw_d.rearrange("(c p) e -> p c e", p=128), ld)
        em.dma("sync", rb_sb[:, :], rb_d, ld)
        em.dma("sync", c8_sb[:, :], c8_d, ld)
        em.dma("sync", idf_sb[:, :], idf_d, ld)
        em.dma("gpsimd", be_sb[:, :], be_d, ldg)
        em.dma("gpsimd", wv, we_d.rearrange("e (c p) n -> p e c n", p=128), ldg)
        ldv, ldgv = ld.n, ldg.n
        em.wait("tensor", ld, ldv)
        em.wait("tensor", ldg, ldgv)
        em.wait("scalar", s_dve, s_dve.n)

        stt_ev = []
        for t in range(NT):
            i = t % 2
            em.dma("sync", xt[i][:, :], xv_t[t], ldx[i],
                   waits=((s_dve, s_dve.n), (s_stp[i], s_stp[i].n)))
            ldxv = ldx[i].n
            em.wait("scalar", ldx[i], ldxv)
            emit_ln_tile(em, xt[i][:, :], None, scr[:, :], cols[:, :],
                         s_act, s_dve, kc[:, 0:1], zf32_ap=zf[:, :],
                         out_waits=((s_pe, s_pe.n),))
            emit_transpose8(em, zf[:, :], zTf[:, :], pstf, idf_sb[:, :],
                            s_dve, s_pe, s_ev)
            em.op("vector", "tensor_copy", zTb[:, :], zTf[:, :],
                  waits=((s_ev, s_ev.n), (s_mm, s_mm.n)), inc=s_dve)
            ztb_dve = s_dve.n
            for c in range(8):
                em.op("tensor", "matmul", prt[:, 0:8],
                      zTf[:, 128 * c:128 * c + 128], rwv[:, c, :],
                      start=(c == 0), stop=False,
                      waits=((s_ev, s_ev.n), (s_act, s_act.n))
                            if c == 0 else ())
            em.op("tensor", "matmul", prt[:, 0:8], ones_f[0:1, :],
                  rb_sb[0:1, :], start=False, stop=True, inc=s_mm)
            em.op("scalar", "copy", lg[:, 0:8], prt[:, 0:8],
                  waits=(s_mm,), inc=s_act)
            em.op("vector", "reduce_max", lg[:, 8:9], lg[:, 0:8], axis=AX.X,
                  waits=(s_act,))
            em.op("vector", "tensor_scalar_mul", lg[:, 9:10], lg[:, 8:9],
                  -1.0, inc=s_dve)
            em.op("scalar", "activation", eq[:, :], lg[:, 0:8], AF.Exp,
                  bias=lg[:, 9:10], accum_out=lg[:, 10:11],
                  waits=(s_dve,), inc=s_act)
            em.op("vector", "reciprocal", lg[:, 11:12], lg[:, 10:11],
                  waits=(s_act,))
            em.op("vector", "tensor_scalar", eq[:, :], lg[:, 0:8],
                  lg[:, 8:9], None, ALU.is_ge)
            em.op("vector", "tensor_mul", eq[:, :], eq[:, :], c8_sb[:, :])
            em.op("vector", "tensor_reduce", lg[:, 12:13], eq[:, :],
                  AX.X, ALU.min)
            em.op("vector", "tensor_scalar", coefs[:, :], c8_sb[:, :],
                  lg[:, 12:13], None, ALU.is_equal)
            em.op("vector", "tensor_scalar", coefs[:, :], coefs[:, :],
                  lg[:, 11:12], 0.5, ALU.mult, ALU.mult, inc=s_dve)
            coef_dve = s_dve.n
            for e in range(8):
                bk = e % 2
                for j in range(2):
                    for c in range(8):
                        w = ()
                        if c == 0:
                            w = ((s_dve, ztb_dve),)
                            if len(stt_ev) >= 2:
                                w = w + ((s_dve, stt_ev[-2]),)
                        em.op("tensor", "matmul", pex[2 * bk + j][:, :],
                              zTb[:, 128 * c:128 * c + 128],
                              wv[:, e, c, 512 * j:512 * j + 512],
                              start=(c == 0), stop=False, waits=w)
                    em.op("tensor", "matmul", pex[2 * bk + j][:, :],
                          ones_b[0:1, :],
                          be_sb[0:1, D * e + 512 * j:D * e + 512 * j + 512],
                          start=False, stop=True, inc=s_mm)
                mm_e = s_mm.n
                for j in range(2):
                    inc = s_dve if j == 1 else None
                    if e == 0:
                        em.op("vector", "tensor_scalar_mul",
                              acc[:, 512 * j:512 * j + 512],
                              pex[2 * bk + j][:, :], coefs[:, e:e + 1],
                              waits=((s_mm, mm_e), (s_dve, coef_dve)),
                              inc=inc)
                    else:
                        em.op("vector", "scalar_tensor_tensor",
                              acc[:, 512 * j:512 * j + 512],
                              pex[2 * bk + j][:, :], coefs[:, e:e + 1],
                              acc[:, 512 * j:512 * j + 512],
                              ALU.mult, ALU.add,
                              waits=((s_mm, mm_e),), inc=inc)
                stt_ev.append(s_dve.n)
            em.op("vector", "tensor_add", xt[i][:, :], xt[i][:, :],
                  acc[:, :], inc=s_dve)
            em.dma("sync", xov_t[t], xt[i][:, :], s_stp[i],
                   waits=((s_dve, s_dve.n),))
        em.wait("gpsimd", s_stp[0], s_stp[0].n)
        em.wait("gpsimd", s_stp[1], s_stp[1].n)

        if _own:
            with nc.Block() as block:
                em.finalize(block)
    return nc


def step_l3ffn(x_parts, ins):
    cols8m = np.ascontiguousarray(
        np.broadcast_to((np.arange(8) - 100.0).astype(np.float32), (128, 8)))
    common = {
        "rw": ins["L3_ffn_router_w"],
        "rb": ins["L3_ffn_router_b"].reshape(1, 8),
        "we": ins["L3_ffn_expert_w"],
        "be": ins["L3_ffn_expert_b"].reshape(1, 8 * D),
        "cols8m": cols8m,
        "identf": np.eye(128, dtype=np.float32),
    }
    res = _launch("l3ffn", build_l3ffn,
                  [dict(common, x=x_parts[c]) for c in range(NCORES)])
    return [np.asarray(res[c]["xo"]) for c in range(NCORES)]


def build_l2attn_proj(fx=None, v_hm=False):
    nc, _em0, pre, _own = _fx_open(fx)
    x_d = _fio(fx, nc, "x", [TC, D], F32, "ExternalInput")
    wqT_d = _fio(fx, nc, "wqT", [D, D], F32, "ExternalInput")
    wkT_d = _fio(fx, nc, "wkT", [D, D], F32, "ExternalInput")
    wvT_d = _fio(fx, nc, "wvT", [D, D], F32, "ExternalInput")
    bq_d = _fio(fx, nc, "bqc", [128, 8], F32, "ExternalInput")
    bk_d = _fio(fx, nc, "bkc", [128, 8], F32, "ExternalInput")
    bv_d = _fio(fx, nc, "bv", [1, D], F32, "ExternalInput")
    id_d = _fio(fx, nc, "ident", [128, 128], BF16, "ExternalInput")
    qt_d = _fio(fx, nc, "qt", [D, TC], BF16, "ExternalOutput")
    kt_d = _fio(fx, nc, "kt", [D, TC], BF16, "ExternalOutput")
    v_d = _fio(fx, nc, "v", [128, NT * D] if v_hm else [TC, D], BF16,
               "ExternalOutput")

    with ExitStack() as stack:
        em = _em0 if _em0 is not None else Em(nc, stack)
        sb = lambda nm, sh, dt=F32: stack.enter_context(nc.sbuf_tensor(pre + nm, sh, dt))
        ps = lambda nm, sh, dt=F32: stack.enter_context(nc.psum_tensor(pre + nm, sh, dt))

        wq = sb("wq", [128, 8 * D], BF16)
        wk = sb("wk", [128, 8 * D], BF16)
        wv = sb("wv", [128, 8 * D], BF16)
        zT = sb("zT", [128, 8 * TC], BF16)
        qt_sb = sb("qt_sb", [128, 8 * TC], BF16)
        kt_sb = sb("kt_sb", [128, 8 * TC], BF16)
        v_sb = sb("v_sb", [128, NT * D], BF16)
        xt = [sb(f"xt{i}", [128, D]) for i in range(2)]
        zbf = [sb(f"zbf{i}", [128, D], BF16) for i in range(2)]
        scr = sb("scr", [128, D], BF16)
        cols = sb("cols", [128, 8])
        bqc = sb("bqc_sb", [128, 8])
        bkc = sb("bkc_sb", [128, 8])
        bvrow = sb("bvrow", [1, D], BF16)
        ones_b = sb("ones_b", [1, 128], BF16)
        id_sb = sb("id_sb", [128, 128], BF16)
        kc = sb("kc", [128, 8])

        pst = ps("pst", [128, 1024], BF16)
        pp = [ps(f"pp{i}", [128, 512]) for i in range(2)]

        ld = em.sem("ld")
        ldg = em.sem("ldg")
        ldx = [em.sem("ldx0"), em.sem("ldx1")]
        s_act = em.sem("act")
        s_dve = em.sem("dve")
        s_pe = em.sem("pe")
        s_ev = em.sem("ev")
        s_mm = em.sem("mm")
        s_st = em.sem("st")

        wqv = wq[:, :].rearrange("p (c n) -> p c n", c=8)
        wkv = wk[:, :].rearrange("p (c n) -> p c n", c=8)
        wvv = wv[:, :].rearrange("p (c n) -> p c n", c=8)
        zTv = zT[:, :].rearrange("p (c t) -> p c t", c=8)
        qtv = qt_sb[:, :].rearrange("p (c t) -> p c t", c=8)
        ktv = kt_sb[:, :].rearrange("p (c t) -> p c t", c=8)
        vv = v_sb[:, :].rearrange("p (t n) -> p t n", t=NT)
        xv_t = x_d.rearrange("(t p) d -> t p d", p=128)

        em.op("vector", "memset", kc[:, :], 0.0)
        em.op("vector", "memset", kc[:, 0:1], EPS)
        em.op("vector", "memset", ones_b[:, :], 1.0, inc=s_dve)
        em.dma("sync", bqc[:, :], bq_d, ld)
        em.dma("sync", bkc[:, :], bk_d, ld)
        em.dma("sync", id_sb[:, :], id_d, ld)
        em.dma("gpsimd", bvrow[:, :], bv_d, ldg)
        em.dma("gpsimd", wqv, wqT_d.rearrange("(c p) n -> p c n", p=128), ldg)
        em.dma("gpsimd", wkv, wkT_d.rearrange("(c p) n -> p c n", p=128), ldg)
        em.dma("gpsimd", wvv, wvT_d.rearrange("(c p) n -> p c n", p=128), ldg)
        ldv, ldgv = ld.n, ldg.n
        em.wait("tensor", ld, ldv)
        em.wait("tensor", ldg, ldgv)
        em.op("vector", "tensor_scalar_mul", bqc[:, :], bqc[:, :], 0.125,
              waits=((ld, ldv),), inc=s_dve)

        for t in range(NT):
            i = t % 2
            em.dma("sync", xt[i][:, :], xv_t[t], ldx[i],
                   waits=((s_dve, s_dve.n),))
            em.wait("scalar", ldx[i], ldx[i].n)
            war = ((s_pe, 8 * (t - 1)),) if t >= 2 else ()
            emit_ln_tile(em, xt[i][:, :], zbf[i][:, :], scr[:, :],
                         cols[:, :], s_act, s_dve, kc[:, 0:1], out_waits=war)
            for c in range(8):
                em.op("tensor", "transpose", pst[:, 0:128],
                      zbf[i][:, 128 * c:128 * c + 128], id_sb[:, :],
                      waits=((s_dve, s_dve.n), (s_ev, s_ev.n)), inc=s_pe)
                em.op("vector", "tensor_copy", zTv[:, c, 128 * t:128 * t + 128],
                      pst[:, 0:128], waits=(s_pe,), inc=s_ev)
        zt_ev = s_ev.n

        NQG = TC // 512
        for qg in range(NQG):
            for which, (wview, dstv, bias_col, scale) in enumerate(
                    ((wqv, qtv, bqc, 0.125), (wkv, ktv, bkc, 1.0))):
                for e in range(8):
                    bk_ = (qg * 16 + which * 8 + e) % 2
                    for c in range(8):
                        em.op("tensor", "matmul", pp[bk_][:, :],
                              wview[:, c, 128 * e:128 * e + 128],
                              zTv[:, c, 512 * qg:512 * qg + 512],
                              start=(c == 0), stop=(c == 7),
                              waits=((s_ev, zt_ev), (s_act, max(0, s_act.n - 1)))
                                    if c == 0 else (),
                              inc=(s_mm if c == 7 else None))
                    em.op("scalar", "activation",
                          dstv[:, e, 512 * qg:512 * qg + 512],
                          pp[bk_][:, :], AF.Identity, scale=scale,
                          bias=bias_col[:, e:e + 1],
                          waits=(s_mm,), inc=s_act)
        for t in range(NT):
            for j in range(2):
                bk_ = (t * 2 + j) % 2
                for c in range(8):
                    em.op("tensor", "matmul", pp[bk_][:, :],
                          zTv[:, c, 128 * t:128 * t + 128],
                          wvv[:, c, 512 * j:512 * j + 512],
                          start=(c == 0), stop=False,
                          waits=((s_ev, zt_ev), (s_act, max(0, s_act.n - 1)))
                                if c == 0 else ())
                em.op("tensor", "matmul", pp[bk_][:, :], ones_b[0:1, :],
                      bvrow[0:1, 512 * j:512 * j + 512],
                      start=False, stop=True, inc=s_mm)
                if v_hm:
                    vhm = v_sb[:, :].rearrange("p (h t n) -> p h t n",
                                               h=16, t=NT)
                    em.op("scalar", "activation", vhm[:, 8 * j:8 * j + 8, t, :],
                          pp[bk_][:, :], AF.Identity, waits=(s_mm,),
                          inc=s_act)
                else:
                    em.op("scalar", "activation",
                          vv[:, t, 512 * j:512 * j + 512],
                          pp[bk_][:, :], AF.Identity, waits=(s_mm,),
                          inc=s_act)

        em.wait("sync", s_act, s_act.n)
        em.dma("sync", qt_d.rearrange("(c p) t -> p c t", p=128), qtv, s_st)
        em.dma("sync", kt_d.rearrange("(c p) t -> p c t", p=128), ktv, s_st)
        if v_hm:
            em.dma("sync", v_d, v_sb[:, :], s_st)
        else:
            em.dma("sync", v_d.rearrange("(t p) n -> p t n", p=128), vv,
                   s_st)
        em.wait("gpsimd", s_st, s_st.n)

        if _own:
            with nc.Block() as block:
                em.finalize(block)
    return nc


def build_l2attn_flash(fx=None):
    H, HD = 16, 64
    KT_TILES = S // 128
    NPAIR = KT_TILES // 2
    NQG = TC // 512
    nc, _em0, pre, _own = _fx_open(fx)
    x_d = _fio(fx, nc, "x", [TC, D], F32, "ExternalInput")
    qt_d = _fio(fx, nc, "qt", [D, TC], BF16, "ExternalInput")
    kt_d = _fio(fx, nc, "kt_ag", [4 * D, TC], BF16, "ExternalInput")
    va_d = _fio(fx, nc, "v_ag", [512, NT * D], BF16, "ExternalInput")
    woT_d = _fio(fx, nc, "woT", [D, D], F32, "ExternalInput")
    bo_d = _fio(fx, nc, "bo", [1, D], F32, "ExternalInput")
    xo_d = _fio(fx, nc, "xo", [TC, D], F32, "ExternalOutput")

    with ExitStack() as stack:
        em = _em0 if _em0 is not None else Em(nc, stack)
        sb = lambda nm, sh, dt=F32: stack.enter_context(nc.sbuf_tensor(pre + nm, sh, dt))
        ps = lambda nm, sh, dt=F32: stack.enter_context(nc.psum_tensor(pre + nm, sh, dt))

        qt_sb = sb("qt_sb", [64, 16 * TC], BF16)
        kth = [sb(f"kth{i}", [64, S], BF16) for i in range(2)]
        vah = [sb(f"vah{i}", [128, 64 * 65], BF16) for i in range(2)]
        vtmp = [sb(f"vtmp{i}", [128, 64 * 64], BF16) for i in range(2)]
        pT = [sb(f"pT{i}", [128, 1024], BF16) for i in range(2)]
        ctxT = sb("ctxT", [128, 8 * TC], BF16)
        wo = sb("wo", [128, 8 * D], BF16)
        borow = sb("borow", [1, D], BF16)
        rec = sb("rec", [1, 512])
        bcast = sb("bcast", [64, 512])
        xt = [sb(f"xt{i}", [128, D]) for i in range(2)]
        ones_b = sb("ones_b", [1, 128], BF16)
        ones_f = sb("ones_f", [1, 64])
        kc = sb("kc", [128, 8])

        pST = [ps(f"pST{i}", [128, 1024]) for i in range(2)]
        pOT = ps("pOT", [128, 512])
        pbc = ps("pbc", [128, 512])
        pw = [ps(f"pw{i}", [128, 512]) for i in range(2)]

        ldg = em.sem("ldg")
        ldk = [em.sem("ldk0"), em.sem("ldk1")]
        ldvv = [em.sem("ldv0"), em.sem("ldv1")]
        ldx = [em.sem("ldx0"), em.sem("ldx1")]
        s_act = em.sem("act")
        s_dve = em.sem("dve")
        s_sm = em.sem("sm")
        s_pv = em.sem("pv")
        s_mm = em.sem("mm")
        s_st = [em.sem("st0"), em.sem("st1")]

        qtv = qt_sb[:, :].rearrange("p (h t) -> p h t", h=16)
        ctv = ctxT[:, :].rearrange("p (c t) -> p c t", c=8)
        wov = wo[:, :].rearrange("p (c n) -> p c n", c=8)
        xv_t = x_d.rearrange("(t p) d -> t p d", p=128)
        xov_t = xo_d.rearrange("(t p) d -> t p d", p=128)

        em.op("vector", "memset", kc[:, :], 0.0)
        em.op("vector", "memset", kc[:, 1:2], -30.0)
        em.op("vector", "memset", ones_f[:, :], 1.0)
        em.op("vector", "memset", ones_b[:, :], 1.0)
        for i in range(2):
            em.op("vector", "memset",
                  vah[i][:, :].rearrange("p (k u) -> p k u", u=65)[:, :, 64:65],
                  1.0, inc=(s_dve if i == 1 else None))
        em.dma("gpsimd", qtv, qt_d.rearrange("(h p) t -> p h t", p=64), ldg)
        em.dma("gpsimd", wov, woT_d.rearrange("(c p) n -> p c n", p=128), ldg)
        em.dma("gpsimd", borow[:, :], bo_d, ldg)
        ldgv = ldg.n
        em.wait("tensor", ldg, ldgv)
        em.wait("scalar", s_dve, s_dve.n)

        s_vau = em.sem("vau")

        def load_head(h, i, w):
            for r in range(4):
                em.dma("sync", kth[i][:, 2048 * r:2048 * r + 2048],
                       kt_d[1024 * r + 64 * h:1024 * r + 64 * h + 64, :],
                       ldk[i], waits=w if r == 0 else ())
                em.dma("sync", vtmp[i][:, 1024 * r:1024 * r + 1024],
                       va_d[128 * r:128 * r + 128,
                            1024 * h:1024 * h + 1024],
                       ldvv[i], waits=w if r == 0 else ())
            em.op("vector", "tensor_copy",
                  vah[i][:, :].rearrange("p (k u) -> p k u", u=65)[:, :, 0:64],
                  vtmp[i][:, :].rearrange("p (k u) -> p k u", u=64),
                  waits=((ldvv[i], ldvv[i].n),), inc=s_vau)
            kld[h] = (ldk[i].n, s_vau.n)

        kld = {}
        for h in range(2):
            load_head(h, h, ((s_dve, s_dve.n),))

        for h in range(H):
            i = h % 2
            hc, hp = h // 2, 64 * (h % 2)
            for qg in range(NQG):
                q0 = 512 * qg
                exp_hist = {}
                pv_hist = {}
                norm_done = s_dve.n

                def emit_S(pr):
                    pb = pr % 2
                    for half in range(2):
                        kt = 2 * pr + half
                        w = ()
                        if half == 0:
                            w = ((ldk[i], kld[h][0]),
                                 (s_act, exp_hist[pr - 2] if pr >= 2
                                  else s_act.n))
                        em.op("tensor", "matmul",
                              pST[pb][:, 512 * half:512 * half + 512],
                              kth[i][:, 128 * kt:128 * kt + 128],
                              qtv[:, h, q0:q0 + 512],
                              start=True, stop=True, waits=w, inc=s_sm)

                def emit_exp(pr):
                    pb = pr % 2
                    w = ((s_sm, s_sm.n),)
                    w += ((s_pv, pv_hist[pr - 2] if pr >= 2 else s_pv.n),)
                    em.op("scalar", "activation", pT[pb][:, :],
                          pST[pb][:, :], AF.Exp, bias=kc[:, 1:2],
                          waits=w, inc=s_act)
                    exp_hist[pr] = s_act.n

                def emit_pv(pr):
                    pb = pr % 2
                    for half in range(2):
                        kt = 2 * pr + half
                        w = ((s_act, exp_hist[pr]), (s_vau, kld[h][1]))
                        if pr == 0 and half == 0:
                            w += ((s_dve, norm_done),)
                        em.op("tensor", "matmul", pOT[0:65, :],
                              vah[i][:, 65 * kt:65 * kt + 65],
                              pT[pb][:, 512 * half:512 * half + 512],
                              start=(pr == 0 and half == 0),
                              stop=(pr == NPAIR - 1 and half == 1),
                              waits=w, inc=s_pv)
                    pv_hist[pr] = s_pv.n

                for pr in range(NPAIR):
                    emit_S(pr)
                    emit_exp(pr)
                    if pr >= 1:
                        emit_pv(pr - 1)
                emit_pv(NPAIR - 1)
                em.op("vector", "reciprocal", rec[0:1, :], pOT[64:65, :],
                      waits=(s_pv,), inc=s_dve)
                em.op("tensor", "matmul", pbc[0:64, :], ones_f[0:1, :],
                      rec[0:1, :], start=True, stop=True,
                      waits=((s_dve, s_dve.n), (s_act, s_act.n)), inc=s_mm)
                em.op("scalar", "copy", bcast[:, :], pbc[0:64, :],
                      waits=(s_mm,), inc=s_act)
                em.op("vector", "tensor_mul", ctv[hp:hp + 64, hc, q0:q0 + 512],
                      pOT[0:64, :], bcast[:, :],
                      waits=(s_act,), inc=s_dve)
            if h + 2 < H:
                load_head(h + 2, i, ((s_sm, s_sm.n), (s_pv, s_pv.n)))
        ctx_dve = s_dve.n

        for t in range(NT):
            i = t % 2
            em.dma("sync", xt[i][:, :], xv_t[t], ldx[i],
                   waits=((s_dve, s_dve.n), (s_st[i], s_st[i].n)))
            ldxv = ldx[i].n
            for j in range(2):
                pbk = pw[j]
                for c in range(8):
                    em.op("tensor", "matmul", pbk[:, :],
                          ctv[:, c, 128 * t:128 * t + 128],
                          wov[:, c, 512 * j:512 * j + 512],
                          start=(c == 0), stop=False,
                          waits=((s_dve, ctx_dve), (s_dve, s_dve.n))
                                if c == 0 else ())
                em.op("tensor", "matmul", pbk[:, :], ones_b[0:1, :],
                      borow[0:1, 512 * j:512 * j + 512],
                      start=False, stop=True, inc=s_mm)
            mmv = s_mm.n
            for j in range(2):
                em.op("vector", "tensor_add", xt[i][:, 512 * j:512 * j + 512],
                      xt[i][:, 512 * j:512 * j + 512], pw[j][:, :],
                      waits=((s_mm, mmv), (ldx[i], ldxv)),
                      inc=(s_dve if j == 1 else None))
            em.dma("sync", xov_t[t], xt[i][:, :], s_st[i],
                   waits=((s_dve, s_dve.n),))
        em.wait("gpsimd", s_st[0], s_st[0].n)
        em.wait("gpsimd", s_st[1], s_st[1].n)

        if _own:
            with nc.Block() as block:
                em.finalize(block)
    return nc


def step_l2attn(x_parts, ins):
    wqT = np.ascontiguousarray(ins["L2_attn_wq"].T)
    wkT = np.ascontiguousarray(ins["L2_attn_wk"].T)
    wvT = np.ascontiguousarray(ins["L2_attn_wv"].T)
    woT = np.ascontiguousarray(ins["L2_attn_wo"].T)
    bqc = np.ascontiguousarray(ins["L2_attn_bq"].reshape(8, 128).T)
    bkc = np.ascontiguousarray(ins["L2_attn_bk"].reshape(8, 128).T)
    common = {"wqT": wqT, "wkT": wkT, "wvT": wvT, "bqc": bqc, "bkc": bkc,
              "bv": ins["L2_attn_bv"].reshape(1, D), "ident": IDENT_BF}
    res = _launch("l2attn_proj_hm", lambda: build_l2attn_proj(v_hm=True),
                  [dict(common, x=x_parts[c]) for c in range(NCORES)])
    qts = [np.asarray(res[c]["qt"]) for c in range(NCORES)]
    kts = [np.asarray(res[c]["kt"]) for c in range(NCORES)]
    vs = [np.asarray(res[c]["v"]) for c in range(NCORES)]

    maps = []
    for b in range(2):
        kt_ag = np.ascontiguousarray(np.concatenate(kts[4 * b:4 * b + 4], 0))
        v_ag = np.ascontiguousarray(np.concatenate(vs[4 * b:4 * b + 4], 0))
        for ci in range(4):
            c = 4 * b + ci
            maps.append({"x": x_parts[c], "qt": qts[c], "kt_ag": kt_ag,
                         "v_ag": v_ag, "woT": woT,
                         "bo": ins["L2_attn_bo"].reshape(1, D)})
    res = _launch("l2attn_flash", build_l2attn_flash, maps)
    return [np.asarray(res[c]["xo"]) for c in range(NCORES)]


def build_l0attn_proj(fx=None):
    nc, _em0, pre, _own = _fx_open(fx)
    x_d = _fio(fx, nc, "x", [TC, D], F32, "ExternalInput")
    wqT_d = _fio(fx, nc, "wqT", [D, D], F32, "ExternalInput")
    wkT_d = _fio(fx, nc, "wkT", [D, D], F32, "ExternalInput")
    wvT_d = _fio(fx, nc, "wvT", [D, 512], F32, "ExternalInput")
    wqvT_d = _fio(fx, nc, "wqvT", [D, 512], F32, "ExternalInput")
    bq_d = _fio(fx, nc, "bqc", [128, 8], F32, "ExternalInput")
    bk_d = _fio(fx, nc, "bkc", [128, 8], F32, "ExternalInput")
    bv_d = _fio(fx, nc, "bv", [1, 512], F32, "ExternalInput")
    bqv_d = _fio(fx, nc, "bqv", [1, 512], F32, "ExternalInput")
    ws1_d = _fio(fx, nc, "ws1t", [1, 512], F32, "ExternalInput")
    bs_d = _fio(fx, nc, "bs", [1, 1], F32, "ExternalInput")
    g1_d = _fio(fx, nc, "g1", [1, D], F32, "ExternalInput")
    b1_d = _fio(fx, nc, "b1", [1, D], F32, "ExternalInput")
    id_d = _fio(fx, nc, "ident", [128, 128], BF16, "ExternalInput")
    qt_d = _fio(fx, nc, "qt", [D, TC], BF16, "ExternalOutput")
    kt_d = _fio(fx, nc, "kt", [D, TC], BF16, "ExternalOutput")
    v_d = _fio(fx, nc, "v", [TC, 512], BF16, "ExternalOutput")
    qvs_d = _fio(fx, nc, "qvs", [128, NT * 16], F32, "ExternalOutput")

    with ExitStack() as stack:
        em = _em0 if _em0 is not None else Em(nc, stack)
        sb = lambda nm, sh, dt=F32: stack.enter_context(nc.sbuf_tensor(pre + nm, sh, dt))
        ps = lambda nm, sh, dt=F32: stack.enter_context(nc.psum_tensor(pre + nm, sh, dt))

        wq = sb("wq", [128, 8 * D], BF16)
        wk = sb("wk", [128, 8 * D], BF16)
        wv = sb("wv", [128, 8 * 512], BF16)
        wqv = sb("wqv", [128, 8 * 512], BF16)
        zT = sb("zT", [128, 8 * TC], BF16)
        qt_sb = sb("qt_sb", [128, 8 * TC], BF16)
        kt_sb = sb("kt_sb", [128, 8 * TC], BF16)
        v_sb = sb("v_sb", [128, NT * 512], BF16)
        qvs_sb = sb("qvs_sb", [128, NT * 16])
        qvw = sb("qvw", [128, 512])
        xt = [sb(f"xt{i}", [128, D]) for i in range(2)]
        zbf = [sb(f"zbf{i}", [128, D], BF16) for i in range(2)]
        zf = sb("zf", [128, D])
        scr = sb("scr", [128, D], BF16)
        cols = sb("cols", [128, 8])
        gbc = sb("gbc", [128, D])
        bbc = sb("bbc", [128, D])
        ws1bc = sb("ws1bc", [128, 512])
        bqc = sb("bqc_sb", [128, 8])
        bkc = sb("bkc_sb", [128, 8])
        bvrow = sb("bvrow", [1, 512], BF16)
        bqvrow = sb("bqvrow", [1, 512], BF16)
        bscol = sb("bscol", [128, 1])
        grow = sb("grow", [1, 2 * D])
        wsrow = sb("wsrow", [1, 512])
        bsrow = sb("bsrow", [1, 1])
        ones_f = sb("ones_f", [1, 128])
        ones_b = sb("ones_b", [1, 128], BF16)
        id_sb = sb("id_sb", [128, 128], BF16)
        kc = sb("kc", [128, 8])

        pst = ps("pst", [128, 1024], BF16)
        pbc = ps("pbc", [128, 512])
        pp = [ps(f"pp{i}", [128, 512]) for i in range(2)]

        ld = em.sem("ld")
        ldg = em.sem("ldg")
        ldx = [em.sem("ldx0"), em.sem("ldx1")]
        s_act = em.sem("act")
        s_dve = em.sem("dve")
        s_pe = em.sem("pe")
        s_ev = em.sem("ev")
        s_mm = em.sem("mm")
        s_st = em.sem("st")

        wqv_v = wq[:, :].rearrange("p (c n) -> p c n", c=8)
        wkv_v = wk[:, :].rearrange("p (c n) -> p c n", c=8)
        wvv_v = wv[:, :].rearrange("p (c n) -> p c n", c=8)
        wqvv_v = wqv[:, :].rearrange("p (c n) -> p c n", c=8)
        zTv = zT[:, :].rearrange("p (c t) -> p c t", c=8)
        qtv = qt_sb[:, :].rearrange("p (c t) -> p c t", c=8)
        ktv = kt_sb[:, :].rearrange("p (c t) -> p c t", c=8)
        vv = v_sb[:, :].rearrange("p (t n) -> p t n", t=NT)
        qvsv = qvs_sb[:, :].rearrange("p (t h) -> p t h", t=NT)
        xv_t = x_d.rearrange("(t p) d -> t p d", p=128)

        em.op("vector", "memset", kc[:, :], 0.0)
        em.op("vector", "memset", kc[:, 0:1], EPS)
        em.op("vector", "memset", ones_f[:, :], 1.0)
        em.op("vector", "memset", ones_b[:, :], 1.0, inc=s_dve)
        em.dma("sync", bqc[:, :], bq_d, ld)
        em.dma("sync", bkc[:, :], bk_d, ld)
        em.dma("sync", id_sb[:, :], id_d, ld)
        em.dma("sync", grow[0:1, 0:D], g1_d, ld)
        em.dma("sync", grow[0:1, D:2 * D], b1_d, ld)
        em.dma("sync", wsrow[:, :], ws1_d, ld)
        em.dma("sync", bsrow[:, :], bs_d, ld)
        em.dma("gpsimd", bvrow[:, :], bv_d, ldg)
        em.dma("gpsimd", bqvrow[:, :], bqv_d, ldg)
        em.dma("gpsimd", wqv_v, wqT_d.rearrange("(c p) n -> p c n", p=128), ldg)
        em.dma("gpsimd", wkv_v, wkT_d.rearrange("(c p) n -> p c n", p=128), ldg)
        em.dma("gpsimd", wvv_v, wvT_d.rearrange("(c p) n -> p c n", p=128), ldg)
        em.dma("gpsimd", wqvv_v, wqvT_d.rearrange("(c p) n -> p c n", p=128), ldg)
        ldv, ldgv = ld.n, ldg.n
        em.wait("tensor", ld, ldv)
        em.wait("tensor", ldg, ldgv)
        em.op("vector", "tensor_scalar_mul", bqc[:, :], bqc[:, :], 0.125,
              waits=((ld, ldv),), inc=s_dve)

        for dst, buf, n in ((gbc, grow[0:1, 0:D], 1024),
                            (bbc, grow[0:1, D:2 * D], 1024),
                            (ws1bc, wsrow[0:1, :], 512),
                            (bscol, bsrow[0:1, :], 1)):
            for j in range((n + 511) // 512):
                jn = min(512, n - 512 * j)
                em.op("tensor", "matmul", pbc[:, 0:jn], ones_f[0:1, :],
                      buf[0:1, 512 * j:512 * j + jn],
                      start=True, stop=True,
                      waits=((ld, ldv), (s_dve, s_dve.n), (s_ev, s_ev.n)),
                      inc=s_mm)
                em.op("vector", "tensor_copy", dst[:, 512 * j:512 * j + jn],
                      pbc[:, 0:jn], waits=(s_mm,), inc=s_ev)
        bc_ev = s_ev.n

        for t in range(NT):
            i = t % 2
            em.dma("sync", xt[i][:, :], xv_t[t], ldx[i],
                   waits=((s_dve, s_dve.n),))
            em.wait("scalar", ldx[i], ldx[i].n)
            emit_ln_tile(em, xt[i][:, :], None, scr[:, :], cols[:, :],
                         s_act, s_dve, kc[:, 0:1], zf32_ap=zf[:, :])
            emit_ln_tile(em, zf[:, :], zbf[i][:, :], scr[:, :], cols[:, :],
                         s_act, s_dve, kc[:, 0:1], gain_ap=gbc[:, :],
                         bias_ap=bbc[:, :], zf32_ap=xt[i][:, :],
                         out_waits=((s_pe, 8 * (t - 1)) if t >= 2 else (s_pe, 0),),
                         in_waits=((s_dve, s_dve.n), (s_ev, bc_ev)))
            for c in range(8):
                em.op("tensor", "transpose", pst[:, 0:128],
                      zbf[i][:, 128 * c:128 * c + 128], id_sb[:, :],
                      waits=((s_dve, s_dve.n), (s_ev, s_ev.n)), inc=s_pe)
                em.op("vector", "tensor_copy", zTv[:, c, 128 * t:128 * t + 128],
                      pst[:, 0:128], waits=(s_pe,), inc=s_ev)
        zt_ev = s_ev.n

        s_bank = [em.sem("bk0"), em.sem("bk1")]
        NQG = TC // 512
        for qg in range(NQG):
            for which, (wview, dstv, bias_col, scale) in enumerate(
                    ((wqv_v, qtv, bqc, 0.125), (wkv_v, ktv, bkc, 1.0))):
                for e in range(8):
                    bk_ = (qg * 16 + which * 8 + e) % 2
                    for c in range(8):
                        em.op("tensor", "matmul", pp[bk_][:, :],
                              wview[:, c, 128 * e:128 * e + 128],
                              zTv[:, c, 512 * qg:512 * qg + 512],
                              start=(c == 0), stop=(c == 7),
                              waits=((s_ev, zt_ev),
                                     (s_bank[bk_], s_bank[bk_].n))
                                    if c == 0 else (),
                              inc=(s_mm if c == 7 else None))
                    em.op("scalar", "activation",
                          dstv[:, e, 512 * qg:512 * qg + 512],
                          pp[bk_][:, :], AF.Identity, scale=scale,
                          bias=bias_col[:, e:e + 1],
                          waits=(s_mm,), inc=s_bank[bk_])
        for t in range(NT):
            for which, (wview, brow) in enumerate(((wvv_v, bvrow),
                                                   (wqvv_v, bqvrow))):
                bk_ = (t * 2 + which) % 2
                for c in range(8):
                    em.op("tensor", "matmul", pp[bk_][:, :],
                          zTv[:, c, 128 * t:128 * t + 128],
                          wview[:, c, 0:512],
                          start=(c == 0), stop=False,
                          waits=((s_ev, zt_ev),
                                 (s_bank[bk_], s_bank[bk_].n))
                                if c == 0 else ())
                em.op("tensor", "matmul", pp[bk_][:, :], ones_b[0:1, :],
                      brow[0:1, :], start=False, stop=True, inc=s_mm)
                if which == 0:
                    em.op("scalar", "activation", vv[:, t, :],
                          pp[bk_][:, :], AF.Identity, waits=(s_mm,),
                          inc=s_bank[bk_])
                else:
                    em.op("vector", "tensor_mul", qvw[:, :], pp[bk_][:, :],
                          ws1bc[:, :], waits=(s_mm,), inc=s_bank[bk_])
                    for h in range(16):
                        em.op("vector", "reduce_sum",
                              cols[:, 0:1], qvw[:, 32 * h:32 * h + 32],
                              axis=AX.X)
                        em.op("vector", "tensor_scalar",
                              qvsv[:, t, h:h + 1], cols[:, 0:1],
                              0.1767766952966369, bscol[:, 0:1],
                              ALU.mult, ALU.add,
                              inc=(s_dve if h == 15 else None))

        em.wait("sync", s_bank[0], s_bank[0].n)
        em.wait("sync", s_bank[1], s_bank[1].n)
        em.wait("sync", s_dve, s_dve.n)
        em.dma("sync", qt_d.rearrange("(c p) t -> p c t", p=128), qtv, s_st)
        em.dma("sync", kt_d.rearrange("(c p) t -> p c t", p=128), ktv, s_st)
        em.dma("sync", v_d.rearrange("(t p) n -> p t n", p=128), vv, s_st)
        em.dma("sync", qvs_d, qvs_sb[:, :], s_st)
        em.wait("gpsimd", s_st, s_st.n)

        if _own:
            with nc.Block() as block:
                em.finalize(block)
    return nc


def build_l0attn_flash(fx=None):
    H = 16
    KT_TILES = S // 128
    NPAIR = KT_TILES // 2
    NQG = TC // 512
    nc, _em0, pre, _own = _fx_open(fx)
    qt_d = _fio(fx, nc, "qt", [D, TC], BF16, "ExternalInput")
    kt_d = _fio(fx, nc, "kt_ag", [4 * D, TC], BF16, "ExternalInput")
    va_d = _fio(fx, nc, "v_ag", [S, 512], BF16, "ExternalInput")
    qvs_d = _fio(fx, nc, "qvs", [128, NT * 16], F32, "ExternalInput")
    ws2_d = _fio(fx, nc, "ws2t", [1, 512], F32, "ExternalInput")
    om_d = _fio(fx, nc, "om", [TC, D], BF16, "ExternalOutput")

    with ExitStack() as stack:
        em = _em0 if _em0 is not None else Em(nc, stack)
        sb = lambda nm, sh, dt=F32: stack.enter_context(nc.sbuf_tensor(pre + nm, sh, dt))
        ps = lambda nm, sh, dt=F32: stack.enter_context(nc.psum_tensor(pre + nm, sh, dt))

        qt_sb = sb("qt_sb", [64, 16 * TC], BF16)
        va = sb("va_sb", [128, KT_TILES * 512], BF16)
        kth = [sb("kth0", [64, S], BF16)]
        pT = [sb(f"pT{i}", [128, 1024], BF16) for i in range(2)]
        om_sb = sb("om_sb", [128, NT * D], BF16)
        osb = sb("osb", [128, 4 * 512], BF16)
        qvs_sb = sb("qvs_sb", [128, NT * 16])
        ws2bc = sb("ws2bc", [128, 512], BF16)
        wsrow = sb("wsrow", [1, 512])
        cw = sb("cw", [128, 512], BF16)
        cr = sb("cr", [128, 8])
        e8 = sb("e8", [128, 8])
        mcol = sb("mcol", [128, 8])
        recd = sb("recd", [128, 4])
        omacc = sb("omacc", [128, 64])
        ones_f = sb("ones_f", [1, 128])
        onescol_b = sb("onescol_b", [128, 1], BF16)
        kc = sb("kc", [128, 8])

        pST = ps("pST", [128, 1024])
        pO = [ps(f"pO{i}", [128, 512]) for i in range(4)]
        pden = ps("pden", [128, 512])
        pbc = ps("pbc", [128, 512])

        ldg = em.sem("ldg")
        ldk = [em.sem("ldk0"), em.sem("ldk1")]
        s_act = em.sem("act")
        s_dve = em.sem("dve")
        s_sm = em.sem("sm")
        s_pv = em.sem("pv")
        s_mm = em.sem("mm")
        s_st = em.sem("st")

        qtv = qt_sb[:, :].rearrange("p (h t) -> p h t", h=16)
        vav = va[:, :].rearrange("p (k n) -> p k n", k=KT_TILES)
        omv = om_sb[:, :].rearrange("p (t d) -> p t d", t=NT)
        qvsv = qvs_sb[:, :].rearrange("p (t h) -> p t h", t=NT)

        em.op("vector", "memset", kc[:, :], 0.0)
        em.op("vector", "memset", kc[:, 1:2], -30.0)
        em.op("vector", "memset", ones_f[:, :], 1.0)
        em.op("vector", "memset", onescol_b[:, :], 1.0, inc=s_dve)
        em.dma("gpsimd", qtv, qt_d.rearrange("(h p) t -> p h t", p=64), ldg)
        em.dma("gpsimd", vav, va_d.rearrange("(k p) n -> p k n", p=128), ldg)
        em.dma("gpsimd", qvs_sb[:, :], qvs_d, ldg)
        em.dma("sync", wsrow[:, :], ws2_d, ldg_s := em.sem("lds"))
        ldsv = ldg_s.n
        ldgv = ldg.n
        em.wait("tensor", ldg, ldgv)
        em.op("tensor", "matmul", pbc[:, :], ones_f[0:1, :], wsrow[0:1, :],
              start=True, stop=True,
              waits=((ldg_s, ldsv), (s_dve, s_dve.n)), inc=s_mm)
        em.op("vector", "tensor_copy", ws2bc[:, :], pbc[:, :],
              waits=(s_mm,), inc=s_dve)

        kld = {}

        def load_kth(h, w):
            for r in range(4):
                em.dma("sync", kth[0][:, 2048 * r:2048 * r + 2048],
                       kt_d[1024 * r + 64 * h:1024 * r + 64 * h + 64, :],
                       ldk[0], waits=w if r == 0 else ())
            kld[h] = ldk[0].n

        load_kth(0, ())

        osb_done = [0]
        recd_done = [0]
        for h in range(H):
            i = 0
            for qg in range(NQG):
                q0 = 512 * qg
                exp_hist = {}
                pv_hist = {}

                def emit_S(pr):
                    for half in range(2):
                        kt = 2 * pr + half
                        w = ()
                        if half == 0:
                            w = ((ldk[i], kld[h]),
                                 (s_act, exp_hist[pr - 1] if pr >= 1
                                  else s_act.n))
                        em.op("tensor", "matmul",
                              pST[:, 512 * half:512 * half + 512],
                              kth[i][:, 128 * kt:128 * kt + 128],
                              qtv[:, h, q0:q0 + 512],
                              start=True, stop=True, waits=w, inc=s_sm)

                def emit_exp(pr):
                    w = ((s_sm, s_sm.n),)
                    w += ((s_pv, pv_hist[pr - 2] if pr >= 2 else s_pv.n),)
                    em.op("scalar", "activation", pT[pr % 2][:, :],
                          pST[:, :], AF.Exp, bias=kc[:, 1:2],
                          waits=w, inc=s_act)
                    exp_hist[pr] = s_act.n

                def emit_pvden(pr):
                    pb = pr % 2
                    for half in range(2):
                        kt = 2 * pr + half
                        for qs in range(4):
                            w = ((s_act, exp_hist[pr]),)
                            if pr == 0 and half == 0:
                                w += ((s_act, osb_done[0]),
                                      (s_dve, recd_done[0]))
                            em.op("tensor", "matmul", pO[qs][:, :],
                                  pT[pb][:, 512 * half + 128 * qs:
                                         512 * half + 128 * qs + 128],
                                  vav[:, kt, :],
                                  start=(pr == 0 and half == 0),
                                  stop=(pr == NPAIR - 1 and half == 1),
                                  waits=w, inc=s_pv)
                        for qs in range(4):
                            em.op("tensor", "matmul", pden[:, qs:qs + 1],
                                  pT[pb][:, 512 * half + 128 * qs:
                                         512 * half + 128 * qs + 128],
                                  onescol_b[:, :],
                                  start=(pr == 0 and half == 0 and qs == 0),
                                  stop=(pr == NPAIR - 1 and half == 1
                                        and qs == 3),
                                  inc=s_pv)
                    pv_hist[pr] = s_pv.n

                for pr in range(NPAIR):
                    emit_S(pr)
                    emit_exp(pr)
                    if pr >= 1:
                        emit_pvden(pr - 1)
                emit_pvden(NPAIR - 1)
                em.op("vector", "reciprocal", recd[:, :], pden[:, 0:4],
                      waits=(s_pv,), inc=s_dve)
                recd_done[0] = s_dve.n
                for qs in range(4):
                    em.op("scalar", "copy", osb[:, 512 * qs:512 * qs + 512],
                          pO[qs][:, :],
                          waits=((s_pv, s_pv.n), (s_dve, s_dve.n)),
                          inc=s_act)
                osb_done[0] = s_act.n
                for qs in range(4):
                    tidx = 4 * qg + qs
                    em.op("vector", "tensor_mul", cw[:, :],
                          osb[:, 512 * qs:512 * qs + 512], ws2bc[:, :],
                          waits=((s_act, osb_done[0]),))
                    for r in range(8):
                        em.op("vector", "reduce_sum", cr[:, r:r + 1],
                              cw[:, 64 * r:64 * r + 64], axis=AX.X)
                    em.op("vector", "tensor_scalar", cr[:, :], cr[:, :],
                          recd[:, qs:qs + 1], None, ALU.mult)
                    em.op("vector", "tensor_scalar_add", cr[:, :], cr[:, :],
                          qvsv[:, tidx, h:h + 1])
                    em.op("vector", "reduce_max", cols0 := kc[:, 2:3],
                          cr[:, :], axis=AX.X)
                    em.op("vector", "tensor_scalar_mul", kc[:, 3:4],
                          cols0, -1.0, inc=s_dve)
                    em.op("scalar", "activation", e8[:, :], cr[:, :],
                          AF.Exp, bias=kc[:, 3:4], accum_out=kc[:, 4:5],
                          waits=(s_dve,), inc=s_act)
                    em.op("vector", "reciprocal", kc[:, 5:6], kc[:, 4:5],
                          waits=(s_act,))
                    em.op("vector", "tensor_mul", kc[:, 5:6], kc[:, 5:6],
                          recd[:, qs:qs + 1])
                    em.op("vector", "tensor_scalar_mul", mcol[:, :],
                          e8[:, :], kc[:, 5:6])
                    for r in range(8):
                        if r == 0:
                            em.op("vector", "tensor_scalar_mul", omacc[:, :],
                                  osb[:, 512 * qs + 0:512 * qs + 64],
                                  mcol[:, 0:1])
                        else:
                            em.op("vector", "scalar_tensor_tensor",
                                  omacc[:, :],
                                  osb[:, 512 * qs + 64 * r:512 * qs + 64 * r + 64],
                                  mcol[:, r:r + 1], omacc[:, :],
                                  ALU.mult, ALU.add)
                    em.op("vector", "tensor_copy",
                          omv[:, tidx, 64 * h:64 * h + 64], omacc[:, :],
                          inc=s_dve)
            if h + 1 < H:
                load_kth(h + 1, ((s_sm, s_sm.n),))

        em.wait("sync", s_dve, s_dve.n)
        em.dma("sync", om_d.rearrange("(t p) d -> p t d", p=128), omv, s_st)
        em.wait("gpsimd", s_st, s_st.n)

        if _own:
            with nc.Block() as block:
                em.finalize(block)
    return nc


def build_l0attn_post(fx=None):
    nc, _em0, pre, _own = _fx_open(fx)
    x_d = _fio(fx, nc, "x", [TC, D], F32, "ExternalInput")
    om_d = _fio(fx, nc, "om", [TC, D], BF16, "ExternalInput")
    wfT_d = _fio(fx, nc, "wfT", [D, D], F32, "ExternalInput")
    bf_d = _fio(fx, nc, "bf", [1, D], F32, "ExternalInput")
    g2_d = _fio(fx, nc, "g2", [1, D], F32, "ExternalInput")
    b2_d = _fio(fx, nc, "b2", [1, D], F32, "ExternalInput")
    wr1T_d = _fio(fx, nc, "wr1T", [D, 2 * D], F32, "ExternalInput")
    br1_d = _fio(fx, nc, "br1c", [128, 16], F32, "ExternalInput")
    wr2T_d = _fio(fx, nc, "wr2T", [2 * D, D], F32, "ExternalInput")
    br2_d = _fio(fx, nc, "br2", [1, D], F32, "ExternalInput")
    id_d = _fio(fx, nc, "ident", [128, 128], BF16, "ExternalInput")
    xo_d = _fio(fx, nc, "xo", [TC, D], F32, "ExternalOutput")

    with ExitStack() as stack:
        em = _em0 if _em0 is not None else Em(nc, stack)
        sb = lambda nm, sh, dt=F32: stack.enter_context(nc.sbuf_tensor(pre + nm, sh, dt))
        ps = lambda nm, sh, dt=F32: stack.enter_context(nc.psum_tensor(pre + nm, sh, dt))

        wf = sb("wf", [128, 8 * D], BF16)
        wr1 = sb("wr1", [128, 8 * 2 * D], BF16)
        wr2 = sb("wr2", [128, 16 * D], BF16)
        xt = [sb(f"xt{i}", [128, D]) for i in range(2)]
        omt = [sb(f"omt{i}", [128, D], BF16) for i in range(2)]
        omT = sb("omT", [128, D], BF16)
        t1 = sb("t1", [128, D])
        onb = sb("onb", [128, D], BF16)
        onf = sb("onf", [128, D])
        onT = sb("onT", [128, D], BF16)
        h1T = sb("h1T", [128, 16 * 128], BF16)
        scr = sb("scr", [128, D], BF16)
        cols = sb("cols", [128, 8])
        cols2 = sb("cols2", [128, 8])
        gbc = sb("gbc", [128, D])
        bbc = sb("bbc", [128, D])
        br1c = sb("br1c_sb", [128, 16])
        grow = sb("grow", [1, 2 * D])
        bfrow = sb("bfrow", [1, D], BF16)
        br2row = sb("br2row", [1, D], BF16)
        ones_b = sb("ones_b", [1, 128], BF16)
        ones_f = sb("ones_f", [1, 128])
        id_sb = sb("id_sb", [128, 128], BF16)
        kc = sb("kc", [128, 8])

        pst = ps("pst", [128, 1024], BF16)
        pbc = ps("pbc", [128, 512])
        pm = [ps(f"pm{i}", [128, 512]) for i in range(2)]
        ph = [ps(f"ph{i}", [128, 512]) for i in range(2)]

        ld = em.sem("ld")
        ldg = em.sem("ldg")
        ldx = [em.sem("ldx0"), em.sem("ldx1")]
        ldo = [em.sem("ldo0"), em.sem("ldo1")]
        s_act = em.sem("act")
        s_dve = em.sem("dve")
        s_pe = em.sem("pe")
        s_ev = em.sem("ev")
        s_mm = em.sem("mm")
        s_stp = [em.sem("st0"), em.sem("st1")]
        s_upd = em.sem("upd")

        wfv = wf[:, :].rearrange("p (c n) -> p c n", c=8)
        wr1v = wr1[:, :].rearrange("p (c n) -> p c n", c=8)
        wr2v = wr2[:, :].rearrange("p (c n) -> p c n", c=16)
        xv_t = x_d.rearrange("(t p) d -> t p d", p=128)
        omv_t = om_d.rearrange("(t p) d -> t p d", p=128)
        xov_t = xo_d.rearrange("(t p) d -> t p d", p=128)

        em.op("vector", "memset", kc[:, :], 0.0)
        em.op("vector", "memset", kc[:, 0:1], EPS)
        em.op("vector", "memset", ones_f[:, :], 1.0)
        em.op("vector", "memset", ones_b[:, :], 1.0, inc=s_dve)
        em.dma("sync", grow[0:1, 0:D], g2_d, ld)
        em.dma("sync", grow[0:1, D:2 * D], b2_d, ld)
        em.dma("sync", br1c[:, :], br1_d, ld)
        em.dma("sync", id_sb[:, :], id_d, ld)
        em.dma("gpsimd", bfrow[:, :], bf_d, ldg)
        em.dma("gpsimd", br2row[:, :], br2_d, ldg)
        em.dma("gpsimd", wfv, wfT_d.rearrange("(c p) n -> p c n", p=128), ldg)
        em.dma("gpsimd", wr1v, wr1T_d.rearrange("(c p) n -> p c n", p=128), ldg)
        em.dma("gpsimd", wr2v, wr2T_d.rearrange("(c p) n -> p c n", p=128), ldg)
        ldv, ldgv = ld.n, ldg.n
        em.wait("tensor", ld, ldv)
        em.wait("tensor", ldg, ldgv)

        for dst, off in ((gbc, 0), (bbc, D)):
            for j in range(2):
                em.op("tensor", "matmul", pbc[:, :], ones_f[0:1, :],
                      grow[0:1, off + 512 * j:off + 512 * j + 512],
                      start=True, stop=True,
                      waits=((ld, ldv), (s_dve, s_dve.n), (s_ev, s_ev.n)),
                      inc=s_mm)
                em.op("vector", "tensor_copy", dst[:, 512 * j:512 * j + 512],
                      pbc[:, :], waits=(s_mm,), inc=s_ev)
        bc_ev = s_ev.n

        hp_hist = []
        for t in range(NT):
            i = t % 2
            em.dma("sync", xt[i][:, :], xv_t[t], ldx[i],
                   waits=((s_upd, s_upd.n), (s_stp[i], s_stp[i].n)))
            ldxv = ldx[i].n
            em.dma("sync", omt[i][:, :], omv_t[t], ldo[i],
                   waits=((s_pe, s_pe.n),))
            ldov = ldo[i].n
            em.wait("scalar", ldx[i], ldxv)
            emit_ln_tile(em, xt[i][:, :], None, scr[:, :], cols[:, :],
                         s_act, s_dve, kc[:, 0:1], zf32_ap=t1[:, :])
            em.wait("tensor", ldo[i], ldov)
            for c in range(8):
                em.op("tensor", "transpose", pst[:, 0:128],
                      omt[i][:, 128 * c:128 * c + 128], id_sb[:, :],
                      waits=((s_ev, s_ev.n),), inc=s_pe)
                em.op("vector", "tensor_copy", omT[:, 128 * c:128 * c + 128],
                      pst[:, 0:128], waits=(s_pe,), inc=s_ev)
            omT_ev = s_ev.n
            for j in range(2):
                for c in range(8):
                    em.op("tensor", "matmul", pm[j][:, :],
                          omT[:, 128 * c:128 * c + 128],
                          wfv[:, c, 512 * j:512 * j + 512],
                          start=(c == 0), stop=False,
                          waits=((s_ev, omT_ev), (s_dve, s_dve.n),
                                 (s_upd, s_upd.n))
                                if c == 0 else ())
                em.op("tensor", "matmul", pm[j][:, :], ones_b[0:1, :],
                      bfrow[0:1, 512 * j:512 * j + 512],
                      start=False, stop=True, inc=s_mm)
            mm1 = s_mm.n
            for j in range(2):
                em.op("vector", "tensor_add", t1[:, 512 * j:512 * j + 512],
                      t1[:, 512 * j:512 * j + 512], pm[j][:, :],
                      waits=((s_mm, mm1),))
            em.op("vector", "tensor_add", xt[i][:, :], xt[i][:, :], t1[:, :],
                  inc=s_dve)
            emit_ln_tile(em, t1[:, :], onb[:, :], scr[:, :], cols2[:, :],
                         s_act, s_dve, kc[:, 0:1], gain_ap=gbc[:, :],
                         bias_ap=bbc[:, :], zf32_ap=onf[:, :],
                         out_waits=((s_pe, s_pe.n),),
                         in_waits=((s_dve, s_dve.n), (s_ev, bc_ev)))
            for c in range(8):
                em.op("tensor", "transpose", pst[:, 0:128],
                      onb[:, 128 * c:128 * c + 128], id_sb[:, :],
                      waits=((s_dve, s_dve.n), (s_ev, s_ev.n)), inc=s_pe)
                em.op("vector", "tensor_copy", onT[:, 128 * c:128 * c + 128],
                      pst[:, 0:128], waits=(s_pe,), inc=s_ev)
            onT_ev = s_ev.n
            for p in range(4):
                bk = p % 2
                for u in range(4):
                    nb = 4 * p + u
                    for c in range(8):
                        first = (u == 0 and c == 0)
                        w = ()
                        if first:
                            w = ((s_ev, onT_ev),)
                            if len(hp_hist) >= 2:
                                w += ((s_act, hp_hist[-2]),)
                        em.op("tensor", "matmul",
                              ph[bk][:, 128 * u:128 * u + 128],
                              wr1v[:, c, 128 * nb:128 * nb + 128],
                              onT[:, 128 * c:128 * c + 128],
                              start=first, stop=(u == 3 and c == 7),
                              waits=w,
                              inc=(s_mm if (u == 3 and c == 7) else None))
                for u in range(4):
                    nb = 4 * p + u
                    em.op("scalar", "activation",
                          h1T[:, 128 * nb:128 * nb + 128],
                          ph[bk][:, 128 * u:128 * u + 128], AF.Relu,
                          bias=br1c[:, nb:nb + 1],
                          waits=(s_mm,) if u == 0 else (), inc=s_act)
                hp_hist.append(s_act.n)
            relu_done = s_act.n
            for j in range(2):
                for c in range(16):
                    em.op("tensor", "matmul", pm[j][:, :],
                          h1T[:, 128 * c:128 * c + 128],
                          wr2v[:, c, 512 * j:512 * j + 512],
                          start=(c == 0), stop=False,
                          waits=((s_act, relu_done), (s_dve, s_dve.n))
                                if c == 0 else ())
                em.op("tensor", "matmul", pm[j][:, :], ones_b[0:1, :],
                      br2row[0:1, 512 * j:512 * j + 512],
                      start=False, stop=True, inc=s_mm)
            mm3 = s_mm.n
            for j in range(2):
                em.op("vector", "tensor_add", xt[i][:, 512 * j:512 * j + 512],
                      xt[i][:, 512 * j:512 * j + 512], pm[j][:, :],
                      waits=((s_mm, mm3),),
                      inc=(s_upd if j == 1 else None))
            em.dma("sync", xov_t[t], xt[i][:, :], s_stp[i],
                   waits=((s_upd, s_upd.n),))
        em.wait("gpsimd", s_stp[0], s_stp[0].n)
        em.wait("gpsimd", s_stp[1], s_stp[1].n)

        if _own:
            with nc.Block() as block:
                em.finalize(block)
    return nc


def step_l0attn(x_parts, ins):
    common_p = {
        "wqT": np.ascontiguousarray(ins["L0_attn_Wq"].T),
        "wkT": np.ascontiguousarray(ins["L0_attn_Wk"].T),
        "wvT": np.ascontiguousarray(ins["L0_attn_Wv"].T),
        "wqvT": np.ascontiguousarray(ins["L0_attn_Wqv"].T),
        "bqc": np.ascontiguousarray(ins["L0_attn_bq"].reshape(8, 128).T),
        "bkc": np.ascontiguousarray(ins["L0_attn_bk"].reshape(8, 128).T),
        "bv": ins["L0_attn_bv"].reshape(1, 512),
        "bqv": ins["L0_attn_bqv"].reshape(1, 512),
        "ws1t": np.ascontiguousarray(np.tile(ins["L0_attn_Ws"][0, :32], 16)).reshape(1, 512),
        "bs": ins["L0_attn_bs"].reshape(1, 1),
        "g1": ins["L0_attn_g1"].reshape(1, D),
        "b1": ins["L0_attn_b1"].reshape(1, D),
        "ident": IDENT_BF,
    }
    res = _launch("l0attn_proj", build_l0attn_proj,
                  [dict(common_p, x=x_parts[c]) for c in range(NCORES)])
    qts = [np.asarray(res[c]["qt"]) for c in range(NCORES)]
    kts = [np.asarray(res[c]["kt"]) for c in range(NCORES)]
    vs = [np.asarray(res[c]["v"]) for c in range(NCORES)]
    qvss = [np.asarray(res[c]["qvs"]) for c in range(NCORES)]

    ws2t = np.ascontiguousarray(np.tile(ins["L0_attn_Ws"][0, 32:96], 8)).reshape(1, 512)
    maps = []
    for b in range(2):
        kt_ag = np.ascontiguousarray(np.concatenate(kts[4 * b:4 * b + 4], 0))
        v_ag = np.ascontiguousarray(np.concatenate(vs[4 * b:4 * b + 4], 0))
        for ci in range(4):
            c = 4 * b + ci
            maps.append({"qt": qts[c], "kt_ag": kt_ag, "v_ag": v_ag,
                         "qvs": qvss[c], "ws2t": ws2t})
    res = _launch("l0attn_flash", build_l0attn_flash, maps)
    oms = [np.asarray(res[c]["om"]) for c in range(NCORES)]

    common_q = {
        "wfT": np.ascontiguousarray(ins["L0_attn_Wf"].T),
        "bf": ins["L0_attn_bf"].reshape(1, D),
        "g2": ins["L0_attn_g2"].reshape(1, D),
        "b2": ins["L0_attn_b2"].reshape(1, D),
        "wr1T": np.ascontiguousarray(ins["L0_attn_Wr1"].T),
        "br1c": np.ascontiguousarray(ins["L0_attn_br1"].reshape(16, 128).T),
        "wr2T": np.ascontiguousarray(ins["L0_attn_Wr2"].T),
        "br2": ins["L0_attn_br2"].reshape(1, D),
        "ident": IDENT_BF,
    }
    res = _launch("l0attn_post", build_l0attn_post,
                  [dict(common_q, x=x_parts[c], om=oms[c])
                   for c in range(NCORES)])
    return [np.asarray(res[c]["xo"]) for c in range(NCORES)]


def build_l1ffn_part(fx=None):
    ES = 256
    nc, _em0, pre, _own = _fx_open(fx)
    x_d = _fio(fx, nc, "x", [TC, D], F32, "ExternalInput")
    mu_d = _fio(fx, nc, "mu", [D, ES], F32, "ExternalInput")
    sc_d = _fio(fx, nc, "scale", [1, 1], F32, "ExternalInput")
    g_d = _fio(fx, nc, "gamma", [1, D], F32, "ExternalInput")
    b_d = _fio(fx, nc, "beta", [1, D], F32, "ExternalInput")
    id_d = _fio(fx, nc, "ident", [128, 128], BF16, "ExternalInput")
    num_d = _fio(fx, nc, "num", [ES, D], F32, "ExternalOutput")
    den_d = _fio(fx, nc, "den", [128, 2], F32, "ExternalOutput")
    w_d = _fio(fx, nc, "w", [TC, ES], BF16, "ExternalOutput")

    with ExitStack() as stack:
        em = _em0 if _em0 is not None else Em(nc, stack)
        sb = lambda nm, sh, dt=F32: stack.enter_context(nc.sbuf_tensor(pre + nm, sh, dt))
        ps = lambda nm, sh, dt=F32: stack.enter_context(nc.psum_tensor(pre + nm, sh, dt))

        mu_sb = sb("mu_sb", [128, 8 * ES])
        mbf = sb("mbf", [128, 8 * ES], BF16)
        musq = sb("musq", [128, 8 * ES], BF16)
        rn = sb("rn", [1, ES])
        rnbc = sb("rnbc", [128, ES])
        gbc = sb("gbc", [128, D])
        bbc = sb("bbc", [128, D])
        grow = sb("grow", [1, 2 * D])
        screw = sb("screw", [1, 2])
        xt = [sb(f"xt{i}", [128, D]) for i in range(2)]
        zf = sb("zf", [128, D])
        xnf = sb("xnf", [128, D])
        xnb = sb("xnb", [128, D], BF16)
        xub = sb("xub", [128, D], BF16)
        xuT = sb("xuT", [128, D], BF16)
        wsb = sb("wsb", [128, NT * ES], BF16)
        numsb = sb("numsb", [128, 2 * D])
        densb = sb("densb", [128, 2])
        scr = sb("scr", [128, D], BF16)
        cols = sb("cols", [128, 8])
        ones_f = sb("ones_f", [1, 128])
        onescol_b = sb("onescol_b", [128, 1], BF16)
        id_sb = sb("id_sb", [128, 128], BF16)
        kc = sb("kc", [128, 8])

        pst = ps("pst", [128, 1024], BF16)
        pbc = ps("pbc", [128, 512])
        plg = ps("plg", [128, 256])
        pnum = [ps(f"pnum{i}", [128, 512]) for i in range(4)]
        pden = ps("pden", [128, 512])

        ld = em.sem("ld")
        ldx = [em.sem("ldx0"), em.sem("ldx1")]
        s_act = em.sem("act")
        s_dve = em.sem("dve")
        s_pe = em.sem("pe")
        s_ev = em.sem("ev")
        s_mm = em.sem("mm")
        s_nm = em.sem("nm")
        s_st = em.sem("st")

        muv = mu_sb[:, :].rearrange("p (c n) -> p c n", c=8)
        mbv = mbf[:, :].rearrange("p (c n) -> p c n", c=8)
        msv = musq[:, :].rearrange("p (c n) -> p c n", c=8)
        wv = wsb[:, :].rearrange("p (t n) -> p t n", t=NT)
        xv_t = x_d.rearrange("(t p) d -> t p d", p=128)

        em.op("vector", "memset", kc[:, :], 0.0)
        em.op("vector", "memset", kc[:, 0:1], EPS)
        em.op("vector", "memset", kc[:, 1:2], 1e-24)
        em.op("vector", "memset", ones_f[:, :], 1.0)
        em.op("vector", "memset", onescol_b[:, :], 1.0, inc=s_dve)
        em.dma("sync", mu_sb[:, :].rearrange("p (c n) -> p c n", c=8),
               mu_d.rearrange("(c p) n -> p c n", p=128), ld)
        em.dma("sync", grow[0:1, 0:D], g_d, ld)
        em.dma("sync", grow[0:1, D:2 * D], b_d, ld)
        em.dma("sync", screw[0:1, 0:1], sc_d, ld)
        em.dma("sync", id_sb[:, :], id_d, ld)
        ldv = ld.n
        em.wait("tensor", ld, ldv)
        em.wait("scalar", ld, ldv)

        for dst, off in ((gbc, 0), (bbc, D)):
            for j in range(2):
                em.op("tensor", "matmul", pbc[:, :], ones_f[0:1, :],
                      grow[0:1, off + 512 * j:off + 512 * j + 512],
                      start=True, stop=True,
                      waits=((ld, ldv), (s_dve, s_dve.n), (s_ev, s_ev.n)),
                      inc=s_mm)
                em.op("vector", "tensor_copy", dst[:, 512 * j:512 * j + 512],
                      pbc[:, :], waits=(s_mm,), inc=s_ev)

        em.op("scalar", "activation", musq[:, :], mu_sb[:, :], AF.Square,
              waits=((ld, ldv),), inc=s_act)
        for c in range(8):
            em.op("tensor", "matmul", plg[0:1, 0:ES],
                  onescol_b[:, :], msv[:, c, :],
                  start=(c == 0), stop=(c == 7),
                  waits=((s_act, s_act.n),) if c == 0 else (),
                  inc=(s_mm if c == 7 else None))
        em.op("vector", "tensor_scalar_max", rn[0:1, :], plg[0:1, 0:ES],
              1e-24, waits=(s_mm,))
        em.op("vector", "engine_nop", inc=s_dve)
        em.op("scalar", "activation", rn[0:1, :], rn[0:1, :], AF.Ln,
              waits=(s_dve,))
        em.op("scalar", "activation", rn[0:1, :], rn[0:1, :], AF.Exp,
              scale=-0.5, inc=s_act)
        em.op("vector", "tensor_scalar_mul", rn[0:1, :], rn[0:1, :],
              screw[0:1, 0:1], waits=(s_act,), inc=s_dve)
        em.op("tensor", "matmul", pbc[:, 0:ES], ones_f[0:1, :], rn[0:1, :],
              start=True, stop=True, waits=((s_dve, s_dve.n), (s_ev, s_ev.n)),
              inc=s_mm)
        em.op("vector", "tensor_copy", rnbc[:, :], pbc[:, 0:ES],
              waits=(s_mm,), inc=s_ev)
        for c in range(8):
            em.op("vector", "tensor_mul", mbv[:, c, :], muv[:, c, :],
                  rnbc[:, :], inc=(s_dve if c == 7 else None))
        mb_dve = s_dve.n

        for t in range(NT):
            i = t % 2
            em.dma("sync", xt[i][:, :], xv_t[t], ldx[i],
                   waits=((s_dve, s_dve.n),))
            em.wait("scalar", ldx[i], ldx[i].n)
            emit_ln_tile(em, xt[i][:, :], None, scr[:, :], cols[:, :],
                         s_act, s_dve, kc[:, 0:1], zf32_ap=zf[:, :])
            emit_ln_tile(em, zf[:, :], xnb[:, :], scr[:, :], cols[:, :],
                         s_act, s_dve, kc[:, 0:1], gain_ap=gbc[:, :],
                         bias_ap=bbc[:, :], zf32_ap=xnf[:, :],
                         in_waits=((s_dve, s_dve.n), (s_ev, s_ev.n)),
                         out_waits=((s_nm, s_nm.n),))
            em.op("scalar", "activation", scr[:, :], xnf[:, :], AF.Square,
                  accum_out=cols[:, 5:6], waits=((s_dve, s_dve.n),),
                  inc=s_act)
            em.op("vector", "tensor_scalar_max", cols[:, 6:7], cols[:, 5:6],
                  kc[:, 1:2], waits=((s_act, s_act.n),))
            em.op("vector", "engine_nop", inc=s_dve)
            em.op("scalar", "activation", cols[:, 6:7], cols[:, 6:7], AF.Ln,
                  waits=(s_dve,))
            em.op("scalar", "activation", cols[:, 6:7], cols[:, 6:7], AF.Exp,
                  scale=-0.5, inc=s_act)
            em.op("vector", "tensor_scalar_mul", xub[:, :], xnf[:, :],
                  cols[:, 6:7], waits=(s_act,), inc=s_dve)
            for c in range(8):
                em.op("tensor", "transpose", pst[:, 0:128],
                      xub[:, 128 * c:128 * c + 128], id_sb[:, :],
                      waits=((s_dve, s_dve.n), (s_ev, s_ev.n)), inc=s_pe)
                em.op("vector", "tensor_copy", xuT[:, 128 * c:128 * c + 128],
                      pst[:, 0:128], waits=(s_pe,), inc=s_ev)
            for c in range(8):
                em.op("tensor", "matmul", plg[:, :],
                      xuT[:, 128 * c:128 * c + 128], mbv[:, c, :],
                      start=(c == 0), stop=(c == 7),
                      waits=((s_ev, s_ev.n), (s_dve, mb_dve),
                             (s_act, s_act.n))
                            if c == 0 else (),
                      inc=(s_mm if c == 7 else None))
            em.op("scalar", "activation", wv[:, t, :], plg[:, :], AF.Exp,
                  waits=((s_mm, s_mm.n),), inc=s_act)
            for u in range(2):
                for j in range(2):
                    em.op("tensor", "matmul", pnum[2 * u + j][:, :],
                          wv[:, t, 128 * u:128 * u + 128],
                          xnb[:, 512 * j:512 * j + 512],
                          start=(t == 0), stop=(t == NT - 1),
                          waits=((s_act, s_act.n), (s_dve, s_dve.n))
                                if j == 0 else (),
                          inc=s_nm)
                em.op("tensor", "matmul", pden[:, u:u + 1],
                      wv[:, t, 128 * u:128 * u + 128], onescol_b[:, :],
                      start=(t == 0 and u == 0),
                      stop=(t == NT - 1 and u == 1), inc=s_nm)
        for u in range(2):
            for j in range(2):
                em.op("vector", "tensor_copy",
                      numsb[:, 1024 * u + 512 * j:1024 * u + 512 * j + 512],
                      pnum[2 * u + j][:, :], waits=((s_nm, s_nm.n),),
                      inc=s_dve)
        em.op("vector", "tensor_copy", densb[:, :], pden[:, 0:2],
              waits=((s_nm, s_nm.n),), inc=s_dve)
        em.wait("sync", s_dve, s_dve.n)
        em.wait("sync", s_act, s_act.n)
        em.dma("sync", num_d.rearrange("(u p) d -> p u d", p=128),
               numsb[:, :].rearrange("p (u d) -> p u d", u=2), s_st)
        em.dma("sync", den_d, densb[:, :], s_st)
        em.dma("sync", w_d.rearrange("(t p) n -> p t n", p=128), wv, s_st)
        em.wait("gpsimd", s_st, s_st.n)

        if _own:
            with nc.Block() as block:
                em.finalize(block)
    return nc


def build_l1ffn_comb(fx=None, nsum=4):
    ES, HH = 256, 256
    nc, _em0, pre, _own = _fx_open(fx)
    x_d = _fio(fx, nc, "x", [TC, D], F32, "ExternalInput")
    nums_d = _fio(fx, nc, "nums", [nsum, ES, D], F32, "ExternalInput")
    dens_d = _fio(fx, nc, "dens", [nsum, 128, 2], F32, "ExternalInput")
    w_dd = _fio(fx, nc, "w", [TC, ES], BF16, "ExternalInput")
    w1_d = _fio(fx, nc, "w1", [128, 8 * 16 * HH], F32, "ExternalInput")
    b1_d = _fio(fx, nc, "b1c", [128, 32], F32, "ExternalInput")
    w2_d = _fio(fx, nc, "w2", [128, 2 * 16 * D], F32, "ExternalInput")
    b2_d = _fio(fx, nc, "b2", [16, D], F32, "ExternalInput")
    sel_d = _fio(fx, nc, "sel", [16, ES], BF16, "ExternalInput")
    id_d = _fio(fx, nc, "ident", [128, 128], BF16, "ExternalInput")
    xo_d = _fio(fx, nc, "xo", [TC, D], F32, "ExternalOutput")

    with ExitStack() as stack:
        em = _em0 if _em0 is not None else Em(nc, stack)
        sb = lambda nm, sh, dt=F32: stack.enter_context(nc.sbuf_tensor(pre + nm, sh, dt))
        ps = lambda nm, sh, dt=F32: stack.enter_context(nc.psum_tensor(pre + nm, sh, dt))

        w1 = sb("w1sb", [128, 8 * 16 * HH], BF16)
        w2 = sb("w2sb", [128, 2 * 16 * D], BF16)
        b2r = sb("b2r", [16, D], BF16)
        selr = sb("selr", [16, ES], BF16)
        b1c = sb("b1c_sb", [128, 32])
        num = sb("num", [128, 2 * D])
        den = sb("den", [128, 2])
        sin_b = sb("sin_b", [128, 2 * D], BF16)
        sinT = sb("sinT", [128, 8 * ES], BF16)
        htb = [sb(f"htb{u}", [128, 16 * 128], BF16) for u in range(2)]
        soub = sb("soub", [128, 2 * D], BF16)
        wt = [sb(f"wt{i}", [128, ES], BF16) for i in range(2)]
        combT = sb("combT", [128, 2 * 128], BF16)
        xt = [sb(f"xt{i}", [128, D]) for i in range(2)]
        rs = sb("rs", [128, 4])
        id_sb = sb("id_sb", [128, 128], BF16)
        ones_b = sb("ones_b", [1, 128], BF16)

        pst = ps("pst", [128, 1024], BF16)
        ph = [ps(f"ph{i}", [128, 512]) for i in range(2)]
        po = [ps(f"po{i}", [128, 512]) for i in range(2)]

        ld = em.sem("ld")
        ldg = em.sem("ldg")
        ldx = [em.sem("ldx0"), em.sem("ldx1")]
        ldw = [em.sem("ldw0"), em.sem("ldw1")]
        s_act = em.sem("act")
        s_dve = em.sem("dve")
        s_pe = em.sem("pe")
        s_ev = em.sem("ev")
        s_mm = em.sem("mm")
        s_stp = [em.sem("st0"), em.sem("st1")]

        w1v = w1[:, :].rearrange("p (c e n) -> p c e n", c=8, e=16)
        w2v = w2[:, :].rearrange("p (c e n) -> p c e n", c=2, e=16)
        xv_t = x_d.rearrange("(t p) d -> t p d", p=128)
        wv_t = w_dd.rearrange("(t p) n -> t p n", p=128)
        xov_t = xo_d.rearrange("(t p) d -> t p d", p=128)

        em.op("vector", "memset", ones_b[:, :], 1.0)
        em.op("vector", "memset", htb[0][:, :], 0.0)
        em.op("vector", "memset", htb[1][:, :], 0.0, inc=s_dve)
        em.dma("sync", b1c[:, :], b1_d, ld)
        em.dma("sync", id_sb[:, :], id_d, ld)
        em.dma("gpsimd", b2r[:, :], b2_d, ldg)
        em.dma("gpsimd", selr[:, :], sel_d, ldg)
        em.dma("gpsimd", w1[:, :], w1_d, ldg)
        em.dma("gpsimd", w2[:, :], w2_d, ldg)
        em.dma("gpsimd", num[:, :].rearrange("p (u d) -> p u d", u=2),
               nums_d[0].rearrange("(u p) d -> p u d", p=128), ldg)
        em.dma("gpsimd", den[:, :], dens_d[0], ldg)
        for i in range(1, nsum):
            em.op("gpsimd", "dma_start",
                  out=num[:, :].rearrange("p (u d) -> p u d", u=2),
                  in_=nums_d[i].rearrange("(u p) d -> p u d", p=128),
                  accum_op=ALU.add, waits=((ldg, ldg.n),), inc=ldg, incv=16)
            em.op("gpsimd", "dma_start", out=den[:, :], in_=dens_d[i],
                  accum_op=ALU.add, waits=((ldg, ldg.n),), inc=ldg, incv=16)
        ldv, ldgv = ld.n, ldg.n
        em.wait("tensor", ld, ldv)
        em.wait("tensor", ldg, ldgv)
        em.wait("vector", ldg, ldgv)

        em.op("vector", "reciprocal", rs[:, 0:2], den[:, :], inc=s_dve)
        for u in range(2):
            em.op("vector", "tensor_scalar_mul",
                  sin_b[:, 1024 * u:1024 * u + 1024],
                  num[:, 1024 * u:1024 * u + 1024], rs[:, u:u + 1],
                  inc=s_dve)
        for u in range(2):
            for c in range(8):
                em.op("tensor", "transpose", pst[:, 0:128],
                      sin_b[:, 1024 * u + 128 * c:1024 * u + 128 * c + 128],
                      id_sb[:, :],
                      waits=((s_dve, s_dve.n), (s_ev, s_ev.n)), inc=s_pe)
                em.op("vector", "tensor_copy",
                      sinT[:, 256 * c + 128 * u:256 * c + 128 * u + 128],
                      pst[:, 0:128], waits=(s_pe,), inc=s_ev)
        sinT_ev = s_ev.n
        sinTv = sinT[:, :].rearrange("p (c n) -> p c n", c=8)

        for e in range(16):
            u_e, r = e // 8, e % 8
            for sub in range(2):
                bk = (2 * e + sub) % 2
                for c in range(8):
                    em.op("tensor", "matmul", ph[bk][:, 0:16],
                          w1v[:, c, e, 128 * sub:128 * sub + 128],
                          sinTv[:, c, 16 * e:16 * e + 16],
                          start=(c == 0), stop=(c == 7),
                          waits=((s_ev, sinT_ev),
                                 (s_act, max(0, s_act.n - 1)))
                                if c == 0 else (),
                          inc=(s_mm if c == 7 else None))
                kc_ = 2 * r + sub
                em.op("scalar", "activation",
                      htb[u_e][:, 128 * kc_ + 16 * r:128 * kc_ + 16 * r + 16],
                      ph[bk][:, 0:16], AF.Gelu,
                      bias=b1c[:, 2 * e + sub:2 * e + sub + 1],
                      waits=(s_mm,), inc=s_act)
        ht_act = s_act.n

        for u in range(2):
            for j in range(2):
                bk = (2 * u + j) % 2
                for kc_ in range(16):
                    r, sub = kc_ // 2, kc_ % 2
                    em.op("tensor", "matmul", po[bk][:, :],
                          htb[u][:, 128 * kc_:128 * kc_ + 128],
                          w2v[:, sub, 8 * u + r, 512 * j:512 * j + 512],
                          start=(kc_ == 0), stop=False,
                          waits=((s_act, ht_act), (s_dve, s_dve.n))
                                if kc_ == 0 else ())
                em.op("tensor", "matmul", po[bk][:, :],
                      selr[0:16, 128 * u:128 * u + 128],
                      b2r[0:16, 512 * j:512 * j + 512],
                      start=False, stop=True, inc=s_mm)
                em.op("vector", "tensor_copy",
                      soub[:, 1024 * u + 512 * j:1024 * u + 512 * j + 512],
                      po[bk][:, :], waits=(s_mm,), inc=s_dve)
        so_dve = s_dve.n

        for t in range(NT):
            i = t % 2
            em.dma("sync", xt[i][:, :], xv_t[t], ldx[i],
                   waits=((s_dve, s_dve.n), (s_stp[i], s_stp[i].n)))
            ldxv = ldx[i].n
            em.dma("sync", wt[i][:, :], wv_t[t], ldw[i],
                   waits=((s_pe, s_pe.n),))
            em.wait("vector", ldw[i], ldw[i].n)
            em.op("vector", "reduce_sum", rs[:, 2:3], wt[i][:, :], axis=AX.X)
            em.op("vector", "reciprocal", rs[:, 3:4], rs[:, 2:3])
            em.op("vector", "tensor_scalar", wt[i][:, :], wt[i][:, :],
                  rs[:, 3:4], 0.5, ALU.mult, ALU.mult, inc=s_dve)
            for u in range(2):
                em.op("tensor", "transpose", pst[:, 0:128],
                      wt[i][:, 128 * u:128 * u + 128], id_sb[:, :],
                      waits=((s_dve, s_dve.n), (s_ev, s_ev.n)), inc=s_pe)
                em.op("vector", "tensor_copy",
                      combT[:, 128 * u:128 * u + 128],
                      pst[:, 0:128], waits=(s_pe,), inc=s_ev)
            cT_ev = s_ev.n
            for j in range(2):
                bk = j
                for u in range(2):
                    em.op("tensor", "matmul", po[bk][:, :],
                          combT[:, 128 * u:128 * u + 128],
                          soub[:, 1024 * u + 512 * j:1024 * u + 512 * j + 512],
                          start=(u == 0), stop=(u == 1),
                          waits=((s_ev, cT_ev), (s_dve, max(so_dve, s_dve.n)))
                                if u == 0 else (),
                          inc=(s_mm if u == 1 else None))
            mmv = s_mm.n
            for j in range(2):
                em.op("vector", "tensor_add", xt[i][:, 512 * j:512 * j + 512],
                      xt[i][:, 512 * j:512 * j + 512], po[j][:, :],
                      waits=((s_mm, mmv), (ldx[i], ldxv)),
                      inc=(s_dve if j == 1 else None))
            em.dma("sync", xov_t[t], xt[i][:, :], s_stp[i],
                   waits=((s_dve, s_dve.n),))
        em.wait("gpsimd", s_stp[0], s_stp[0].n)
        em.wait("gpsimd", s_stp[1], s_stp[1].n)

        if _own:
            with nc.Block() as block:
                em.finalize(block)
    return nc


def step_l1ffn(x_parts, ins):
    common = {
        "mu": np.ascontiguousarray(ins["L1_ffn_mu"].reshape(D, 256)),
        "scale": ins["L1_ffn_scale"].reshape(1, 1),
        "gamma": ins["L1_ffn_ln_gamma"].reshape(1, D),
        "beta": ins["L1_ffn_ln_beta"].reshape(1, D),
        "ident": IDENT_BF,
    }
    res = _launch("l1ffn_part", build_l1ffn_part,
                  [dict(common, x=x_parts[c]) for c in range(NCORES)])
    nums = [np.asarray(res[c]["num"]) for c in range(NCORES)]
    dens = [np.asarray(res[c]["den"]) for c in range(NCORES)]
    ws = [np.asarray(res[c]["w"]) for c in range(NCORES)]

    sel = np.zeros((16, 256), dtype=BF)
    for e in range(16):
        sel[e, 16 * e:16 * e + 16] = 1
    b1c = np.ascontiguousarray(
        ins["L1_ffn_b1"].reshape(16, 2, 128).transpose(2, 0, 1).reshape(128, 32))
    w1l = np.ascontiguousarray(
        ins["L1_ffn_w1"].reshape(16, 8, 128, 256).transpose(2, 1, 0, 3)
        .reshape(128, 8 * 16 * 256))
    w2l = np.ascontiguousarray(
        ins["L1_ffn_w2"].reshape(16, 2, 128, 1024).transpose(2, 1, 0, 3)
        .reshape(128, 2 * 16 * 1024))
    maps = []
    for b in range(2):
        nstack = np.ascontiguousarray(np.stack(nums[4 * b:4 * b + 4]))
        dstack = np.ascontiguousarray(np.stack(dens[4 * b:4 * b + 4]))
        for ci in range(4):
            c = 4 * b + ci
            maps.append({"x": x_parts[c], "nums": nstack, "dens": dstack,
                         "w": ws[c], "w1": w1l, "b1c": b1c,
                         "w2": w2l, "b2": ins["L1_ffn_b2"],
                         "sel": sel, "ident": IDENT_BF})
    res = _launch("l1ffn_comb", build_l1ffn_comb, maps)
    return [np.asarray(res[c]["xo"]) for c in range(NCORES)]


def build_l3band(fx=None):
    H = 16
    nc, _em0, pre, _own = _fx_open(fx)
    x_d = _fio(fx, nc, "x", [TC, D], F32, "ExternalInput")
    qt_d = _fio(fx, nc, "qt", [D, TC], BF16, "ExternalInput")
    ktx_d = _fio(fx, nc, "ktx", [D, TC + 256], BF16, "ExternalInput")
    vbx_d = _fio(fx, nc, "vbx", [128, 18 * D], BF16, "ExternalInput")
    mk_d = _fio(fx, nc, "masks", [NT, 128, 384], F32, "ExternalInput")
    woT_d = _fio(fx, nc, "woT", [D, D], F32, "ExternalInput")
    id_d = _fio(fx, nc, "ident", [128, 128], BF16, "ExternalInput")
    xo_d = _fio(fx, nc, "xo", [TC, D], F32, "ExternalOutput")

    with ExitStack() as stack:
        em = _em0 if _em0 is not None else Em(nc, stack)
        sb = lambda nm, sh, dt=F32: stack.enter_context(nc.sbuf_tensor(pre + nm, sh, dt))
        ps = lambda nm, sh, dt=F32: stack.enter_context(nc.psum_tensor(pre + nm, sh, dt))

        qt_sb = sb("qt_sb", [64, 16 * TC], BF16)
        ktx = sb("ktx_sb", [64, 16 * (TC + 256)], BF16)
        vbx = sb("vbx_sb", [128, 18 * D], BF16)
        wo = sb("wo", [128, 8 * D], BF16)
        mk = [sb(f"mk{i}", [128, 384]) for i in range(2)]
        sm = sb("sm", [128, 384])
        pb = sb("pb", [128, 384], BF16)
        pTb = sb("pTb", [128, 3 * 128], BF16)
        ctx = sb("ctx", [128, D], BF16)
        ctxT = sb("ctxT", [128, D], BF16)
        xt = [sb(f"xt{i}", [128, D]) for i in range(2)]
        cls = sb("cls", [128, 8])
        id_sb = sb("id_sb", [128, 128], BF16)

        pst = ps("pst", [128, 1024], BF16)
        pS = ps("pS", [128, 512])
        pc = [ps(f"pc{i}", [128, 512]) for i in range(2)]
        pw = [ps(f"pw{i}", [128, 512]) for i in range(2)]

        ldg = em.sem("ldg")
        ldx = [em.sem("ldx0"), em.sem("ldx1")]
        ldm = [em.sem("ldm0"), em.sem("ldm1")]
        s_act = em.sem("act")
        s_dve = em.sem("dve")
        s_pe = em.sem("pe")
        s_ev = em.sem("ev")
        s_mm = em.sem("mm")
        s_stp = [em.sem("st0"), em.sem("st1")]
        s_upd = em.sem("upd")

        qtv = qt_sb[:, :].rearrange("p (h t) -> p h t", h=16)
        ktv = ktx[:, :].rearrange("p (h t) -> p h t", h=16)
        vbv = vbx[:, :].rearrange("p (t n) -> p t n", t=18)
        wov = wo[:, :].rearrange("p (c n) -> p c n", c=8)
        xv_t = x_d.rearrange("(t p) d -> t p d", p=128)
        xov_t = xo_d.rearrange("(t p) d -> t p d", p=128)

        em.dma("gpsimd", qtv, qt_d.rearrange("(h p) t -> p h t", p=64), ldg)
        em.dma("gpsimd", ktv, ktx_d.rearrange("(h p) t -> p h t", p=64), ldg)
        em.dma("gpsimd", vbx[:, :], vbx_d, ldg)
        em.dma("gpsimd", wov, woT_d.rearrange("(c p) n -> p c n", p=128), ldg)
        em.dma("sync", id_sb[:, :], id_d, ld2 := em.sem("ld2"))
        ld2v = ld2.n
        ldgv = ldg.n
        em.wait("tensor", ldg, ldgv)
        em.wait("tensor", ld2, ld2v)

        for t in range(NT):
            i = t % 2
            em.dma("sync", xt[i][:, :], xv_t[t], ldx[i],
                   waits=((s_upd, s_upd.n), (s_stp[i], s_stp[i].n)))
            ldxv = ldx[i].n
            em.dma("sync", mk[i][:, :], mk_d[t], ldm[i],
                   waits=((s_dve, s_dve.n),))
            ldmv = ldm[i].n
            for h in range(H):
                em.op("tensor", "matmul", pS[:, 0:384],
                      qtv[:, h, 128 * t:128 * t + 128],
                      ktv[:, h, 128 * t:128 * t + 384],
                      start=True, stop=True,
                      waits=((s_dve, s_dve.n), (ldg, ldgv)), inc=s_mm)
                em.op("vector", "tensor_add", sm[:, :], pS[:, 0:384],
                      mk[i][:, :], waits=((s_mm, s_mm.n), (ldm[i], ldmv)))
                em.op("vector", "reduce_max", cls[:, 0:1], sm[:, :],
                      axis=AX.X)
                em.op("vector", "tensor_scalar_mul", cls[:, 1:2], cls[:, 0:1],
                      -1.0, inc=s_dve)
                em.op("scalar", "activation", pb[:, :], sm[:, :], AF.Exp,
                      bias=cls[:, 1:2], accum_out=cls[:, 2:3],
                      waits=((s_dve, s_dve.n),), inc=s_act)
                em.op("vector", "reciprocal", cls[:, 3:4], cls[:, 2:3],
                      waits=(s_act,), inc=s_dve)
                for c in range(3):
                    em.op("tensor", "transpose", pst[:, 0:128],
                          pb[:, 128 * c:128 * c + 128], id_sb[:, :],
                          waits=((s_dve, s_dve.n), (s_ev, s_ev.n)), inc=s_pe)
                    em.op("vector", "tensor_copy",
                          pTb[:, 128 * c:128 * c + 128], pst[:, 0:128],
                          waits=(s_pe,), inc=s_ev)
                cb = pc[h % 2]
                for c in range(3):
                    em.op("tensor", "matmul", cb[:, 0:64],
                          pTb[:, 128 * c:128 * c + 128],
                          vbv[:, t + c, 64 * h:64 * h + 64],
                          start=(c == 0), stop=(c == 2),
                          waits=((s_ev, s_ev.n), (s_dve, s_dve.n))
                                if c == 0 else (),
                          inc=(s_mm if c == 2 else None))
                em.op("vector", "tensor_scalar_mul", ctx[:, 64 * h:64 * h + 64],
                      cb[:, 0:64], cls[:, 3:4], waits=(s_mm,), inc=s_dve)
            for c in range(8):
                em.op("tensor", "transpose", pst[:, 0:128],
                      ctx[:, 128 * c:128 * c + 128], id_sb[:, :],
                      waits=((s_dve, s_dve.n), (s_ev, s_ev.n)), inc=s_pe)
                em.op("vector", "tensor_copy", ctxT[:, 128 * c:128 * c + 128],
                      pst[:, 0:128], waits=(s_pe,), inc=s_ev)
            cT_ev = s_ev.n
            for j in range(2):
                for c in range(8):
                    em.op("tensor", "matmul", pw[j][:, :],
                          ctxT[:, 128 * c:128 * c + 128],
                          wov[:, c, 512 * j:512 * j + 512],
                          start=(c == 0), stop=(c == 7),
                          waits=((s_ev, cT_ev), (s_upd, s_upd.n))
                                if c == 0 else (),
                          inc=(s_mm if c == 7 else None))
            mmv = s_mm.n
            for j in range(2):
                em.op("vector", "tensor_add", xt[i][:, 512 * j:512 * j + 512],
                      xt[i][:, 512 * j:512 * j + 512], pw[j][:, :],
                      waits=((s_mm, mmv), (ldx[i], ldxv)),
                      inc=(s_upd if j == 1 else None))
            em.dma("sync", xov_t[t], xt[i][:, :], s_stp[i],
                   waits=((s_upd, s_upd.n),))
        em.wait("gpsimd", s_stp[0], s_stp[0].n)
        em.wait("gpsimd", s_stp[1], s_stp[1].n)

        if _own:
            with nc.Block() as block:
                em.finalize(block)
    return nc


def _attn_proj_maps(x_parts, wq, bq, wk, bk, wv, bv, scaleq=True):
    common = {"wqT": np.ascontiguousarray(wq.T),
              "wkT": np.ascontiguousarray(wk.T),
              "wvT": np.ascontiguousarray(wv.T),
              "bqc": np.ascontiguousarray(bq.reshape(8, 128).T),
              "bkc": np.ascontiguousarray(bk.reshape(8, 128).T),
              "bv": bv.reshape(1, D), "ident": IDENT_BF}
    return [dict(common, x=x_parts[c]) for c in range(NCORES)]


def _l3_masks(core):
    lp, rp = 8, 7
    t0 = (core % 4) * TC
    masks = np.full((NT, 128, 384), -1e30, dtype=np.float32)
    for t in range(NT):
        for qi in range(128):
            gq = t0 + 128 * t + qi
            for w in range(16):
                gk = gq - lp + w
                if 0 <= gk < S:
                    rel = gk - (t0 + 128 * t - 128)
                    if 0 <= rel < 384:
                        masks[t, qi, rel] = 0.0
    return masks


_L3_MASKS = None


def step_l3attn(x_parts, ins):
    maps = _attn_proj_maps(x_parts, ins["L3_attn_Wq"],
                           np.zeros(D, np.float32), ins["L3_attn_Wk"],
                           np.zeros(D, np.float32), ins["L3_attn_Wv"],
                           np.zeros(D, np.float32))
    res = _launch("l2attn_proj", build_l2attn_proj, maps)
    qts = [np.asarray(res[c]["qt"]) for c in range(NCORES)]
    kts = [np.asarray(res[c]["kt"]) for c in range(NCORES)]
    vs = [np.asarray(res[c]["v"]) for c in range(NCORES)]

    global _L3_MASKS
    if _L3_MASKS is None:
        _L3_MASKS = [_l3_masks(c) for c in range(NCORES)]
    woT = np.ascontiguousarray(ins["L3_attn_Wo"].T)
    zt_kt = np.zeros((D, 128), dtype=BF)
    zt_v = np.zeros((128, D), dtype=BF)
    maps = []
    for c in range(NCORES):
        left_k = kts[c - 1][:, -128:] if c % 4 != 0 else zt_kt
        right_k = kts[c + 1][:, :128] if c % 4 != 3 else zt_kt
        ktx = np.ascontiguousarray(
            np.concatenate([left_k, kts[c], right_k], axis=1))
        left_v = vs[c - 1][-128:] if c % 4 != 0 else zt_v
        right_v = vs[c + 1][:128] if c % 4 != 3 else zt_v
        vb = np.concatenate([left_v, vs[c], right_v], 0)
        vbx = np.ascontiguousarray(
            vb.reshape(18, 128, D).transpose(1, 0, 2).reshape(128, 18 * D))
        maps.append({"x": x_parts[c], "qt": qts[c], "ktx": ktx, "vbx": vbx,
                     "masks": _L3_MASKS[c], "woT": woT, "ident": IDENT_BF})
    res = _launch("l3band", build_l3band, maps)
    return [np.asarray(res[c]["xo"]) for c in range(NCORES)]


def build_l1band(fx=None):
    H = 16
    nc, _em0, pre, _own = _fx_open(fx)
    qt_d = _fio(fx, nc, "qt", [D, TC], BF16, "ExternalInput")
    ktx_d = _fio(fx, nc, "ktx", [D, TC + 256], BF16, "ExternalInput")
    k0_d = _fio(fx, nc, "k0col", [64, 16], BF16, "ExternalInput")
    q0_d = _fio(fx, nc, "q0col", [64, 16], BF16, "ExternalInput")
    vbx_d = _fio(fx, nc, "vbx", [128, 18 * D], BF16, "ExternalInput")
    v0_d = _fio(fx, nc, "v0row", [1, D], BF16, "ExternalInput")
    mk_d = _fio(fx, nc, "masks", [NT, 128, 385], F32, "ExternalInput")
    b0_d = _fio(fx, nc, "bias0", [128, 16], F32, "ExternalInput")
    id_d = _fio(fx, nc, "ident", [128, 128], BF16, "ExternalInput")
    ctx_d = _fio(fx, nc, "ctxu", [TC, D], BF16, "ExternalOutput")
    den_d = _fio(fx, nc, "dens", [128, NT * 16], F32, "ExternalOutput")
    r0_d = _fio(fx, nc, "r0p", [1, 1040], F32, "ExternalOutput")
    vs_d = _fio(fx, nc, "vsum", [1, D], F32, "ExternalOutput")

    with ExitStack() as stack:
        em = _em0 if _em0 is not None else Em(nc, stack)
        sb = lambda nm, sh, dt=F32: stack.enter_context(nc.sbuf_tensor(pre + nm, sh, dt))
        ps = lambda nm, sh, dt=F32: stack.enter_context(nc.psum_tensor(pre + nm, sh, dt))

        qt_sb = sb("qt_sb", [64, 16 * TC], BF16)
        ktx = sb("ktx_sb", [64, 16 * (TC + 256)], BF16)
        vbx = sb("vbx_sb", [128, 18 * D], BF16)
        k0c = sb("k0c", [64, 16], BF16)
        q0c = sb("q0c", [64, 16], BF16)
        v0r = sb("v0r", [1, D], BF16)
        mk = [sb(f"mk{i}", [128, 385]) for i in range(2)]
        b0c = sb("b0c", [128, 16])
        sm = sb("sm", [128, 385])
        m1 = sb("m1", [128, 385], BF16)
        ebuf = sb("ebuf", [128, 385], BF16)
        pbuf = sb("pbuf", [128, 385], BF16)
        pTb = sb("pTb", [128, 3 * 128], BF16)
        pT0 = sb("pT0", [1, 128], BF16)
        ctx = sb("ctx", [128, D], BF16)
        densb = sb("densb", [128, NT * 16])
        sm0 = sb("sm0", [128, 16])
        m10 = sb("m10", [128, 16], BF16)
        p0b = sb("p0b", [128, 16], BF16)
        p0s = sb("p0s", [128, 1])
        r0sb = sb("r0sb", [1, 1040])
        vssb = sb("vssb", [1, D])
        onescol_b = sb("onescol_b", [128, 1], BF16)
        onescol_f = sb("onescol_f", [128, 1])
        kc = sb("kc", [128, 8])
        id_sb = sb("id_sb", [128, 128], BF16)

        pst = ps("pst", [128, 1024], BF16)
        pS = ps("pS", [128, 512])
        pc = [ps(f"pc{i}", [128, 512]) for i in range(2)]

        ldg = em.sem("ldg")
        ld2 = em.sem("ld2")
        ldm = [em.sem("ldm0"), em.sem("ldm1")]
        s_act = em.sem("act")
        s_dve = em.sem("dve")
        s_pe = em.sem("pe")
        s_ev = em.sem("ev")
        s_mm = em.sem("mm")
        s_st = em.sem("st")

        qtv = qt_sb[:, :].rearrange("p (h t) -> p h t", h=16)
        ktv = ktx[:, :].rearrange("p (h t) -> p h t", h=16)
        vbv = vbx[:, :].rearrange("p (t n) -> p t n", t=18)
        ctxv_d = ctx_d.rearrange("(t p) d -> t p d", p=128)

        em.op("vector", "memset", kc[:, :], 0.0)
        em.op("vector", "memset", kc[:, 1:2], -30.0)
        em.op("vector", "memset", onescol_b[:, :], 1.0)
        em.op("vector", "memset", onescol_f[:, :], 1.0, inc=s_dve)
        em.dma("gpsimd", qtv, qt_d.rearrange("(h p) t -> p h t", p=64), ldg)
        em.dma("gpsimd", ktv, ktx_d.rearrange("(h p) t -> p h t", p=64), ldg)
        em.dma("gpsimd", vbx[:, :], vbx_d, ldg)
        em.dma("sync", k0c[:, :], k0_d, ld2)
        em.dma("sync", q0c[:, :], q0_d, ld2)
        em.dma("sync", v0r[:, :], v0_d, ld2)
        em.dma("sync", b0c[:, :], b0_d, ld2)
        em.dma("sync", id_sb[:, :], id_d, ld2)
        ldgv, ld2v = ldg.n, ld2.n
        em.wait("tensor", ldg, ldgv)
        em.wait("tensor", ld2, ld2v)
        em.wait("scalar", ld2, ld2v)

        for j in range(2):
            for tt in range(NT):
                em.op("tensor", "matmul", pc[j][0:1, :], onescol_b[:, :],
                      vbv[:, 1 + tt, 512 * j:512 * j + 512],
                      start=(tt == 0), stop=(tt == NT - 1),
                      waits=((s_dve, s_dve.n),) if tt == 0 else (),
                      inc=(s_mm if tt == NT - 1 else None))
            em.op("vector", "tensor_copy", vssb[0:1, 512 * j:512 * j + 512],
                  pc[j][0:1, :], waits=(s_mm,), inc=s_dve)
        em.wait("sync", s_dve, s_dve.n)
        em.dma("sync", vs_d, vssb[:, :], s_st)

        for h in range(H):
            for c in range(NT):
                em.op("tensor", "matmul", pS[:, c:c + 1],
                      ktv[:, h, 128 + 128 * c:256 + 128 * c],
                      q0c[:, h:h + 1],
                      start=(c == 0), stop=(c == NT - 1),
                      waits=((s_dve, s_dve.n), (s_act, s_act.n))
                            if c == 0 else (),
                      inc=(s_mm if c == NT - 1 else None))
            em.op("vector", "tensor_add", sm0[:, :], pS[:, 0:16], b0c[:, :],
                  waits=(s_mm,))
            em.op("vector", "tensor_scalar", m10[:, :], sm0[:, :], 0.0, None,
                  ALU.is_ge, inc=s_dve)
            em.op("scalar", "activation", p0b[:, :], sm0[:, :], AF.Exp,
                  bias=kc[:, 1:2], waits=(s_dve,), inc=s_act)
            em.op("vector", "scalar_tensor_tensor", p0b[:, :], p0b[:, :],
                  1.0, m10[:, :], ALU.mult, ALU.mult,
                  accum_out=p0s[:, 0:1], waits=(s_act,), inc=s_dve)
            em.op("tensor", "matmul", pc[0][0:1, 0:1], p0s[:, 0:1],
                  onescol_f[:, :], start=True, stop=True,
                  waits=((s_dve, s_dve.n), (s_act, s_act.n)), inc=s_mm)
            em.op("scalar", "copy", r0sb[0:1, 1024 + h:1025 + h],
                  pc[0][0:1, 0:1], waits=(s_mm,), inc=s_act)
            for c in range(NT):
                em.op("tensor", "matmul", pc[1][0:1, 0:64],
                      p0b[:, c:c + 1], vbv[:, 1 + c, 64 * h:64 * h + 64],
                      start=(c == 0), stop=(c == NT - 1),
                      waits=((s_dve, s_dve.n), (s_act, s_act.n))
                            if c == 0 else (),
                      inc=(s_mm if c == NT - 1 else None))
            em.op("scalar", "copy", r0sb[0:1, 64 * h:64 * h + 64],
                  pc[1][0:1, 0:64], waits=(s_mm,), inc=s_act)
        em.wait("sync", s_act, s_act.n)
        em.dma("sync", r0_d, r0sb[:, :], s_st)

        for t in range(NT):
            i = t % 2
            em.dma("sync", mk[i][:, :], mk_d[t], ldm[i],
                   waits=((s_dve, s_dve.n),))
            ldmv = ldm[i].n
            for h in range(H):
                em.op("tensor", "matmul", pS[:, 0:384],
                      qtv[:, h, 128 * t:128 * t + 128],
                      ktv[:, h, 128 * t:128 * t + 384],
                      start=True, stop=False,
                      waits=((s_dve, s_dve.n), (s_act, s_act.n)))
                em.op("tensor", "matmul", pS[:, 384:385],
                      qtv[:, h, 128 * t:128 * t + 128],
                      k0c[:, h:h + 1],
                      start=False, stop=True, inc=s_mm)
                em.op("vector", "tensor_add", sm[:, :], pS[:, 0:385],
                      mk[i][:, :], waits=((s_mm, s_mm.n), (ldm[i], ldmv)))
                em.op("vector", "tensor_scalar", m1[:, :], sm[:, :], 0.0,
                      None, ALU.is_ge, inc=s_dve)
                em.op("scalar", "activation", ebuf[:, :], sm[:, :], AF.Exp,
                      bias=kc[:, 1:2], waits=(s_dve,), inc=s_act)
                em.op("vector", "scalar_tensor_tensor", pbuf[:, :],
                      ebuf[:, :], 1.0, m1[:, :], ALU.mult, ALU.mult,
                      accum_out=densb[:, 16 * t + h:16 * t + h + 1],
                      waits=(s_act,), inc=s_dve)
                for c in range(3):
                    em.op("tensor", "transpose", pst[:, 0:128],
                          pbuf[:, 128 * c:128 * c + 128], id_sb[:, :],
                          waits=((s_dve, s_dve.n), (s_ev, s_ev.n)), inc=s_pe)
                    em.op("vector", "tensor_copy",
                          pTb[:, 128 * c:128 * c + 128], pst[:, 0:128],
                          waits=(s_pe,), inc=s_ev)
                em.op("tensor", "transpose", pst[0:1, 0:128],
                      pbuf[:, 384:385], id_sb[:, :],
                      waits=((s_ev, s_ev.n),), inc=s_pe)
                em.op("vector", "tensor_copy", pT0[0:1, :], pst[0:1, 0:128],
                      waits=(s_pe,), inc=s_ev)
                pt_ev = s_ev.n
                cb = pc[h % 2]
                for c in range(3):
                    em.op("tensor", "matmul", cb[:, 0:64],
                          pTb[:, 128 * c:128 * c + 128],
                          vbv[:, t + c, 64 * h:64 * h + 64],
                          start=(c == 0), stop=False,
                          waits=((s_ev, pt_ev), (s_act, s_act.n))
                                if c == 0 else ())
                em.op("tensor", "matmul", cb[:, 0:64], pT0[0:1, :],
                      v0r[0:1, 64 * h:64 * h + 64],
                      start=False, stop=True, inc=s_mm)
                em.op("scalar", "copy", ctx[:, 64 * h:64 * h + 64],
                      cb[:, 0:64],
                      waits=((s_mm, s_mm.n), (s_st, s_st.n))
                            if h == 0 else ((s_mm, s_mm.n),),
                      inc=s_act)
            em.wait("sync", s_act, s_act.n)
            em.dma("sync", ctxv_d[t], ctx[:, :], s_st,
                   waits=((s_act, s_act.n),))
        em.wait("vector", s_st, s_st.n)
        em.dma("sync", den_d, densb[:, :], s_st2 := em.sem("st2"),
               waits=((s_dve, s_dve.n),))
        em.wait("gpsimd", s_st, s_st.n)
        em.wait("gpsimd", s_st2, s_st2.n)

        if _own:
            with nc.Block() as block:
                em.finalize(block)
    return nc


def build_l1fin(fx=None, nsum=4):
    nc, _em0, pre, _own = _fx_open(fx)
    x_d = _fio(fx, nc, "x", [TC, D], F32, "ExternalInput")
    ctx_d = _fio(fx, nc, "ctxu", [TC, D], BF16, "ExternalInput")
    den_d = _fio(fx, nc, "dens", [128, NT * 16], F32, "ExternalInput")
    r0_d = _fio(fx, nc, "r0ps", [1, nsum * 1040], F32, "ExternalInput")
    vs_d = _fio(fx, nc, "vsums", [1, nsum * D], F32, "ExternalInput")
    bo1_d = _fio(fx, nc, "blockones", [16, D], F32, "ExternalInput")
    rf_d = _fio(fx, nc, "r0flag", [1, 2], F32, "ExternalInput")
    woT_d = _fio(fx, nc, "woT", [D, D], F32, "ExternalInput")
    bo_d = _fio(fx, nc, "bo", [1, D], F32, "ExternalInput")
    id_d = _fio(fx, nc, "ident", [128, 128], BF16, "ExternalInput")
    xo_d = _fio(fx, nc, "xo", [TC, D], F32, "ExternalOutput")

    with ExitStack() as stack:
        em = _em0 if _em0 is not None else Em(nc, stack)
        sb = lambda nm, sh, dt=F32: stack.enter_context(nc.sbuf_tensor(pre + nm, sh, dt))
        ps = lambda nm, sh, dt=F32: stack.enter_context(nc.psum_tensor(pre + nm, sh, dt))

        wo = sb("wo", [128, 8 * D], BF16)
        borow = sb("borow", [1, D], BF16)
        dens = sb("dens_sb", [128, NT * 16])
        r0p = sb("r0p", [1, nsum * 1040])
        vssb = sb("vssb", [1, nsum * D])
        bo1 = sb("bo1", [16, D])
        rf = sb("rf", [1, 2])
        vmb = sb("vmb", [128, D])
        recc = sb("recc", [128, 16])
        coef = sb("coef", [128, 16])
        finv = sb("finv", [128, 16])
        ctxu = [sb(f"ctxu{i}", [128, D], BF16) for i in range(2)]
        ctxn = sb("ctxn", [128, D])
        ctxb = sb("ctxb", [128, D], BF16)
        ctxT = sb("ctxT", [128, D], BF16)
        xt = [sb(f"xt{i}", [128, D]) for i in range(2)]
        r0f = sb("r0f", [1, D])
        r0tmp = sb("r0tmp", [1, D])
        rec0 = sb("rec0", [1, 16])
        rec0T = sb("rec0T", [16, 1])
        fl0 = sb("fl0", [1, 16])
        fl0T = sb("fl0T", [16, 1])
        kc = sb("kc", [128, 8])
        ones_f = sb("ones_f", [1, 128])
        ones_b = sb("ones_b", [1, 128], BF16)
        id_sb = sb("id_sb", [128, 128], BF16)

        pst = ps("pst", [128, 1024], BF16)
        pbc = ps("pbc", [128, 512])
        pw = [ps(f"pw{i}", [128, 512]) for i in range(2)]

        ld = em.sem("ld")
        ldg = em.sem("ldg")
        ldx = [em.sem("ldx0"), em.sem("ldx1")]
        ldc = [em.sem("ldc0"), em.sem("ldc1")]
        s_act = em.sem("act")
        s_dve = em.sem("dve")
        s_pe = em.sem("pe")
        s_ev = em.sem("ev")
        s_mm = em.sem("mm")
        s_stp = [em.sem("st0"), em.sem("st1")]
        s_upd = em.sem("upd")

        wov = wo[:, :].rearrange("p (c n) -> p c n", c=8)
        xv_t = x_d.rearrange("(t p) d -> t p d", p=128)
        cv_t = ctx_d.rearrange("(t p) d -> t p d", p=128)
        xov_t = xo_d.rearrange("(t p) d -> t p d", p=128)

        em.op("vector", "memset", kc[:, :], 0.0)
        em.op("vector", "memset", kc[:, 1:2], 1e-30)
        em.op("vector", "memset", ones_f[:, :], 1.0)
        em.op("vector", "memset", ones_b[:, :], 1.0, inc=s_dve)
        em.dma("sync", dens[:, :], den_d, ld)
        em.dma("sync", r0p[:, :], r0_d, ld)
        em.dma("sync", vssb[:, :], vs_d, ld)
        em.dma("sync", bo1[:, :], bo1_d, ld)
        em.dma("sync", rf[:, :], rf_d, ld)
        em.dma("sync", id_sb[:, :], id_d, ld)
        em.dma("gpsimd", wov, woT_d.rearrange("(c p) n -> p c n", p=128), ldg)
        em.dma("gpsimd", borow[:, :], bo_d, ldg)
        ldv, ldgv = ld.n, ldg.n
        em.wait("tensor", ld, ldv)
        em.wait("tensor", ldg, ldgv)
        em.wait("vector", ld, ldv)

        for i in range(1, nsum):
            em.op("vector", "tensor_add", vssb[0:1, 0:D], vssb[0:1, 0:D],
                  vssb[0:1, D * i:D * i + D])
            em.op("vector", "tensor_add", r0p[0:1, 0:1040], r0p[0:1, 0:1040],
                  r0p[0:1, 1040 * i:1040 * i + 1040])
        em.op("vector", "tensor_scalar_mul", vssb[0:1, 0:D], vssb[0:1, 0:D],
              1.0 / S, inc=s_dve)
        for j in range(2):
            em.op("tensor", "matmul", pbc[:, :], ones_f[0:1, :],
                  vssb[0:1, 512 * j:512 * j + 512], start=True, stop=True,
                  waits=((s_dve, s_dve.n), (s_ev, s_ev.n)), inc=s_mm)
            em.op("vector", "tensor_copy", vmb[:, 512 * j:512 * j + 512],
                  pbc[:, :], waits=(s_mm,), inc=s_ev)
        em.op("vector", "tensor_scalar_max", rec0[0:1, :],
              r0p[0:1, 1024:1040], 1e-30, waits=((s_dve, s_dve.n),))
        em.op("vector", "reciprocal", rec0[0:1, :], rec0[0:1, :])
        em.op("vector", "tensor_scalar", fl0[0:1, :], r0p[0:1, 1024:1040],
              0.0, None, ALU.is_gt, inc=s_dve)
        rec0_dram = nc.dram_tensor(pre + "rec0_scr", [1, 16], F32)
        fl0_dram = nc.dram_tensor(pre + "fl0_scr", [1, 16], F32)
        em.dma("sync", rec0_dram.ap(), rec0[0:1, :], sc1 := em.sem("sc1"),
               waits=((s_dve, s_dve.n),))
        em.dma("sync", fl0_dram.ap(), fl0[0:1, :], sc1)
        em.dma("sync", rec0T[:, :],
               rec0_dram.ap().rearrange("o (p u) -> (o p) u", u=1),
               sc2 := em.sem("sc2"), waits=((sc1, sc1.n),))
        em.dma("sync", fl0T[:, :],
               fl0_dram.ap().rearrange("o (p u) -> (o p) u", u=1), sc2)
        for j in range(2):
            em.op("tensor", "matmul", pbc[0:1, 0:512],
                  rec0T[:, :], bo1[0:16, 512 * j:512 * j + 512],
                  start=True, stop=True,
                  waits=((sc2, sc2.n), (s_ev, s_ev.n)), inc=s_mm)
            em.op("vector", "tensor_copy", r0tmp[0:1, 512 * j:512 * j + 512],
                  pbc[0:1, 0:512], waits=(s_mm,), inc=s_ev)
            em.op("tensor", "matmul", pbc[0:1, 0:512],
                  fl0T[:, :], bo1[0:16, 512 * j:512 * j + 512],
                  start=True, stop=True, waits=((s_ev, s_ev.n),), inc=s_mm)
            em.op("vector", "tensor_copy", r0f[0:1, 512 * j:512 * j + 512],
                  pbc[0:1, 0:512], waits=(s_mm,), inc=s_ev)
        em.op("vector", "tensor_mul", r0tmp[0:1, :], r0tmp[0:1, :],
              r0p[0:1, 0:1024], waits=((s_ev, s_ev.n),))
        em.op("vector", "tensor_mul", r0tmp[0:1, :], r0tmp[0:1, :],
              r0f[0:1, :])
        em.op("vector", "tensor_scalar", r0f[0:1, :], r0f[0:1, :], -1.0, 1.0,
              ALU.mult, ALU.add)
        em.op("vector", "tensor_mul", r0f[0:1, :], r0f[0:1, :], vmb[0:1, :])
        em.op("vector", "tensor_add", r0tmp[0:1, :], r0tmp[0:1, :],
              r0f[0:1, :], inc=s_dve)
        r0_dve = s_dve.n

        for t in range(NT):
            i = t % 2
            em.dma("sync", xt[i][:, :], xv_t[t], ldx[i],
                   waits=((s_upd, s_upd.n), (s_stp[i], s_stp[i].n)))
            ldxv = ldx[i].n
            em.dma("sync", ctxu[i][:, :], cv_t[t], ldc[i],
                   waits=((s_dve, s_dve.n),))
            ldcv = ldc[i].n
            em.op("vector", "tensor_scalar_max", recc[:, :],
                  dens[:, 16 * t:16 * t + 16], kc[:, 1:2],
                  waits=((ldc[i], ldcv),))
            em.op("vector", "reciprocal", recc[:, :], recc[:, :])
            em.op("vector", "tensor_scalar", coef[:, :],
                  dens[:, 16 * t:16 * t + 16], 0.0, None, ALU.is_gt)
            em.op("vector", "tensor_scalar", finv[:, :], coef[:, :], -1.0,
                  1.0, ALU.mult, ALU.add)
            em.op("vector", "tensor_mul", coef[:, :], coef[:, :], recc[:, :])
            for h in range(16):
                em.op("vector", "tensor_scalar_mul",
                      ctxn[:, 64 * h:64 * h + 64],
                      ctxu[i][:, 64 * h:64 * h + 64], coef[:, h:h + 1])
                em.op("vector", "scalar_tensor_tensor",
                      ctxn[:, 64 * h:64 * h + 64], vmb[:, 64 * h:64 * h + 64],
                      finv[:, h:h + 1], ctxn[:, 64 * h:64 * h + 64],
                      ALU.mult, ALU.add)
            if t == 0:
                em.op("vector", "tensor_scalar_mul", r0f[0:1, :],
                      r0tmp[0:1, :], rf[0:1, 0:1],
                      waits=((s_dve, r0_dve),))
                em.op("vector", "tensor_scalar", ctxn[0:1, :], ctxn[0:1, :],
                      rf[0:1, 1:2], None, ALU.mult)
                em.op("vector", "tensor_add", ctxn[0:1, :], ctxn[0:1, :],
                      r0f[0:1, :])
            em.op("vector", "tensor_copy", ctxb[:, :], ctxn[:, :],
                  inc=s_dve)
            for c in range(8):
                em.op("tensor", "transpose", pst[:, 0:128],
                      ctxb[:, 128 * c:128 * c + 128], id_sb[:, :],
                      waits=((s_dve, s_dve.n), (s_ev, s_ev.n)), inc=s_pe)
                em.op("vector", "tensor_copy", ctxT[:, 128 * c:128 * c + 128],
                      pst[:, 0:128], waits=(s_pe,), inc=s_ev)
            cT_ev = s_ev.n
            for j in range(2):
                for c in range(8):
                    em.op("tensor", "matmul", pw[j][:, :],
                          ctxT[:, 128 * c:128 * c + 128],
                          wov[:, c, 512 * j:512 * j + 512],
                          start=(c == 0), stop=False,
                          waits=((s_ev, cT_ev), (s_upd, s_upd.n))
                                if c == 0 else ())
                em.op("tensor", "matmul", pw[j][:, :],
                      ones_b[0:1, :], borow[0:1, 512 * j:512 * j + 512],
                      start=False, stop=True, inc=s_mm)
            mmv = s_mm.n
            for j in range(2):
                em.op("vector", "tensor_add", xt[i][:, 512 * j:512 * j + 512],
                      xt[i][:, 512 * j:512 * j + 512], pw[j][:, :],
                      waits=((s_mm, mmv), (ldx[i], ldxv)),
                      inc=(s_upd if j == 1 else None))
            em.dma("sync", xov_t[t], xt[i][:, :], s_stp[i],
                   waits=((s_upd, s_upd.n),))
        em.wait("gpsimd", s_stp[0], s_stp[0].n)
        em.wait("gpsimd", s_stp[1], s_stp[1].n)

        if _own:
            with nc.Block() as block:
                em.finalize(block)
    return nc


POS_NEG = -1000000000.0


def _l1_pos_bias(d):
    return (np.exp(-0.5 * (d / 2.0) ** 2)
            + np.cos(2.0 * np.pi * d / 401.0)).astype(np.float32)


def _l1_masks(core):
    t0 = (core % 4) * TC
    masks = np.full((NT, 128, 385), POS_NEG, dtype=np.float32)
    d = np.arange(-2, 3)
    pb = _l1_pos_bias(d.astype(np.float64))
    for t in range(NT):
        for qi in range(128):
            gq = t0 + 128 * t + qi
            for w in range(5):
                gk = gq + d[w]
                if 0 <= gk < S:
                    rel = gk - (t0 + 128 * t - 128)
                    if 0 <= rel < 384:
                        masks[t, qi, rel] = pb[w]
            if gq > 2:
                masks[t, qi, 384] = _l1_pos_bias(np.float64(gq))
    return masks


_L1_MASKS = None


def step_l1attn(x_parts, ins):
    maps = _attn_proj_maps(x_parts, ins["L1_attn_wq"], ins["L1_attn_bq"],
                           ins["L1_attn_wk"], ins["L1_attn_bk"],
                           ins["L1_attn_wv"], ins["L1_attn_bv"])
    res = _launch("l2attn_proj", build_l2attn_proj, maps)
    qts = [np.asarray(res[c]["qt"]) for c in range(NCORES)]
    kts = [np.asarray(res[c]["kt"]) for c in range(NCORES)]
    vs = [np.asarray(res[c]["v"]) for c in range(NCORES)]

    global _L1_MASKS
    if _L1_MASKS is None:
        _L1_MASKS = [_l1_masks(c) for c in range(NCORES)]
    zt_kt = np.zeros((D, 128), dtype=BF)
    zt_v = np.zeros((128, D), dtype=BF)
    bias0s = []
    for c in range(NCORES):
        t0 = (c % 4) * TC
        j = (t0 + np.arange(TC)).astype(np.float64)
        b0 = _l1_pos_bias(j).reshape(NT, 128).T
        bias0s.append(np.ascontiguousarray(b0))
    maps = []
    for c in range(NCORES):
        b = c // 4
        left_k = kts[c - 1][:, -128:] if c % 4 != 0 else zt_kt
        right_k = kts[c + 1][:, :128] if c % 4 != 3 else zt_kt
        ktxa = np.ascontiguousarray(
            np.concatenate([left_k, kts[c], right_k], axis=1))
        left_v = vs[c - 1][-128:] if c % 4 != 0 else zt_v
        right_v = vs[c + 1][:128] if c % 4 != 3 else zt_v
        vb = np.concatenate([left_v, vs[c], right_v], 0)
        vbx = np.ascontiguousarray(
            vb.reshape(18, 128, D).transpose(1, 0, 2).reshape(128, 18 * D))
        k0col = np.ascontiguousarray(kts[4 * b][:, 0].reshape(16, 64).T)
        q0col = np.ascontiguousarray(qts[4 * b][:, 0].reshape(16, 64).T)
        v0row = np.ascontiguousarray(vs[4 * b][0]).reshape(1, D)
        maps.append({"qt": qts[c], "ktx": ktxa, "k0col": k0col,
                     "q0col": q0col, "vbx": vbx, "v0row": v0row,
                     "masks": _L1_MASKS[c], "bias0": bias0s[c],
                     "ident": IDENT_BF})
    res = _launch("l1band", build_l1band, maps)
    ctxus = [np.asarray(res[c]["ctxu"]) for c in range(NCORES)]
    denss = [np.asarray(res[c]["dens"]) for c in range(NCORES)]
    r0ps = [np.asarray(res[c]["r0p"]) for c in range(NCORES)]
    vsums = [np.asarray(res[c]["vsum"]) for c in range(NCORES)]

    bo1 = np.zeros((16, D), np.float32)
    for h in range(16):
        bo1[h, 64 * h:64 * h + 64] = 1
    woT = np.ascontiguousarray(ins["L1_attn_wo"].T)
    maps = []
    for c in range(NCORES):
        b = c // 4
        r0stack = np.concatenate([r0ps[4 * b + k] for k in range(4)],
                                 axis=1)
        vsstack = np.concatenate([vsums[4 * b + k] for k in range(4)],
                                 axis=1)
        rf = np.array([[1.0, 0.0]] if c % 4 == 0 else [[0.0, 1.0]],
                      np.float32)
        maps.append({"x": x_parts[c], "ctxu": ctxus[c], "dens": denss[c],
                     "r0ps": r0stack, "vsums": vsstack, "blockones": bo1,
                     "r0flag": rf, "woT": woT,
                     "bo": ins["L1_attn_bo"].reshape(1, D),
                     "ident": IDENT_BF})
    res = _launch("l1fin", build_l1fin, maps)
    return [np.asarray(res[c]["xo"]) for c in range(NCORES)]


def _shard_tokens(x):
    xf = np.ascontiguousarray(np.asarray(x).reshape(T, D))
    return [xf[c * TC:(c + 1) * TC] for c in range(NCORES)]


def _gather_tokens(parts):
    return np.concatenate(parts, 0).reshape(B, S, D)


def step_ffn0(x_parts, mh):
    res = _launch("ffn0", build_ffn0,
                  [{"x": x_parts[c], "mh": mh, "ident": IDENT_BF}
                   for c in range(NCORES)])
    return [np.asarray(res[c]["xo"]) for c in range(NCORES)]


def make_mh(w1, w2):
    w1 = np.asarray(w1)
    w2 = np.asarray(w2)
    res = _launch("mbuild", build_mbuild,
                  [{"w1": np.ascontiguousarray(w1[c]),
                    "w2": np.ascontiguousarray(w2[c]),
                    "ident": IDENT_BF} for c in range(NCORES)])
    mall = np.stack([np.asarray(res[c]["m"]) for c in range(NCORES)], 0)
    res2 = _launch("msum", build_msum, [{"mall": mall}] * NCORES)
    return np.asarray(res2[0]["mh"])


G4 = [[0, 1, 2, 3], [4, 5, 6, 7]]
G8 = [list(range(8))]
GP_EVEN = [[0, 1], [2, 3], [4, 5], [6, 7]]
GP_ODD = [[1, 2], [3, 4], [5, 6], [0, 7]]


class _Done(Exception):
    pass


_PHASE_CAP = 999


def build_fused():
    fx = _FX()
    nc, em = fx.nc, fx.em
    _p = [0]
    _ccn = [0]
    cap = _PHASE_CAP

    def phase(builder, alias, **kw):
        if _p[0] + 1 > cap:
            raise _Done
        _p[0] += 1
        fx.prefix = f"p{_p[0]}_"
        fx.alias = alias
        builder(fx=fx, **kw)
        em.barrier()

    def cc(kind, op, groups, src_ap, dst_ap):
        if _p[0] >= cap:
            raise _Done
        _ccn[0] += 1
        s = em.sem(f"cc{_ccn[0]}")
        em.op("gpsimd", "collective_compute", kind, op,
              replica_groups=groups, ins=[src_ap], outs=[dst_ap],
              inc=s, incv=1)
        em.wait("gpsimd", s, 1)

    def it(name, shape, dtype):
        if name not in fx.io:
            fx.io[name] = nc.dram_tensor(name, shape, dtype).ap()
        return fx.io[name]

    fx.io["xA"] = nc.dram_tensor("xA", [TC, D], F32,
                                 kind="ExternalInput").ap()
    xB = it("xB", [TC, D], F32)
    out_d = nc.dram_tensor("out", [TC, D], F32, kind="ExternalOutput").ap()
    fx.io["out"] = out_d

    try:
        _run_fused_body(fx, nc, em, _p, cap, phase, cc, it, xB)
    except _Done:
        pass
    return fx.close()


def _run_fused_body(fx, nc, em, _p, cap, phase, cc, it, xB):
    m_part = it("m_part", [D, D], F32)
    m_sum = it("m_sum", [D, D], F32)
    mh_d = it("mh_d", [D, D], BF16)
    phase(build_mbuild, {"w1": "mb_w1", "w2": "mb_w2", "m": "m_part",
                         "ident": "identb"})
    cc("AllReduce", ALU.add, G8, m_part, m_sum)
    em.barrier()
    _p[0] += 1
    fx.prefix = f"p{_p[0]}_"
    with ExitStack() as stack:
        msb = stack.enter_context(nc.sbuf_tensor(fx.prefix + "msb",
                                                 [128, 8 * D], F32))
        mbf = stack.enter_context(nc.sbuf_tensor(fx.prefix + "mbf",
                                                 [128, 8 * D], BF16))
        s_ld = em.sem("ld")
        s_dve = em.sem("dve")
        s_st = em.sem("st")
        em.dma("sync", msb[:, :].rearrange("p (c n) -> p c n", c=8),
               m_sum.rearrange("(c p) n -> p c n", p=128), s_ld)
        em.op("vector", "tensor_scalar_mul", mbf[:, :], msb[:, :], 0.5,
              waits=((s_ld, s_ld.n),), inc=s_dve)
        em.dma("sync", mh_d.rearrange("(c p) n -> p c n", p=128),
               mbf[:, :].rearrange("p (c n) -> p c n", c=8), s_st,
               waits=((s_dve, s_dve.n),))
    em.barrier()

    def ffn0(xin, xout):
        phase(build_ffn0, {"x": xin, "mh": "mh_d", "xo": xout,
                           "ident": "identb"})

    ffn0("xA", "xB")
    it("l0_qt", [D, TC], BF16)
    it("l0_kt", [D, TC], BF16)
    it("l0_v", [TC, 512], BF16)
    it("l0_qvs", [128, NT * 16], F32)
    it("l0_ktag", [4 * D, TC], BF16)
    it("l0_vag", [S, 512], BF16)
    it("l0_om", [TC, D], BF16)
    phase(build_l0attn_proj,
          {"x": "xB", "wqT": "l0p_wqT", "wkT": "l0p_wkT", "wvT": "l0p_wvT",
           "wqvT": "l0p_wqvT", "bqc": "l0p_bqc", "bkc": "l0p_bkc",
           "bv": "l0p_bv", "bqv": "l0p_bqv", "ws1t": "l0p_ws1t",
           "bs": "l0p_bs", "g1": "l0p_g1", "b1": "l0p_b1",
           "ident": "identb", "qt": "l0_qt", "kt": "l0_kt", "v": "l0_v",
           "qvs": "l0_qvs"})
    cc("AllGather", ALU.bypass, G4, fx.io["l0_kt"], fx.io["l0_ktag"])
    cc("AllGather", ALU.bypass, G4, fx.io["l0_v"], fx.io["l0_vag"])
    em.barrier()
    phase(build_l0attn_flash,
          {"qt": "l0_qt", "kt_ag": "l0_ktag", "v_ag": "l0_vag",
           "qvs": "l0_qvs", "ws2t": "l0p_ws2t", "om": "l0_om"})
    phase(build_l0attn_post,
          {"x": "xB", "om": "l0_om", "wfT": "l0q_wfT", "bf": "l0q_bf",
           "g2": "l0q_g2", "b2": "l0q_b2", "wr1T": "l0q_wr1T",
           "br1c": "l0q_br1c", "wr2T": "l0q_wr2T", "br2": "l0q_br2",
           "ident": "identb", "xo": "xA"})
    ffn0("xA", "xB")

    def l1ffn(xin, xout, tag):
        it(f"{tag}_num", [256, D], F32)
        it(f"{tag}_den", [128, 2], F32)
        it(f"{tag}_nums", [1, 256, D], F32)
        it(f"{tag}_dens", [1, 128, 2], F32)
        it(f"{tag}_w", [TC, 256], BF16)
        phase(build_l1ffn_part,
              {"x": xin, "mu": "l1f_mu", "scale": "l1f_scale",
               "gamma": "l1f_gamma", "beta": "l1f_beta", "ident": "identb",
               "num": f"{tag}_num", "den": f"{tag}_den", "w": f"{tag}_w"})
        cc("AllReduce", ALU.add, G4, fx.io[f"{tag}_num"],
           fx.io[f"{tag}_nums"].rearrange("o e d -> (o e) d"))
        cc("AllReduce", ALU.add, G4, fx.io[f"{tag}_den"],
           fx.io[f"{tag}_dens"].rearrange("o e d -> (o e) d"))
        em.barrier()
        phase(build_l1ffn_comb,
              {"x": xin, "nums": f"{tag}_nums", "dens": f"{tag}_dens",
               "w": f"{tag}_w", "w1": "l1f_w1l", "b1c": "l1f_b1c",
               "w2": "l1f_w2l", "b2": "l1f_b2", "sel": "l1f_sel",
               "ident": "identb", "xo": xout}, nsum=1)

    l1ffn("xB", "xA", "p1a")

    def attn_proj(xin, tag, wpfx, hm=False):
        it(f"{tag}_qt", [D, TC], BF16)
        it(f"{tag}_kt", [D, TC], BF16)
        it(f"{tag}_v", [128, NT * D] if hm else [TC, D], BF16)
        phase((lambda fx=None: build_l2attn_proj(fx=fx, v_hm=hm))
              if hm else build_l2attn_proj,
              {"x": xin, "wqT": f"{wpfx}_wqT", "wkT": f"{wpfx}_wkT",
               "wvT": f"{wpfx}_wvT", "bqc": f"{wpfx}_bqc",
               "bkc": f"{wpfx}_bkc", "bv": f"{wpfx}_bv", "ident": "identb",
               "qt": f"{tag}_qt", "kt": f"{tag}_kt", "v": f"{tag}_v"})

    def band_assembly(tag, with_k0):
        kt_i = fx.io[f"{tag}_kt"]
        v_i = fx.io[f"{tag}_v"]
        ktx = it(f"{tag}_ktx", [D, TC + 256], BF16)
        vbx = it(f"{tag}_vbx", [128, 18 * D], BF16)
        ktp = it(f"{tag}_ktp", [D, 256], BF16)
        vp = it(f"{tag}_vp", [256, D], BF16)
        ktp_ag = it(f"{tag}_ktpag", [4 * D, 256], BF16)
        vp_ag = it(f"{tag}_vpag", [1024, D], BF16)
        s_cp = em.sem(tag + "_pk")
        em.dma("sync", ktp[:, 0:128], kt_i[:, 0:128], s_cp)
        em.dma("sync", ktp[:, 128:256], kt_i[:, TC - 128:TC], s_cp)
        em.dma("sync", vp[0:128, :], v_i[0:128, :], s_cp)
        em.dma("sync", vp[128:256, :], v_i[TC - 128:TC, :], s_cp)
        em.dma("sync", ktx[:, 128:128 + TC], kt_i, s_cp)
        em.dma("sync",
               vbx[:, :].rearrange("p (t n) -> p t n", t=18)[:, 1:17, :],
               v_i.rearrange("(t p) n -> p t n", p=128), s_cp)
        if with_k0:
            pk4 = it(f"{tag}_pk4", [64, 48], BF16)
            pk4_ag = it(f"{tag}_pk4ag", [256, 48], BF16)
            em.op("sync", "dma_start", out=pk4[:, 0:16],
                  in_=kt_i[:, 0:1].rearrange("(c p) o -> p (c o)", p=64),
                  allow_slow_non_contiguous=True, inc=s_cp, incv=16)
            em.op("sync", "dma_start", out=pk4[:, 16:32],
                  in_=fx.io[f"{tag}_qt"][:, 0:1].rearrange(
                      "(c p) o -> p (c o)", p=64),
                  allow_slow_non_contiguous=True, inc=s_cp, incv=16)
            em.dma("sync", pk4[:, 32:48],
                   v_i[0:1, :].rearrange("o (p c) -> (o p) c", p=64), s_cp)
        em.wait("gpsimd", s_cp, s_cp.n)
        cc("AllGather", ALU.bypass, G4, ktp, ktp_ag)
        cc("AllGather", ALU.bypass, G4, vp, vp_ag)
        if with_k0:
            cc("AllGather", ALU.bypass, G4, pk4, pk4_ag)
        em.barrier()
        _p[0] += 1
        fx.prefix = f"p{_p[0]}_"
        with ExitStack() as stack:
            sbn = lambda nm, sh, dt: stack.enter_context(
                nc.sbuf_tensor(fx.prefix + nm, sh, dt))
            cnd = sbn("cnd", [128, 8 * 128], BF16)
            acc = sbn("acc", [128, 8 * 128], BF16)
            vcnd = sbn("vcnd", [128, D], BF16)
            vacc = sbn("vacc", [128, D], BF16)
            fs = sbn("fs", [128, 8], F32)
            s_ld = em.sem("ld")
            s_dve = em.sem("dve")
            s_cp2 = em.sem(tag + "_bl")
            fsel = fx.get("fsel4", [128, 8], F32, "ExternalInput")
            em.dma("sync", fs[:, :], fsel, s_ld)
            for side in range(2):
                kcol = 128 if side == 0 else 0
                vrow = 128 if side == 0 else 0
                kdst = (ktx[:, 0:128] if side == 0
                        else ktx[:, 128 + TC:256 + TC])
                vdst = vbx[:, :].rearrange("p (t n) -> p t n", t=18)[
                    :, 0 if side == 0 else 17, :]
                for r in range(4):
                    em.dma("sync",
                           cnd[:, :].rearrange("p (c n) -> p c n", c=8),
                           ktp_ag[1024 * r:1024 * r + 1024,
                                  kcol:kcol + 128].rearrange(
                               "(c p) n -> p c n", p=128),
                           s_ld, waits=((s_dve, s_dve.n),))
                    em.dma("sync", vcnd[:, :],
                           vp_ag[256 * r + vrow:256 * r + vrow + 128, :],
                           s_ld)
                    col = fs[:, 4 * side + r:4 * side + r + 1]
                    if r == 0:
                        em.op("vector", "tensor_scalar_mul", acc[:, :],
                              cnd[:, :], col,
                              waits=((s_ld, s_ld.n), (s_cp2, s_cp2.n)))
                        em.op("vector", "tensor_scalar_mul", vacc[:, :],
                              vcnd[:, :], col, inc=s_dve)
                    else:
                        em.op("vector", "scalar_tensor_tensor", acc[:, :],
                              cnd[:, :], col, acc[:, :], ALU.mult, ALU.add,
                              waits=((s_ld, s_ld.n),))
                        em.op("vector", "scalar_tensor_tensor", vacc[:, :],
                              vcnd[:, :], col, vacc[:, :], ALU.mult,
                              ALU.add, inc=s_dve)
                em.dma("sync", kdst.rearrange("(c p) n -> p c n", p=128),
                       acc[:, :].rearrange("p (c n) -> p c n", c=8), s_cp2,
                       waits=((s_dve, s_dve.n),))
                em.dma("sync", vdst, vacc[:, :], s_cp2)
            if with_k0:
                k0c = it(f"{tag}_k0", [64, 16], BF16)
                q0c = it(f"{tag}_q0", [64, 16], BF16)
                v0r = it(f"{tag}_v0", [1, D], BF16)
                em.dma("sync", k0c, pk4_ag[0:64, 0:16], s_cp2)
                em.dma("sync", q0c, pk4_ag[0:64, 16:32], s_cp2)
                em.dma("sync",
                       v0r.rearrange("o (p c) -> (o p) c", p=64),
                       pk4_ag[0:64, 32:48], s_cp2)
            em.wait("gpsimd", s_cp2, s_cp2.n)
        em.barrier()

    attn_proj("xA", "l1", "l1a")
    band_assembly("l1", with_k0=True)
    it("l1_ctxu", [TC, D], BF16)
    it("l1_dens", [128, NT * 16], F32)
    it("l1_r0p", [1, 1040], F32)
    it("l1_vsum", [1, D], F32)
    it("l1_r0ps", [1, 1040], F32)
    it("l1_vsums", [1, D], F32)
    phase(build_l1band,
          {"qt": "l1_qt", "ktx": "l1_ktx", "k0col": "l1_k0",
           "q0col": "l1_q0", "vbx": "l1_vbx", "v0row": "l1_v0",
           "masks": "l1_masks", "bias0": "l1_bias0", "ident": "identb",
           "ctxu": "l1_ctxu", "dens": "l1_dens", "r0p": "l1_r0p",
           "vsum": "l1_vsum"})
    cc("AllReduce", ALU.add, G4, fx.io["l1_r0p"], fx.io["l1_r0ps"])
    cc("AllReduce", ALU.add, G4, fx.io["l1_vsum"], fx.io["l1_vsums"])
    em.barrier()
    phase(build_l1fin,
          {"x": "xA", "ctxu": "l1_ctxu", "dens": "l1_dens",
           "r0ps": "l1_r0ps", "vsums": "l1_vsums",
           "blockones": "l1_blockones", "r0flag": "l1_r0flag",
           "woT": "l1a_woT", "bo": "l1a_bo", "ident": "identb",
           "xo": "xB"}, nsum=1)
    l1ffn("xB", "xA", "p1b")

    def l2ffn(xin, xout):
        phase(build_l2ffn,
              {"x": xin, "gamma": "l2f_gamma", "beta": "l2f_beta",
               "q1": "l2f_q1", "s1": "l2f_s1", "b1": "l2f_b1",
               "q2": "l2f_q2", "s2": "l2f_s2", "b2": "l2f_b2",
               "ident": "identb", "xo": xout})

    l2ffn("xA", "xB")
    attn_proj("xB", "l2", "l2a", hm=True)
    it("l2_ktag", [4 * D, TC], BF16)
    it("l2_vag", [512, NT * D], BF16)
    cc("AllGather", ALU.bypass, G4, fx.io["l2_kt"], fx.io["l2_ktag"])
    cc("AllGather", ALU.bypass, G4, fx.io["l2_v"], fx.io["l2_vag"])
    em.barrier()
    phase(build_l2attn_flash,
          {"x": "xB", "qt": "l2_qt", "kt_ag": "l2_ktag", "v_ag": "l2_vag",
           "woT": "l2a_woT", "bo": "l2a_bo", "xo": "xA"})
    l2ffn("xA", "xB")

    def l3ffn(xin, xout):
        phase(build_l3ffn,
              {"x": xin, "rw": "l3f_rw", "rb": "l3f_rb", "we": "l3f_we",
               "be": "l3f_be", "cols8m": "l3f_cols8m",
               "identf": "identf", "xo": xout})

    l3ffn("xB", "xA")
    attn_proj("xA", "l3", "l3a")
    band_assembly("l3", with_k0=False)
    phase(build_l3band,
          {"x": "xA", "qt": "l3_qt", "ktx": "l3_ktx", "vbx": "l3_vbx",
           "masks": "l3_masks", "woT": "l3a_woT", "ident": "identb",
           "xo": "xB"})
    l3ffn("xB", "out")


def kernel_fused(**ins):
    x_parts = _shard_tokens(ins["L0_attn_x"])
    common = {
        "identb": IDENT_BF,
        "identf": np.eye(128, dtype=np.float32),
        "l0p_wqT": np.ascontiguousarray(ins["L0_attn_Wq"].T),
        "l0p_wkT": np.ascontiguousarray(ins["L0_attn_Wk"].T),
        "l0p_wvT": np.ascontiguousarray(ins["L0_attn_Wv"].T),
        "l0p_wqvT": np.ascontiguousarray(ins["L0_attn_Wqv"].T),
        "l0p_bqc": np.ascontiguousarray(ins["L0_attn_bq"].reshape(8, 128).T),
        "l0p_bkc": np.ascontiguousarray(ins["L0_attn_bk"].reshape(8, 128).T),
        "l0p_bv": ins["L0_attn_bv"].reshape(1, 512),
        "l0p_bqv": ins["L0_attn_bqv"].reshape(1, 512),
        "l0p_ws1t": np.ascontiguousarray(
            np.tile(ins["L0_attn_Ws"][0, :32], 16)).reshape(1, 512),
        "l0p_bs": ins["L0_attn_bs"].reshape(1, 1),
        "l0p_g1": ins["L0_attn_g1"].reshape(1, D),
        "l0p_b1": ins["L0_attn_b1"].reshape(1, D),
        "l0p_ws2t": np.ascontiguousarray(
            np.tile(ins["L0_attn_Ws"][0, 32:96], 8)).reshape(1, 512),
        "l0q_wfT": np.ascontiguousarray(ins["L0_attn_Wf"].T),
        "l0q_bf": ins["L0_attn_bf"].reshape(1, D),
        "l0q_g2": ins["L0_attn_g2"].reshape(1, D),
        "l0q_b2": ins["L0_attn_b2"].reshape(1, D),
        "l0q_wr1T": np.ascontiguousarray(ins["L0_attn_Wr1"].T),
        "l0q_br1c": np.ascontiguousarray(
            ins["L0_attn_br1"].reshape(16, 128).T),
        "l0q_wr2T": np.ascontiguousarray(ins["L0_attn_Wr2"].T),
        "l0q_br2": ins["L0_attn_br2"].reshape(1, D),
        "l1f_mu": np.ascontiguousarray(ins["L1_ffn_mu"].reshape(D, 256)),
        "l1f_scale": ins["L1_ffn_scale"].reshape(1, 1),
        "l1f_gamma": ins["L1_ffn_ln_gamma"].reshape(1, D),
        "l1f_beta": ins["L1_ffn_ln_beta"].reshape(1, D),
        "l1f_b1c": np.ascontiguousarray(
            ins["L1_ffn_b1"].reshape(16, 2, 128).transpose(2, 0, 1)
            .reshape(128, 32)),
        "l1f_w1l": np.ascontiguousarray(
            ins["L1_ffn_w1"].reshape(16, 8, 128, 256).transpose(2, 1, 0, 3)
            .reshape(128, 8 * 16 * 256)),
        "l1f_w2l": np.ascontiguousarray(
            ins["L1_ffn_w2"].reshape(16, 2, 128, 1024).transpose(2, 1, 0, 3)
            .reshape(128, 2 * 16 * 1024)),
        "l1f_b2": ins["L1_ffn_b2"],
        "l1f_sel": None,
        "l1a_wqT": np.ascontiguousarray(ins["L1_attn_wq"].T),
        "l1a_wkT": np.ascontiguousarray(ins["L1_attn_wk"].T),
        "l1a_wvT": np.ascontiguousarray(ins["L1_attn_wv"].T),
        "l1a_bqc": np.ascontiguousarray(ins["L1_attn_bq"].reshape(8, 128).T),
        "l1a_bkc": np.ascontiguousarray(ins["L1_attn_bk"].reshape(8, 128).T),
        "l1a_bv": ins["L1_attn_bv"].reshape(1, D),
        "l1a_woT": np.ascontiguousarray(ins["L1_attn_wo"].T),
        "l1a_bo": ins["L1_attn_bo"].reshape(1, D),
        "l1_blockones": None,
        "l2f_gamma": ins["L2_ffn_ln_gamma"].reshape(1, D),
        "l2f_beta": ins["L2_ffn_ln_beta"].reshape(1, D),
        "l2f_q1": ins["L2_ffn_fc1_q"],
        "l2f_s1": ins["L2_ffn_fc1_scales"],
        "l2f_b1": np.ascontiguousarray(
            ins["L2_ffn_fc1_bias"].reshape(32, 128).T),
        "l2f_q2": ins["L2_ffn_fc2_q"],
        "l2f_s2": ins["L2_ffn_fc2_scales"],
        "l2f_b2": ins["L2_ffn_fc2_bias"].reshape(1, D),
        "l2a_wqT": np.ascontiguousarray(ins["L2_attn_wq"].T),
        "l2a_wkT": np.ascontiguousarray(ins["L2_attn_wk"].T),
        "l2a_wvT": np.ascontiguousarray(ins["L2_attn_wv"].T),
        "l2a_bqc": np.ascontiguousarray(ins["L2_attn_bq"].reshape(8, 128).T),
        "l2a_bkc": np.ascontiguousarray(ins["L2_attn_bk"].reshape(8, 128).T),
        "l2a_bv": ins["L2_attn_bv"].reshape(1, D),
        "l2a_woT": np.ascontiguousarray(ins["L2_attn_wo"].T),
        "l2a_bo": ins["L2_attn_bo"].reshape(1, D),
        "l3f_rw": ins["L3_ffn_router_w"],
        "l3f_rb": ins["L3_ffn_router_b"].reshape(1, 8),
        "l3f_we": ins["L3_ffn_expert_w"],
        "l3f_be": ins["L3_ffn_expert_b"].reshape(1, 8 * D),
        "l3f_cols8m": np.ascontiguousarray(np.broadcast_to(
            (np.arange(8) - 100.0).astype(np.float32), (128, 8))),
        "l3a_wqT": np.ascontiguousarray(ins["L3_attn_Wq"].T),
        "l3a_wkT": np.ascontiguousarray(ins["L3_attn_Wk"].T),
        "l3a_wvT": np.ascontiguousarray(ins["L3_attn_Wv"].T),
        "l3a_bqc": np.zeros((128, 8), np.float32),
        "l3a_bkc": np.zeros((128, 8), np.float32),
        "l3a_bv": np.zeros((1, D), np.float32),
        "l3a_woT": np.ascontiguousarray(ins["L3_attn_Wo"].T),
    }
    bo1 = np.zeros((16, D), np.float32)
    for h in range(16):
        bo1[h, 64 * h:64 * h + 64] = 1
    common["l1_blockones"] = bo1
    sel = np.zeros((16, 256), dtype=BF)
    for e in range(16):
        sel[e, 16 * e:16 * e + 16] = 1
    common["l1f_sel"] = sel
    global _L1_MASKS, _L3_MASKS
    if _L1_MASKS is None:
        _L1_MASKS = [_l1_masks(c) for c in range(NCORES)]
    if _L3_MASKS is None:
        _L3_MASKS = [_l3_masks(c) for c in range(NCORES)]
    bias0s = []
    for c in range(NCORES):
        t0 = (c % 4) * TC
        j = (t0 + np.arange(TC)).astype(np.float64)
        bias0s.append(np.ascontiguousarray(
            _l1_pos_bias(j).reshape(NT, 128).T))

    w1 = np.asarray(ins["L0_ffn_w1"])
    w2 = np.asarray(ins["L0_ffn_w2"])
    maps = []
    for c in range(NCORES):
        m = dict(common)
        m["xA"] = x_parts[c]
        m["mb_w1"] = np.ascontiguousarray(w1[c])
        m["mb_w2"] = np.ascontiguousarray(w2[c])
        r = c % 4
        fs = np.zeros(8, np.float32)
        if r > 0:
            fs[r - 1] = 1.0
        if r < 3:
            fs[4 + r + 1] = 1.0
        m["fsel4"] = np.ascontiguousarray(
            np.broadcast_to(fs, (128, 8)).copy())
        m["l1_masks"] = _L1_MASKS[c]
        m["l1_bias0"] = bias0s[c]
        m["l1_r0flag"] = np.array(
            [[1.0, 0.0]] if c % 4 == 0 else [[0.0, 1.0]], np.float32)
        m["l3_masks"] = _L3_MASKS[c]
        maps.append(m)
    res = _launch("fused", build_fused, maps)
    return _gather_tokens(
        [np.asarray(res[c]["out"]) for c in range(NCORES)]).astype(np.float32)


def kernel(**inputs):
    return kernel_fused(**inputs)


def kernel_unfused(**inputs):
    ins = inputs
    x_parts = _shard_tokens(ins["L0_attn_x"])
    mh = make_mh(ins["L0_ffn_w1"], ins["L0_ffn_w2"])
    x_parts = step_ffn0(x_parts, mh)
    x_parts = step_l0attn(x_parts, ins)
    x_parts = step_ffn0(x_parts, mh)
    x_parts = step_l1ffn(x_parts, ins)
    x_parts = step_l1attn(x_parts, ins)
    x_parts = step_l1ffn(x_parts, ins)
    x_parts = step_l2ffn(x_parts, ins)
    x_parts = step_l2attn(x_parts, ins)
    x_parts = step_l2ffn(x_parts, ins)
    x_parts = step_l3ffn(x_parts, ins)
    x_parts = step_l3attn(x_parts, ins)
    x_parts = step_l3ffn(x_parts, ins)
    return _gather_tokens(x_parts).astype(np.float32)


# version 09
# speedup vs baseline: 1.2133x; 1.2133x over previous
import numpy as np
import ml_dtypes
import concourse.bass as bass
import concourse.mybir as mybir
from concourse.bass_utils import run_bass_kernel_spmd
from contextlib import ExitStack

F32 = mybir.dt.float32
BF16 = mybir.dt.bfloat16
I32 = mybir.dt.int32
AF = mybir.ActivationFunctionType
ALU = mybir.AluOpType
AX = mybir.AxisListType

BF = ml_dtypes.bfloat16

NCORES = 8
B, S, D = 2, 8192, 1024
T = B * S
TC = T // NCORES
NT = TC // 128
EPS = 1e-5

ENGS = ("sync", "scalar", "vector", "tensor", "gpsimd")
AUTODRAIN = True


class _Sem:
    __slots__ = ("h", "n", "name")

    def __init__(self, h, name):
        self.h = h
        self.n = 0
        self.name = name


class Em:

    def __init__(self, nc, stack):
        self.nc = nc
        self.stack = stack
        self.q = {e: [] for e in ENGS}
        self._semi = 0

    def sem(self, name=None):
        self._semi += 1
        name = name or f"s{self._semi}"
        if not hasattr(self, "_sems"):
            self._sems = {}
        if name in self._sems:
            return self._sems[name]
        h = self.stack.enter_context(self.nc.semaphore(name))
        s = _Sem(h, name)
        self._sems[name] = s
        return s

    def barrier(self):
        for e in ENGS:
            for s in self._sems.values():
                self.wait(e, s, s.n)

    def op(self, eng, method, *args, waits=(), inc=None, incv=1, **kw):
        wl = []
        for w in waits:
            if isinstance(w, tuple):
                wl.append((w[0], w[1]))
            else:
                wl.append((w, w.n))
        if inc is not None:
            inc.n += incv

        drain = (AUTODRAIN and eng in ("vector", "scalar", "gpsimd")
                 and method not in ("dma_start", "collective_compute"))

        def fn(e, method=method, args=args, kw=kw, wl=tuple(wl), inc=inc,
               incv=incv, drain=drain):
            for s, v in wl:
                if v > 0:
                    e.wait_ge(s.h, v)
            r = getattr(e, method)(*args, **kw)
            if inc is not None:
                if method == "collective_compute":
                    r.then_inc(inc.h)
                else:
                    r.then_inc(inc.h, incv)
            if drain:
                e.drain()

        self.q[eng].append(fn)
        return inc.n if inc is not None else None

    def wait(self, eng, sem, val=None):
        v = sem.n if val is None else val
        if v > 0:
            self.q[eng].append(lambda e, s=sem, v=v: e.wait_ge(s.h, v))

    def raw(self, eng, fn):
        self.q[eng].append(fn)

    def dma(self, eng, out, in_, sem, waits=()):
        return self.op(eng, "dma_start", waits=waits, inc=sem, incv=16,
                       out=out, in_=in_)

    def finalize(self, block):
        for e in ENGS:
            if self.q[e]:
                getattr(block, e)(
                    lambda eng, ops=tuple(self.q[e]): [f(eng) for f in ops])


def emit_ln_tile(em, x_ap, zbf_ap, scratch, cols, s_act, s_dve, eps_col,
                 gain_ap=None, bias_ap=None, zf32_ap=None, out_waits=(),
                 in_waits=()):
    sum_c = cols[:, 0:1]
    ssq_c = cols[:, 1:2]
    mean_c = cols[:, 2:3]
    var_c = cols[:, 3:4]
    rstd_c = cols[:, 4:5]
    em.op("scalar", "activation", scratch, x_ap, AF.Identity,
          accum_out=sum_c, waits=tuple(in_waits))
    em.op("scalar", "activation", scratch, x_ap, AF.Square,
          accum_out=ssq_c, inc=s_act)
    em.op("vector", "tensor_scalar_mul", mean_c, sum_c, 1.0 / D,
          waits=(s_act,))
    em.op("vector", "tensor_scalar", var_c, ssq_c, 1.0 / D, None, ALU.mult)
    em.op("vector", "scalar_tensor_tensor", var_c, mean_c, mean_c, var_c,
          ALU.mult, ALU.subtract, inc=s_dve)
    em.op("scalar", "activation", var_c, var_c, AF.Ln, scale=-1.0,
          bias=eps_col, waits=(s_dve,))
    em.op("scalar", "activation", rstd_c, var_c, AF.Exp, scale=-0.5,
          inc=s_act)
    if gain_ap is None:
        first = zbf_ap if zbf_ap is not None else zf32_ap
        em.op("vector", "tensor_scalar", first, x_ap, mean_c, rstd_c,
              ALU.subtract, ALU.mult, waits=(s_act,) + tuple(out_waits),
              inc=s_dve)
        if zbf_ap is not None and zf32_ap is not None:
            em.op("vector", "tensor_scalar", zf32_ap, x_ap, mean_c, rstd_c,
                  ALU.subtract, ALU.mult, inc=s_dve)
    else:
        assert zf32_ap is not None
        em.op("vector", "tensor_scalar", zf32_ap, x_ap, mean_c, rstd_c,
              ALU.subtract, ALU.mult, waits=(s_act,) + tuple(out_waits))
        em.op("vector", "tensor_mul", zf32_ap, zf32_ap, gain_ap)
        if bias_ap is not None:
            em.op("vector", "tensor_add", zf32_ap, zf32_ap, bias_ap)
        em.op("vector", "tensor_copy", zbf_ap, zf32_ap, inc=s_dve)


def emit_transpose8(em, zbf_view, zT_view, pst, ident, s_in, s_pe, s_ev,
                    nchunk=8, evac_eng="vector"):
    for c in range(nchunk):
        em.op("tensor", "transpose", pst[:, 0:128],
              zbf_view[:, c * 128:(c + 1) * 128],
              ident, waits=((s_in, s_in.n), (s_ev, s_ev.n)), inc=s_pe)
        em.op(evac_eng, "tensor_copy" if evac_eng == "vector" else "copy",
              zT_view[:, c * 128:(c + 1) * 128], pst[:, 0:128],
              waits=(s_pe,), inc=s_ev)


_PROGS = {}


def _get_prog(name, builder):
    if name not in _PROGS:
        _PROGS[name] = builder()
    return _PROGS[name]


def _launch(name, builder, in_maps):
    nc = _get_prog(name, builder)
    res = run_bass_kernel_spmd(nc, in_maps, core_ids=list(range(NCORES)))
    return res.results


IDENT_BF = np.eye(128, dtype=BF)



class _FX:

    def __init__(self):
        self.nc = bass.Bass("TRN2", target_bir_lowering=False)
        self._gs = ExitStack()
        self._gs.__enter__()
        self.em = Em(self.nc, self._gs)
        self.prefix = ""
        self.io = {}
        self.alias = {}

    def get(self, name, shape, dtype, kind):
        key = self.alias.get(name, name)
        if key not in self.io:
            if kind == "Internal":
                self.io[key] = self.nc.dram_tensor(key, shape, dtype).ap()
            else:
                self.io[key] = self.nc.dram_tensor(key, shape, dtype,
                                                   kind=kind).ap()
        return self.io[key]

    def close(self):
        with self.nc.Block() as block:
            self.em.finalize(block)
        self._gs.__exit__(None, None, None)
        return self.nc


def _fx_open(fx):
    if fx is None:
        return bass.Bass("TRN2", target_bir_lowering=False), None, "", True
    return fx.nc, fx.em, fx.prefix, False


def _fio(fx, nc, name, shape, dtype, kind):
    if fx is None:
        return nc.dram_tensor(name, shape, dtype, kind=kind).ap()
    return fx.get(name, shape, dtype, kind)


def build_mbuild(fx=None):
    nc, _em0, pre, _own = _fx_open(fx)
    w1_d = _fio(fx, nc, "w1", [D, 4 * D], F32, "ExternalInput")
    w2_d = _fio(fx, nc, "w2", [4 * D, D], F32, "ExternalInput")
    id_d = _fio(fx, nc, "ident", [128, 128], BF16, "ExternalInput")
    m_d = _fio(fx, nc, "m", [D, D], F32, "ExternalOutput")
    NK = 4 * D // 128

    with ExitStack() as stack:
        em = _em0 if _em0 is not None else Em(nc, stack)
        sb = lambda nm, sh, dt=F32: stack.enter_context(nc.sbuf_tensor(pre + nm, sh, dt))
        ps = lambda nm, sh, dt=F32: stack.enter_context(nc.psum_tensor(pre + nm, sh, dt))

        w1_sb = sb("w1_sb", [128, 8 * 4 * D], BF16)
        w1t_sb = sb("w1t_sb", [128, NK * 128], BF16)
        w2_sb = sb("w2_sb", [128, NK * D], BF16)
        id_sb = sb("id_sb", [128, 128], BF16)
        o_sb = sb("o_sb", [128, D])
        pst = ps("pst", [128, 1024], BF16)
        po = [ps(f"po{j}", [128, 512]) for j in range(2)]

        ld = em.sem("ld")
        ldg = em.sem("ldg")
        s_pe = em.sem("pe")
        s_ev = em.sem("ev")
        s_mm = em.sem("mm")
        s_st = em.sem("st")

        em.dma("gpsimd", w1_sb[:, :].rearrange("p (c n) -> p c n", c=8),
               w1_d.rearrange("(c p) n -> p c n", p=128), ldg)
        em.dma("gpsimd", w2_sb[:, :].rearrange("p (k n) -> p k n", k=NK),
               w2_d.rearrange("(k p) n -> p k n", p=128), ldg)
        em.dma("sync", id_sb[:, :], id_d, ld)
        ldv0, ldgv0 = ld.n, ldg.n

        for i in range(8):
            for k in range(NK):
                em.op("tensor", "transpose", pst[:, 0:128],
                      w1_sb[:, :].rearrange("p (c n) -> p c n", c=8)[:, i, 128 * k:128 * k + 128],
                      id_sb[:, :],
                      waits=((ld, ldv0), (ldg, ldgv0), (s_ev, s_ev.n)),
                      inc=s_pe)
                em.op("vector", "tensor_copy", w1t_sb[:, 128 * k:128 * k + 128],
                      pst[:, 0:128], waits=(s_pe,), inc=s_ev)
            for j in range(2):
                for k in range(NK):
                    em.op("tensor", "matmul", po[j][:, :],
                          w1t_sb[:, 128 * k:128 * k + 128],
                          w2_sb[:, :].rearrange("p (k n) -> p k n", k=NK)[:, k, 512 * j:512 * j + 512],
                          start=(k == 0), stop=(k == NK - 1),
                          waits=((s_ev, s_ev.n), (s_st, s_st.n)) if k == 0 else (),
                          inc=(s_mm if k == NK - 1 else None))
                em.op("vector", "tensor_copy", o_sb[:, 512 * j:512 * j + 512],
                      po[j][:, :], waits=((s_mm, s_mm.n), (s_st, s_st.n)), inc=s_ev)
            em.wait("sync", s_ev)
            em.dma("sync", m_d[128 * i:128 * i + 128, :], o_sb[:, :], s_st)
        em.wait("gpsimd", s_st, s_st.n)

        if _own:
            with nc.Block() as block:
                em.finalize(block)
    return nc


def build_msum(fx=None):
    nc, _em0, pre, _own = _fx_open(fx)
    mall_d = _fio(fx, nc, "mall", [NCORES, D, D], F32, "ExternalInput")
    mh_d = _fio(fx, nc, "mh", [D, D], BF16, "ExternalOutput")

    with ExitStack() as stack:
        em = _em0 if _em0 is not None else Em(nc, stack)
        acc = stack.enter_context(nc.sbuf_tensor("acc", [128, 8 * D], F32))
        cur = stack.enter_context(nc.sbuf_tensor("cur", [128, 8 * D], F32))
        obf = stack.enter_context(nc.sbuf_tensor("obf", [128, 8 * D], BF16))
        ld = em.sem("ld")
        s_dve = em.sem("dve")
        s_st = em.sem("st")

        accv = acc[:, :].rearrange("p (c n) -> p c n", c=8)
        curv = cur[:, :].rearrange("p (c n) -> p c n", c=8)
        em.dma("sync", accv, mall_d[0].rearrange("(c p) n -> p c n", p=128), ld)
        for i in range(1, NCORES):
            em.dma("sync", curv, mall_d[i].rearrange("(c p) n -> p c n", p=128),
                   ld, waits=((s_dve, s_dve.n),))
            em.op("vector", "tensor_add", acc[:, :], acc[:, :], cur[:, :],
                  waits=((ld, ld.n),), inc=s_dve)
        em.op("vector", "tensor_scalar_mul", obf[:, :], acc[:, :], 0.5,
              inc=s_dve)
        em.wait("gpsimd", s_dve)
        em.dma("gpsimd", mh_d.rearrange("(c p) n -> p c n", p=128),
               obf[:, :].rearrange("p (c n) -> p c n", c=8), s_st)
        em.wait("gpsimd", s_st, s_st.n)

        if _own:
            with nc.Block() as block:
                em.finalize(block)
    return nc


def build_ffn0(fx=None):
    nc, _em0, pre, _own = _fx_open(fx)
    x_d = _fio(fx, nc, "x", [TC, D], F32, "ExternalInput")
    mh_d = _fio(fx, nc, "mh", [D, D], BF16, "ExternalInput")
    id_d = _fio(fx, nc, "ident", [128, 128], BF16, "ExternalInput")
    xo_d = _fio(fx, nc, "xo", [TC, D], F32, "ExternalOutput")

    with ExitStack() as stack:
        em = _em0 if _em0 is not None else Em(nc, stack)
        sb = lambda nm, sh, dt=F32: stack.enter_context(nc.sbuf_tensor(pre + nm, sh, dt))
        ps = lambda nm, sh, dt=F32: stack.enter_context(nc.psum_tensor(pre + nm, sh, dt))

        x_sb = sb("x_sb", [128, NT * D])
        m_sb = sb("m_sb", [128, 8 * D], BF16)
        id_sb = sb("id_sb", [128, 128], BF16)
        zbf = [sb(f"zbf{i}", [128, D], BF16) for i in range(2)]
        zT = [sb(f"zT{i}", [128, D], BF16) for i in range(2)]
        scr = [sb(f"scr{i}", [128, D]) for i in range(2)]
        cols = [sb(f"cols{i}", [128, 8]) for i in range(2)]
        pst = ps("pst", [128, 1024], BF16)
        po = [ps(f"po{j}", [128, 512]) for j in range(2)]

        kc = sb("kc", [128, 8])

        ld = em.sem("ld")
        s_act = em.sem("act")
        s_dve = em.sem("dve")
        s_pe = em.sem("pe")
        s_ev = em.sem("ev")
        s_mm = em.sem("mm")
        s_res = em.sem("res")
        s_st = em.sem("st")

        xv = x_sb[:, :].rearrange("p (t d) -> p t d", t=NT)
        mv = m_sb[:, :].rearrange("p (c d) -> p c d", c=8)

        ldg = em.sem("ldg")
        em.dma("sync", xv, x_d.rearrange("(t p) d -> p t d", p=128), ld)
        em.dma("sync", id_sb[:, :], id_d, ld)
        em.dma("gpsimd", mv, mh_d.rearrange("(c p) d -> p c d", p=128), ldg)
        ldv = ld.n
        ldgv = ldg.n
        em.op("vector", "memset", kc[:, :], 0.0)
        em.op("vector", "memset", kc[:, 0:1], EPS, inc=s_dve)
        em.wait("scalar", s_dve, s_dve.n)
        em.wait("tensor", ld, ldv)
        em.wait("tensor", ldg, ldgv)

        for t in range(NT):
            i = t % 2
            em.wait("scalar", ld, ldv)
            em.wait("scalar", ldg, ldgv)
            war = ((s_pe, 8 * (t - 1)),) if t >= 2 else ()
            emit_ln_tile(em, xv[:, t], zbf[i][:, :], scr[i][:, :],
                         cols[i][:, :], s_act, s_dve, kc[:, 0:1],
                         out_waits=war)
            emit_transpose8(em, zbf[i][:, :], zT[i][:, :], pst, id_sb[:, :],
                            s_dve, s_pe, s_ev)
            for j in range(2):
                for c in range(8):
                    em.op("tensor", "matmul", po[j][:, :],
                          zT[i][:, 128 * c:128 * c + 128],
                          mv[:, c, 512 * j:512 * j + 512],
                          start=(c == 0), stop=(c == 7),
                          waits=((ld, ldv), (ldg, ldgv), (s_ev, s_ev.n),
                                 (s_res, s_res.n)) if c == 0 else (),
                          inc=(s_mm if c == 7 else None))
                em.op("vector", "tensor_add", xv[:, t, 512 * j:512 * j + 512],
                      xv[:, t, 512 * j:512 * j + 512], po[j][:, :],
                      waits=(s_mm,), inc=s_res)
        em.wait("sync", s_res)
        em.dma("sync", xo_d.rearrange("(t p) d -> p t d", p=128), xv, s_st)
        em.wait("gpsimd", s_st, s_st.n)

        if _own:
            with nc.Block() as block:
                em.finalize(block)
    return nc


def build_l2ffn(fx=None):
    nc, _em0, pre, _own = _fx_open(fx)
    x_d = _fio(fx, nc, "x", [TC, D], F32, "ExternalInput")
    g_d = _fio(fx, nc, "gamma", [1, D], F32, "ExternalInput")
    b_d = _fio(fx, nc, "beta", [1, D], F32, "ExternalInput")
    q1_d = _fio(fx, nc, "q1", [D, 4 * D], I32, "ExternalInput")
    s1_d = _fio(fx, nc, "s1", [8, 4 * D], F32, "ExternalInput")
    b1_d = _fio(fx, nc, "b1", [128, 32], F32, "ExternalInput")
    q2_d = _fio(fx, nc, "q2", [4 * D, D], I32, "ExternalInput")
    s2_d = _fio(fx, nc, "s2", [32, D], F32, "ExternalInput")
    b2_d = _fio(fx, nc, "b2", [1, D], F32, "ExternalInput")
    id_d = _fio(fx, nc, "ident", [128, 128], BF16, "ExternalInput")
    xo_d = _fio(fx, nc, "xo", [TC, D], F32, "ExternalOutput")

    HNT = NT // 2

    with ExitStack() as stack:
        em = _em0 if _em0 is not None else Em(nc, stack)
        sb = lambda nm, sh, dt=F32: stack.enter_context(nc.sbuf_tensor(pre + nm, sh, dt))
        ps = lambda nm, sh, dt=F32: stack.enter_context(nc.psum_tensor(pre + nm, sh, dt))

        w1 = sb("w1", [128, 8 * 4 * D], BF16)
        w2 = sb("w2", [128, 32 * D], BF16)
        hT = sb("hT", [128, HNT * D], BF16)
        yT = sb("yT", [128, 32 * 128], BF16)
        xt = [sb(f"xt{i}", [128, D]) for i in range(2)]
        uz = sb("uz", [128, D])
        zscr = sb("zscr", [128, D])
        zbf = sb("zbf", [128, D], BF16)
        stats = sb("stats", [128, 8 * NT])
        cols2 = sb("cols2", [128, 8])
        gbc = sb("gbc", [128, D])
        bbc = sb("bbc", [128, D])
        srow = [sb(f"srow{i}", [1, 2048]) for i in range(2)]
        b1c = sb("b1c", [128, 32])
        b2row = sb("b2row", [1, D], BF16)
        ones_f = sb("ones_f", [1, 128])
        ones_b = sb("ones_b", [1, 128], BF16)
        id_sb = sb("id_sb", [128, 128], BF16)
        kc = sb("kc", [128, 8])

        scr = sb("scr", [128, D], BF16)
        pst = ps("pst", [128, 1024], BF16)
        pbc = ps("pbc", [128, 512])
        pb1 = [ps(f"pb1{i}", [128, 512]) for i in range(2)]
        pb2 = [ps(f"pb2{i}", [128, 512]) for i in range(2)]

        ld = em.sem("ld")
        lds = em.sem("lds")
        ldx = [em.sem("ldx0"), em.sem("ldx1")]
        s_act = em.sem("act")
        s_dve = em.sem("dve")
        s_pe = em.sem("pe")
        s_ev = em.sem("ev")
        s_mm = em.sem("mm")
        s_stp = [em.sem("st0"), em.sem("st1")]
        s_upd = em.sem("upd")

        w1v = w1[:, :].rearrange("p (c n) -> p c n", c=8)
        w2v = w2[:, :].rearrange("p (c n) -> p c n", c=32)
        hTv = hT[:, :].rearrange("p (t n) -> p t n", t=HNT)
        xv_t = x_d.rearrange("(t p) d -> t p d", p=128)
        xov_t = xo_d.rearrange("(t p) d -> t p d", p=128)

        em.op("vector", "memset", kc[:, :], 0.0)
        em.op("vector", "memset", kc[:, 0:1], EPS)
        em.op("vector", "memset", ones_f[:, :], 1.0)
        em.op("vector", "memset", ones_b[:, :], 1.0, inc=s_dve)
        em.dma("sync", srow[0][0:1, 0:D], g_d, lds)
        em.dma("sync", srow[1][0:1, 0:D], b_d, lds)
        em.dma("sync", b1c[:, :], b1_d, ld)
        em.dma("sync", id_sb[:, :], id_d, ld)
        ldg = em.sem("ldg")
        em.dma("gpsimd", b2row[:, :], b2_d, ldg)
        em.dma("gpsimd", w1v, q1_d.rearrange("(c p) n -> p c n", p=128), ldg)
        em.dma("gpsimd", w2v, q2_d.rearrange("(c p) n -> p c n", p=128), ldg)
        ldv = ld.n
        ldgv = ldg.n

        em.wait("tensor", ld, ldv)
        for dst, buf in ((gbc, srow[0]), (bbc, srow[1])):
            for j in range(2):
                em.op("tensor", "matmul", pbc[:, :], ones_f[0:1, :],
                      buf[0:1, 512 * j:512 * j + 512],
                      start=True, stop=True,
                      waits=((lds, lds.n), (s_dve, s_dve.n), (s_ev, s_ev.n)),
                      inc=s_mm)
                em.op("vector", "tensor_copy", dst[:, 512 * j:512 * j + 512],
                      pbc[:, :], waits=(s_mm,), inc=s_ev)
        gb_mm = s_mm.n

        pieces = []
        for c in range(8):
            for h in range(2):
                pieces.append(("w1", c, h))
        for c in range(32):
            pieces.append(("w2", c, 0))
        for idx, (wname, c, h) in enumerate(pieces):
            buf = srow[idx % 2]
            ncols = 2048 if wname == "w1" else 1024
            if wname == "w1":
                src_ap = s1_d[c:c + 1, 2048 * h:2048 * h + 2048]
            else:
                src_ap = s2_d[c:c + 1, :]
            em.dma("sync", buf[0:1, 0:ncols], src_ap, lds,
                   waits=((s_mm, s_mm.n),))
            for j in range(ncols // 512):
                em.op("tensor", "matmul", pbc[:, :], ones_f[0:1, :],
                      buf[0:1, 512 * j:512 * j + 512],
                      start=True, stop=True,
                      waits=((lds, lds.n), (ldg, ldgv), (s_ev, s_ev.n)),
                      inc=s_mm)
                wv = w1v if wname == "w1" else w2v
                off = (2048 * h if wname == "w1" else 0) + 512 * j
                em.op("vector", "scalar_tensor_tensor",
                      wv[:, c, off:off + 512], wv[:, c, off:off + 512],
                      -128.0, pbc[:, :], ALU.add, ALU.mult,
                      waits=(s_mm,), inc=s_ev)
        deq_ev = s_ev.n

        for half in range(2):
            t0 = half * HNT
            st_half = (s_stp[0].n, s_stp[1].n)
            ht_ev = []
            for tt in range(HNT):
                t = t0 + tt
                i = t % 2
                em.dma("sync", xt[i][:, :], xv_t[t], ldx[i],
                       waits=((s_dve, s_dve.n), (s_stp[0], st_half[0]),
                              (s_stp[1], st_half[1])))
                em.wait("scalar", ldx[i], ldx[i].n)
                st_cols = stats[:, 8 * t:8 * t + 8]
                emit_ln_tile(em, xt[i][:, :], None, scr[:, :], st_cols,
                             s_act, s_dve, kc[:, 0:1], zf32_ap=zscr[:, :])
                emit_ln_tile(em, zscr[:, :], zbf[:, :], scr[:, :],
                             cols2[:, :], s_act, s_dve, kc[:, 0:1],
                             gain_ap=gbc[:, :], bias_ap=bbc[:, :],
                             zf32_ap=uz[:, :], out_waits=((s_pe, s_pe.n),),
                             in_waits=((s_dve, s_dve.n),))
                emit_transpose8(em, zbf[:, :], hTv[:, tt], pst, id_sb[:, :],
                                s_dve, s_pe, s_ev)
                ht_ev.append(s_ev.n)
            phaseA_dve = s_dve.n

            sact_pack = getattr(build_l2ffn, "_dummy", None) or []
            sact_pack = []
            for tt in range(HNT):
                t = t0 + tt
                i = t % 2
                em.dma("sync", xt[i][:, :], xv_t[t], ldx[i],
                       waits=((s_upd, s_upd.n), (s_stp[i], s_stp[i].n))
                             if tt >= 2
                             else ((s_dve, phaseA_dve),
                                   (s_stp[0], st_half[0]),
                                   (s_stp[1], st_half[1])))
                ldxv = ldx[i].n
                for p in range(8):
                    bk = p % 2
                    gp = len(sact_pack)
                    for u in range(4):
                        nb = 4 * p + u
                        for c in range(8):
                            first = (u == 0 and c == 0)
                            w = ()
                            if first:
                                w = ((s_ev, max(deq_ev, ht_ev[tt])),)
                                if gp >= 2:
                                    w = w + ((s_act, sact_pack[gp - 2]),)
                            em.op("tensor", "matmul",
                                  pb1[bk][:, 128 * u:128 * u + 128],
                                  w1v[:, c, 128 * nb:128 * nb + 128],
                                  hTv[:, tt, 128 * c:128 * c + 128],
                                  start=first, stop=(u == 3 and c == 7),
                                  waits=w,
                                  inc=(s_mm if (u == 3 and c == 7) else None))
                    for u in range(4):
                        nb = 4 * p + u
                        em.op("scalar", "activation",
                              yT[:, 128 * nb:128 * nb + 128],
                              pb1[bk][:, 128 * u:128 * u + 128], AF.Silu,
                              bias=b1c[:, nb:nb + 1],
                              waits=(s_mm,) if u == 0 else (), inc=s_act)
                    sact_pack.append(s_act.n)
                silu_done = s_act.n
                for j in range(2):
                    for c in range(32):
                        em.op("tensor", "matmul", pb2[j][:, :],
                              yT[:, 128 * c:128 * c + 128],
                              w2v[:, c, 512 * j:512 * j + 512],
                              start=(c == 0), stop=False,
                              waits=((s_act, silu_done), (s_upd, s_upd.n))
                                    if c == 0 else ())
                    em.op("tensor", "matmul", pb2[j][:, :], ones_b[0:1, :],
                          b2row[0:1, 512 * j:512 * j + 512],
                          start=False, stop=True, inc=s_mm)
                fc2_done = s_mm.n
                em.op("vector", "tensor_scalar", uz[:, :], xt[i][:, :],
                      stats[:, 8 * t + 2:8 * t + 3],
                      stats[:, 8 * t + 4:8 * t + 5],
                      ALU.subtract, ALU.mult,
                      waits=((ldx[i], ldxv), (s_mm, fc2_done)))
                for j in range(2):
                    em.op("vector", "scalar_tensor_tensor",
                          xt[i][:, 512 * j:512 * j + 512], pb2[j][:, :],
                          0.25, xt[i][:, 512 * j:512 * j + 512],
                          ALU.mult, ALU.add)
                em.op("vector", "scalar_tensor_tensor", xt[i][:, :],
                      uz[:, :], 0.5, xt[i][:, :], ALU.mult, ALU.add,
                      inc=s_upd)
                em.dma("sync", xov_t[t], xt[i][:, :], s_stp[i],
                       waits=((s_upd, s_upd.n),))
        em.wait("gpsimd", s_stp[0], s_stp[0].n)
        em.wait("gpsimd", s_stp[1], s_stp[1].n)

        if _own:
            with nc.Block() as block:
                em.finalize(block)
    return nc


def step_l2ffn(x_parts, ins):
    b1c = np.ascontiguousarray(ins["L2_ffn_fc1_bias"].reshape(32, 128).T)
    common = {
        "gamma": ins["L2_ffn_ln_gamma"].reshape(1, D),
        "beta": ins["L2_ffn_ln_beta"].reshape(1, D),
        "q1": ins["L2_ffn_fc1_q"],
        "s1": ins["L2_ffn_fc1_scales"],
        "b1": b1c,
        "q2": ins["L2_ffn_fc2_q"],
        "s2": ins["L2_ffn_fc2_scales"],
        "b2": ins["L2_ffn_fc2_bias"].reshape(1, D),
        "ident": IDENT_BF,
    }
    res = _launch("l2ffn", build_l2ffn,
                  [dict(common, x=x_parts[c]) for c in range(NCORES)])
    return [np.asarray(res[c]["xo"]) for c in range(NCORES)]


def build_l3ffn(fx=None):
    nc, _em0, pre, _own = _fx_open(fx)
    x_d = _fio(fx, nc, "x", [TC, D], F32, "ExternalInput")
    rw_d = _fio(fx, nc, "rw", [D, 8], F32, "ExternalInput")
    rb_d = _fio(fx, nc, "rb", [1, 8], F32, "ExternalInput")
    we_d = _fio(fx, nc, "we", [8, D, D], F32, "ExternalInput")
    be_d = _fio(fx, nc, "be", [1, 8 * D], F32, "ExternalInput")
    c8_d = _fio(fx, nc, "cols8m", [128, 8], F32, "ExternalInput")
    idf_d = _fio(fx, nc, "identf", [128, 128], F32, "ExternalInput")
    xo_d = _fio(fx, nc, "xo", [TC, D], F32, "ExternalOutput")

    with ExitStack() as stack:
        em = _em0 if _em0 is not None else Em(nc, stack)
        sb = lambda nm, sh, dt=F32: stack.enter_context(nc.sbuf_tensor(pre + nm, sh, dt))
        ps = lambda nm, sh, dt=F32: stack.enter_context(nc.psum_tensor(pre + nm, sh, dt))

        wexp = sb("wexp", [128, 64 * D], BF16)
        rw_sb = sb("rw_sb", [128, 8 * 8])
        rb_sb = sb("rb_sb", [1, 8])
        be_sb = sb("be_sb", [1, 8 * D], BF16)
        c8_sb = sb("c8_sb", [128, 8])
        idf_sb = sb("idf_sb", [128, 128])
        xt = [sb(f"xt{i}", [128, D]) for i in range(2)]
        zf = sb("zf", [128, D])
        zTf = sb("zTf", [128, D])
        zTb = sb("zTb", [128, D], BF16)
        acc = sb("acc", [128, D])
        scr = sb("scr", [128, D], BF16)
        cols = sb("cols", [128, 8])
        lg = sb("lg", [128, 16])
        eq = sb("eq", [128, 8])
        coefs = sb("coefs", [128, 8])
        ones_f = sb("ones_f", [1, 128])
        ones_b = sb("ones_b", [1, 128], BF16)
        kc = sb("kc", [128, 8])

        pstf = ps("pstf", [128, 512])
        prt = ps("prt", [128, 512])
        pex = [ps(f"pex{i}", [128, 512]) for i in range(4)]

        ld = em.sem("ld")
        ldg = em.sem("ldg")
        ldx = [em.sem("ldx0"), em.sem("ldx1")]
        s_act = em.sem("act")
        s_dve = em.sem("dve")
        s_pe = em.sem("pe")
        s_ev = em.sem("ev")
        s_mm = em.sem("mm")
        s_stp = [em.sem("st0"), em.sem("st1")]

        wv = wexp[:, :].rearrange("p (e c n) -> p e c n", e=8, c=8)
        rwv = rw_sb[:, :].rearrange("p (c e) -> p c e", c=8)
        xv_t = x_d.rearrange("(t p) d -> t p d", p=128)
        xov_t = xo_d.rearrange("(t p) d -> t p d", p=128)

        em.op("vector", "memset", kc[:, :], 0.0)
        em.op("vector", "memset", kc[:, 0:1], EPS)
        em.op("vector", "memset", ones_f[:, :], 1.0)
        em.op("vector", "memset", ones_b[:, :], 1.0, inc=s_dve)
        em.dma("sync", rw_sb[:, :].rearrange("p (c e) -> p c e", c=8),
               rw_d.rearrange("(c p) e -> p c e", p=128), ld)
        em.dma("sync", rb_sb[:, :], rb_d, ld)
        em.dma("sync", c8_sb[:, :], c8_d, ld)
        em.dma("sync", idf_sb[:, :], idf_d, ld)
        em.dma("gpsimd", be_sb[:, :], be_d, ldg)
        em.dma("gpsimd", wv, we_d.rearrange("e (c p) n -> p e c n", p=128), ldg)
        ldv, ldgv = ld.n, ldg.n
        em.wait("tensor", ld, ldv)
        em.wait("tensor", ldg, ldgv)
        em.wait("scalar", s_dve, s_dve.n)

        stt_ev = []
        for t in range(NT):
            i = t % 2
            em.dma("sync", xt[i][:, :], xv_t[t], ldx[i],
                   waits=((s_dve, s_dve.n), (s_stp[i], s_stp[i].n)))
            ldxv = ldx[i].n
            em.wait("scalar", ldx[i], ldxv)
            emit_ln_tile(em, xt[i][:, :], None, scr[:, :], cols[:, :],
                         s_act, s_dve, kc[:, 0:1], zf32_ap=zf[:, :],
                         out_waits=((s_pe, s_pe.n),))
            emit_transpose8(em, zf[:, :], zTf[:, :], pstf, idf_sb[:, :],
                            s_dve, s_pe, s_ev)
            em.op("vector", "tensor_copy", zTb[:, :], zTf[:, :],
                  waits=((s_ev, s_ev.n), (s_mm, s_mm.n)), inc=s_dve)
            ztb_dve = s_dve.n
            for c in range(8):
                em.op("tensor", "matmul", prt[:, 0:8],
                      zTf[:, 128 * c:128 * c + 128], rwv[:, c, :],
                      start=(c == 0), stop=False,
                      waits=((s_ev, s_ev.n), (s_act, s_act.n))
                            if c == 0 else ())
            em.op("tensor", "matmul", prt[:, 0:8], ones_f[0:1, :],
                  rb_sb[0:1, :], start=False, stop=True, inc=s_mm)
            em.op("scalar", "copy", lg[:, 0:8], prt[:, 0:8],
                  waits=(s_mm,), inc=s_act)
            em.op("vector", "reduce_max", lg[:, 8:9], lg[:, 0:8], axis=AX.X,
                  waits=(s_act,))
            em.op("vector", "tensor_scalar_mul", lg[:, 9:10], lg[:, 8:9],
                  -1.0, inc=s_dve)
            em.op("scalar", "activation", eq[:, :], lg[:, 0:8], AF.Exp,
                  bias=lg[:, 9:10], accum_out=lg[:, 10:11],
                  waits=(s_dve,), inc=s_act)
            em.op("vector", "reciprocal", lg[:, 11:12], lg[:, 10:11],
                  waits=(s_act,))
            em.op("vector", "tensor_scalar", eq[:, :], lg[:, 0:8],
                  lg[:, 8:9], None, ALU.is_ge)
            em.op("vector", "tensor_mul", eq[:, :], eq[:, :], c8_sb[:, :])
            em.op("vector", "tensor_reduce", lg[:, 12:13], eq[:, :],
                  AX.X, ALU.min)
            em.op("vector", "tensor_scalar", coefs[:, :], c8_sb[:, :],
                  lg[:, 12:13], None, ALU.is_equal)
            em.op("vector", "tensor_scalar", coefs[:, :], coefs[:, :],
                  lg[:, 11:12], 0.5, ALU.mult, ALU.mult, inc=s_dve)
            coef_dve = s_dve.n
            for e in range(8):
                bk = e % 2
                for j in range(2):
                    for c in range(8):
                        w = ()
                        if c == 0:
                            w = ((s_dve, ztb_dve),)
                            if len(stt_ev) >= 2:
                                w = w + ((s_dve, stt_ev[-2]),)
                        em.op("tensor", "matmul", pex[2 * bk + j][:, :],
                              zTb[:, 128 * c:128 * c + 128],
                              wv[:, e, c, 512 * j:512 * j + 512],
                              start=(c == 0), stop=False, waits=w)
                    em.op("tensor", "matmul", pex[2 * bk + j][:, :],
                          ones_b[0:1, :],
                          be_sb[0:1, D * e + 512 * j:D * e + 512 * j + 512],
                          start=False, stop=True, inc=s_mm)
                mm_e = s_mm.n
                for j in range(2):
                    inc = s_dve if j == 1 else None
                    if e == 0:
                        em.op("vector", "tensor_scalar_mul",
                              acc[:, 512 * j:512 * j + 512],
                              pex[2 * bk + j][:, :], coefs[:, e:e + 1],
                              waits=((s_mm, mm_e), (s_dve, coef_dve)),
                              inc=inc)
                    else:
                        em.op("vector", "scalar_tensor_tensor",
                              acc[:, 512 * j:512 * j + 512],
                              pex[2 * bk + j][:, :], coefs[:, e:e + 1],
                              acc[:, 512 * j:512 * j + 512],
                              ALU.mult, ALU.add,
                              waits=((s_mm, mm_e),), inc=inc)
                stt_ev.append(s_dve.n)
            em.op("vector", "tensor_add", xt[i][:, :], xt[i][:, :],
                  acc[:, :], inc=s_dve)
            em.dma("sync", xov_t[t], xt[i][:, :], s_stp[i],
                   waits=((s_dve, s_dve.n),))
        em.wait("gpsimd", s_stp[0], s_stp[0].n)
        em.wait("gpsimd", s_stp[1], s_stp[1].n)

        if _own:
            with nc.Block() as block:
                em.finalize(block)
    return nc


def step_l3ffn(x_parts, ins):
    cols8m = np.ascontiguousarray(
        np.broadcast_to((np.arange(8) - 100.0).astype(np.float32), (128, 8)))
    common = {
        "rw": ins["L3_ffn_router_w"],
        "rb": ins["L3_ffn_router_b"].reshape(1, 8),
        "we": ins["L3_ffn_expert_w"],
        "be": ins["L3_ffn_expert_b"].reshape(1, 8 * D),
        "cols8m": cols8m,
        "identf": np.eye(128, dtype=np.float32),
    }
    res = _launch("l3ffn", build_l3ffn,
                  [dict(common, x=x_parts[c]) for c in range(NCORES)])
    return [np.asarray(res[c]["xo"]) for c in range(NCORES)]


def build_l2attn_proj(fx=None, v_hm=False):
    nc, _em0, pre, _own = _fx_open(fx)
    x_d = _fio(fx, nc, "x", [TC, D], F32, "ExternalInput")
    wqT_d = _fio(fx, nc, "wqT", [D, D], F32, "ExternalInput")
    wkT_d = _fio(fx, nc, "wkT", [D, D], F32, "ExternalInput")
    wvT_d = _fio(fx, nc, "wvT", [D, D], F32, "ExternalInput")
    bq_d = _fio(fx, nc, "bqc", [128, 8], F32, "ExternalInput")
    bk_d = _fio(fx, nc, "bkc", [128, 8], F32, "ExternalInput")
    bv_d = _fio(fx, nc, "bv", [1, D], F32, "ExternalInput")
    id_d = _fio(fx, nc, "ident", [128, 128], BF16, "ExternalInput")
    qt_d = _fio(fx, nc, "qt", [D, TC], BF16, "ExternalOutput")
    kt_d = _fio(fx, nc, "kt", [D, TC], BF16, "ExternalOutput")
    v_d = _fio(fx, nc, "v", [128, NT * D] if v_hm else [TC, D], BF16,
               "ExternalOutput")

    with ExitStack() as stack:
        em = _em0 if _em0 is not None else Em(nc, stack)
        sb = lambda nm, sh, dt=F32: stack.enter_context(nc.sbuf_tensor(pre + nm, sh, dt))
        ps = lambda nm, sh, dt=F32: stack.enter_context(nc.psum_tensor(pre + nm, sh, dt))

        wq = sb("wq", [128, 8 * D], BF16)
        wk = sb("wk", [128, 8 * D], BF16)
        wv = sb("wv", [128, 8 * D], BF16)
        zT = sb("zT", [128, 8 * TC], BF16)
        qt_sb = sb("qt_sb", [128, 8 * TC], BF16)
        kt_sb = sb("kt_sb", [128, 8 * TC], BF16)
        v_sb = sb("v_sb", [128, NT * D], BF16)
        xt = [sb(f"xt{i}", [128, D]) for i in range(2)]
        zbf = [sb(f"zbf{i}", [128, D], BF16) for i in range(2)]
        scr = sb("scr", [128, D], BF16)
        cols = sb("cols", [128, 8])
        bqc = sb("bqc_sb", [128, 8])
        bkc = sb("bkc_sb", [128, 8])
        bvrow = sb("bvrow", [1, D], BF16)
        ones_b = sb("ones_b", [1, 128], BF16)
        id_sb = sb("id_sb", [128, 128], BF16)
        kc = sb("kc", [128, 8])

        pst = ps("pst", [128, 1024], BF16)
        pp = [ps(f"pp{i}", [128, 512]) for i in range(2)]

        ld = em.sem("ld")
        ldg = em.sem("ldg")
        ldx = [em.sem("ldx0"), em.sem("ldx1")]
        s_act = em.sem("act")
        s_dve = em.sem("dve")
        s_pe = em.sem("pe")
        s_ev = em.sem("ev")
        s_mm = em.sem("mm")
        s_st = em.sem("st")

        wqv = wq[:, :].rearrange("p (c n) -> p c n", c=8)
        wkv = wk[:, :].rearrange("p (c n) -> p c n", c=8)
        wvv = wv[:, :].rearrange("p (c n) -> p c n", c=8)
        zTv = zT[:, :].rearrange("p (c t) -> p c t", c=8)
        qtv = qt_sb[:, :].rearrange("p (c t) -> p c t", c=8)
        ktv = kt_sb[:, :].rearrange("p (c t) -> p c t", c=8)
        vv = v_sb[:, :].rearrange("p (t n) -> p t n", t=NT)
        xv_t = x_d.rearrange("(t p) d -> t p d", p=128)

        em.op("vector", "memset", kc[:, :], 0.0)
        em.op("vector", "memset", kc[:, 0:1], EPS)
        em.op("vector", "memset", ones_b[:, :], 1.0, inc=s_dve)
        em.dma("sync", bqc[:, :], bq_d, ld)
        em.dma("sync", bkc[:, :], bk_d, ld)
        em.dma("sync", id_sb[:, :], id_d, ld)
        em.dma("gpsimd", bvrow[:, :], bv_d, ldg)
        em.dma("gpsimd", wqv, wqT_d.rearrange("(c p) n -> p c n", p=128), ldg)
        em.dma("gpsimd", wkv, wkT_d.rearrange("(c p) n -> p c n", p=128), ldg)
        em.dma("gpsimd", wvv, wvT_d.rearrange("(c p) n -> p c n", p=128), ldg)
        ldv, ldgv = ld.n, ldg.n
        em.wait("tensor", ld, ldv)
        em.wait("tensor", ldg, ldgv)
        em.op("vector", "tensor_scalar_mul", bqc[:, :], bqc[:, :], 0.125,
              waits=((ld, ldv),), inc=s_dve)

        for t in range(NT):
            i = t % 2
            em.dma("sync", xt[i][:, :], xv_t[t], ldx[i],
                   waits=((s_dve, s_dve.n),))
            em.wait("scalar", ldx[i], ldx[i].n)
            war = ((s_pe, 8 * (t - 1)),) if t >= 2 else ()
            emit_ln_tile(em, xt[i][:, :], zbf[i][:, :], scr[:, :],
                         cols[:, :], s_act, s_dve, kc[:, 0:1], out_waits=war)
            for c in range(8):
                em.op("tensor", "transpose", pst[:, 0:128],
                      zbf[i][:, 128 * c:128 * c + 128], id_sb[:, :],
                      waits=((s_dve, s_dve.n), (s_ev, s_ev.n)), inc=s_pe)
                em.op("vector", "tensor_copy", zTv[:, c, 128 * t:128 * t + 128],
                      pst[:, 0:128], waits=(s_pe,), inc=s_ev)
        zt_ev = s_ev.n

        NQG = TC // 512
        for qg in range(NQG):
            for which, (wview, dstv, bias_col, scale) in enumerate(
                    ((wqv, qtv, bqc, 0.125), (wkv, ktv, bkc, 1.0))):
                for e in range(8):
                    bk_ = (qg * 16 + which * 8 + e) % 2
                    for c in range(8):
                        em.op("tensor", "matmul", pp[bk_][:, :],
                              wview[:, c, 128 * e:128 * e + 128],
                              zTv[:, c, 512 * qg:512 * qg + 512],
                              start=(c == 0), stop=(c == 7),
                              waits=((s_ev, zt_ev), (s_act, max(0, s_act.n - 1)))
                                    if c == 0 else (),
                              inc=(s_mm if c == 7 else None))
                    em.op("scalar", "activation",
                          dstv[:, e, 512 * qg:512 * qg + 512],
                          pp[bk_][:, :], AF.Identity, scale=scale,
                          bias=bias_col[:, e:e + 1],
                          waits=(s_mm,), inc=s_act)
        for t in range(NT):
            for j in range(2):
                bk_ = (t * 2 + j) % 2
                for c in range(8):
                    em.op("tensor", "matmul", pp[bk_][:, :],
                          zTv[:, c, 128 * t:128 * t + 128],
                          wvv[:, c, 512 * j:512 * j + 512],
                          start=(c == 0), stop=False,
                          waits=((s_ev, zt_ev), (s_act, max(0, s_act.n - 1)))
                                if c == 0 else ())
                em.op("tensor", "matmul", pp[bk_][:, :], ones_b[0:1, :],
                      bvrow[0:1, 512 * j:512 * j + 512],
                      start=False, stop=True, inc=s_mm)
                if v_hm:
                    vhm = v_sb[:, :].rearrange("p (h t n) -> p h t n",
                                               h=16, t=NT)
                    em.op("scalar", "activation", vhm[:, 8 * j:8 * j + 8, t, :],
                          pp[bk_][:, :], AF.Identity, waits=(s_mm,),
                          inc=s_act)
                else:
                    em.op("scalar", "activation",
                          vv[:, t, 512 * j:512 * j + 512],
                          pp[bk_][:, :], AF.Identity, waits=(s_mm,),
                          inc=s_act)

        em.wait("sync", s_act, s_act.n)
        em.dma("sync", qt_d.rearrange("(c p) t -> p c t", p=128), qtv, s_st)
        em.dma("sync", kt_d.rearrange("(c p) t -> p c t", p=128), ktv, s_st)
        if v_hm:
            em.dma("sync", v_d, v_sb[:, :], s_st)
        else:
            em.dma("sync", v_d.rearrange("(t p) n -> p t n", p=128), vv,
                   s_st)
        em.wait("gpsimd", s_st, s_st.n)

        if _own:
            with nc.Block() as block:
                em.finalize(block)
    return nc


def build_l2attn_flash(fx=None):
    H, HD = 16, 64
    KT_TILES = S // 128
    NPAIR = KT_TILES // 2
    NQG = TC // 512
    nc, _em0, pre, _own = _fx_open(fx)
    x_d = _fio(fx, nc, "x", [TC, D], F32, "ExternalInput")
    qt_d = _fio(fx, nc, "qt", [D, TC], BF16, "ExternalInput")
    kt_d = _fio(fx, nc, "kt_ag", [4 * D, TC], BF16, "ExternalInput")
    va_d = _fio(fx, nc, "v_ag", [512, NT * D], BF16, "ExternalInput")
    woT_d = _fio(fx, nc, "woT", [D, D], F32, "ExternalInput")
    bo_d = _fio(fx, nc, "bo", [1, D], F32, "ExternalInput")
    xo_d = _fio(fx, nc, "xo", [TC, D], F32, "ExternalOutput")

    with ExitStack() as stack:
        em = _em0 if _em0 is not None else Em(nc, stack)
        sb = lambda nm, sh, dt=F32: stack.enter_context(nc.sbuf_tensor(pre + nm, sh, dt))
        ps = lambda nm, sh, dt=F32: stack.enter_context(nc.psum_tensor(pre + nm, sh, dt))

        qt_sb = sb("qt_sb", [64, 16 * TC], BF16)
        kth = [sb(f"kth{i}", [64, S], BF16) for i in range(2)]
        vah = [sb(f"vah{i}", [128, 64 * 65], BF16) for i in range(2)]
        vtmp = [sb(f"vtmp{i}", [128, 64 * 64], BF16) for i in range(2)]
        pT = [sb(f"pT{i}", [128, 1024], BF16) for i in range(2)]
        ctxT = sb("ctxT", [128, 8 * TC], BF16)
        wo = sb("wo", [128, 8 * D], BF16)
        borow = sb("borow", [1, D], BF16)
        rec = sb("rec", [1, 512])
        bcast = sb("bcast", [64, 512])
        xt = [sb(f"xt{i}", [128, D]) for i in range(2)]
        ones_b = sb("ones_b", [1, 128], BF16)
        ones_f = sb("ones_f", [1, 64])
        kc = sb("kc", [128, 8])

        pST = [ps(f"pST{i}", [128, 1024]) for i in range(2)]
        pOT = ps("pOT", [128, 512])
        pbc = ps("pbc", [128, 512])
        pw = [ps(f"pw{i}", [128, 512]) for i in range(2)]

        ldg = em.sem("ldg")
        ldk = [em.sem("ldk0"), em.sem("ldk1")]
        ldvv = [em.sem("ldv0"), em.sem("ldv1")]
        ldx = [em.sem("ldx0"), em.sem("ldx1")]
        s_act = em.sem("act")
        s_dve = em.sem("dve")
        s_sm = em.sem("sm")
        s_pv = em.sem("pv")
        s_mm = em.sem("mm")
        s_st = [em.sem("st0"), em.sem("st1")]

        qtv = qt_sb[:, :].rearrange("p (h t) -> p h t", h=16)
        ctv = ctxT[:, :].rearrange("p (c t) -> p c t", c=8)
        wov = wo[:, :].rearrange("p (c n) -> p c n", c=8)
        xv_t = x_d.rearrange("(t p) d -> t p d", p=128)
        xov_t = xo_d.rearrange("(t p) d -> t p d", p=128)

        em.op("vector", "memset", kc[:, :], 0.0)
        em.op("vector", "memset", kc[:, 1:2], -30.0)
        em.op("vector", "memset", ones_f[:, :], 1.0)
        em.op("vector", "memset", ones_b[:, :], 1.0)
        for i in range(2):
            em.op("vector", "memset",
                  vah[i][:, :].rearrange("p (k u) -> p k u", u=65)[:, :, 64:65],
                  1.0, inc=(s_dve if i == 1 else None))
        em.dma("gpsimd", qtv, qt_d.rearrange("(h p) t -> p h t", p=64), ldg)
        em.dma("gpsimd", wov, woT_d.rearrange("(c p) n -> p c n", p=128), ldg)
        em.dma("gpsimd", borow[:, :], bo_d, ldg)
        ldgv = ldg.n
        em.wait("tensor", ldg, ldgv)
        em.wait("scalar", s_dve, s_dve.n)

        s_vau = em.sem("vau")

        def load_head(h, i, w):
            for r in range(4):
                em.dma("sync", kth[i][:, 2048 * r:2048 * r + 2048],
                       kt_d[1024 * r + 64 * h:1024 * r + 64 * h + 64, :],
                       ldk[i], waits=w if r == 0 else ())
                em.dma("sync", vtmp[i][:, 1024 * r:1024 * r + 1024],
                       va_d[128 * r:128 * r + 128,
                            1024 * h:1024 * h + 1024],
                       ldvv[i], waits=w if r == 0 else ())
            em.op("vector", "tensor_copy",
                  vah[i][:, :].rearrange("p (k u) -> p k u", u=65)[:, :, 0:64],
                  vtmp[i][:, :].rearrange("p (k u) -> p k u", u=64),
                  waits=((ldvv[i], ldvv[i].n),), inc=s_vau)
            kld[h] = (ldk[i].n, s_vau.n)

        kld = {}
        for h in range(2):
            load_head(h, h, ((s_dve, s_dve.n),))

        for h in range(H):
            i = h % 2
            hc, hp = h // 2, 64 * (h % 2)
            for qg in range(NQG):
                q0 = 512 * qg
                exp_hist = {}
                pv_hist = {}
                norm_done = s_dve.n

                def emit_S(pr):
                    pb = pr % 2
                    for half in range(2):
                        kt = 2 * pr + half
                        w = ()
                        if half == 0:
                            w = ((ldk[i], kld[h][0]),
                                 (s_act, exp_hist[pr - 2] if pr >= 2
                                  else s_act.n))
                        em.op("tensor", "matmul",
                              pST[pb][:, 512 * half:512 * half + 512],
                              kth[i][:, 128 * kt:128 * kt + 128],
                              qtv[:, h, q0:q0 + 512],
                              start=True, stop=True, waits=w, inc=s_sm)

                def emit_exp(pr):
                    pb = pr % 2
                    w = ((s_sm, s_sm.n),)
                    w += ((s_pv, pv_hist[pr - 2] if pr >= 2 else s_pv.n),)
                    em.op("scalar", "activation", pT[pb][:, :],
                          pST[pb][:, :], AF.Exp, bias=kc[:, 1:2],
                          waits=w, inc=s_act)
                    exp_hist[pr] = s_act.n

                def emit_pv(pr):
                    pb = pr % 2
                    for half in range(2):
                        kt = 2 * pr + half
                        w = ((s_act, exp_hist[pr]), (s_vau, kld[h][1]))
                        if pr == 0 and half == 0:
                            w += ((s_dve, norm_done),)
                        em.op("tensor", "matmul", pOT[0:65, :],
                              vah[i][:, 65 * kt:65 * kt + 65],
                              pT[pb][:, 512 * half:512 * half + 512],
                              start=(pr == 0 and half == 0),
                              stop=(pr == NPAIR - 1 and half == 1),
                              waits=w, inc=s_pv)
                    pv_hist[pr] = s_pv.n

                for pr in range(NPAIR):
                    emit_S(pr)
                    emit_exp(pr)
                    if pr >= 1:
                        emit_pv(pr - 1)
                emit_pv(NPAIR - 1)
                em.op("vector", "reciprocal", rec[0:1, :], pOT[64:65, :],
                      waits=(s_pv,), inc=s_dve)
                em.op("tensor", "matmul", pbc[0:64, :], ones_f[0:1, :],
                      rec[0:1, :], start=True, stop=True,
                      waits=((s_dve, s_dve.n), (s_act, s_act.n)), inc=s_mm)
                em.op("scalar", "copy", bcast[:, :], pbc[0:64, :],
                      waits=(s_mm,), inc=s_act)
                em.op("vector", "tensor_mul", ctv[hp:hp + 64, hc, q0:q0 + 512],
                      pOT[0:64, :], bcast[:, :],
                      waits=(s_act,), inc=s_dve)
            if h + 2 < H:
                load_head(h + 2, i, ((s_sm, s_sm.n), (s_pv, s_pv.n)))
        ctx_dve = s_dve.n

        for t in range(NT):
            i = t % 2
            em.dma("sync", xt[i][:, :], xv_t[t], ldx[i],
                   waits=((s_dve, s_dve.n), (s_st[i], s_st[i].n)))
            ldxv = ldx[i].n
            for j in range(2):
                pbk = pw[j]
                for c in range(8):
                    em.op("tensor", "matmul", pbk[:, :],
                          ctv[:, c, 128 * t:128 * t + 128],
                          wov[:, c, 512 * j:512 * j + 512],
                          start=(c == 0), stop=False,
                          waits=((s_dve, ctx_dve), (s_dve, s_dve.n))
                                if c == 0 else ())
                em.op("tensor", "matmul", pbk[:, :], ones_b[0:1, :],
                      borow[0:1, 512 * j:512 * j + 512],
                      start=False, stop=True, inc=s_mm)
            mmv = s_mm.n
            for j in range(2):
                em.op("vector", "tensor_add", xt[i][:, 512 * j:512 * j + 512],
                      xt[i][:, 512 * j:512 * j + 512], pw[j][:, :],
                      waits=((s_mm, mmv), (ldx[i], ldxv)),
                      inc=(s_dve if j == 1 else None))
            em.dma("sync", xov_t[t], xt[i][:, :], s_st[i],
                   waits=((s_dve, s_dve.n),))
        em.wait("gpsimd", s_st[0], s_st[0].n)
        em.wait("gpsimd", s_st[1], s_st[1].n)

        if _own:
            with nc.Block() as block:
                em.finalize(block)
    return nc


def step_l2attn(x_parts, ins):
    wqT = np.ascontiguousarray(ins["L2_attn_wq"].T)
    wkT = np.ascontiguousarray(ins["L2_attn_wk"].T)
    wvT = np.ascontiguousarray(ins["L2_attn_wv"].T)
    woT = np.ascontiguousarray(ins["L2_attn_wo"].T)
    bqc = np.ascontiguousarray(ins["L2_attn_bq"].reshape(8, 128).T)
    bkc = np.ascontiguousarray(ins["L2_attn_bk"].reshape(8, 128).T)
    common = {"wqT": wqT, "wkT": wkT, "wvT": wvT, "bqc": bqc, "bkc": bkc,
              "bv": ins["L2_attn_bv"].reshape(1, D), "ident": IDENT_BF}
    res = _launch("l2attn_proj_hm", lambda: build_l2attn_proj(v_hm=True),
                  [dict(common, x=x_parts[c]) for c in range(NCORES)])
    qts = [np.asarray(res[c]["qt"]) for c in range(NCORES)]
    kts = [np.asarray(res[c]["kt"]) for c in range(NCORES)]
    vs = [np.asarray(res[c]["v"]) for c in range(NCORES)]

    maps = []
    for b in range(2):
        kt_ag = np.ascontiguousarray(np.concatenate(kts[4 * b:4 * b + 4], 0))
        v_ag = np.ascontiguousarray(np.concatenate(vs[4 * b:4 * b + 4], 0))
        for ci in range(4):
            c = 4 * b + ci
            maps.append({"x": x_parts[c], "qt": qts[c], "kt_ag": kt_ag,
                         "v_ag": v_ag, "woT": woT,
                         "bo": ins["L2_attn_bo"].reshape(1, D)})
    res = _launch("l2attn_flash", build_l2attn_flash, maps)
    return [np.asarray(res[c]["xo"]) for c in range(NCORES)]


def build_l0attn_proj(fx=None):
    nc, _em0, pre, _own = _fx_open(fx)
    x_d = _fio(fx, nc, "x", [TC, D], F32, "ExternalInput")
    wqT_d = _fio(fx, nc, "wqT", [D, D], F32, "ExternalInput")
    wkT_d = _fio(fx, nc, "wkT", [D, D], F32, "ExternalInput")
    wvT_d = _fio(fx, nc, "wvT", [D, 512], F32, "ExternalInput")
    wqvT_d = _fio(fx, nc, "wqvT", [D, 512], F32, "ExternalInput")
    bq_d = _fio(fx, nc, "bqc", [128, 8], F32, "ExternalInput")
    bk_d = _fio(fx, nc, "bkc", [128, 8], F32, "ExternalInput")
    bv_d = _fio(fx, nc, "bv", [1, 512], F32, "ExternalInput")
    bqv_d = _fio(fx, nc, "bqv", [1, 512], F32, "ExternalInput")
    ws1_d = _fio(fx, nc, "ws1t", [1, 512], F32, "ExternalInput")
    bs_d = _fio(fx, nc, "bs", [1, 1], F32, "ExternalInput")
    g1_d = _fio(fx, nc, "g1", [1, D], F32, "ExternalInput")
    b1_d = _fio(fx, nc, "b1", [1, D], F32, "ExternalInput")
    id_d = _fio(fx, nc, "ident", [128, 128], BF16, "ExternalInput")
    qt_d = _fio(fx, nc, "qt", [D, TC], BF16, "ExternalOutput")
    kt_d = _fio(fx, nc, "kt", [D, TC], BF16, "ExternalOutput")
    v_d = _fio(fx, nc, "v", [TC, 512], BF16, "ExternalOutput")
    qvs_d = _fio(fx, nc, "qvs", [128, NT * 16], F32, "ExternalOutput")

    with ExitStack() as stack:
        em = _em0 if _em0 is not None else Em(nc, stack)
        sb = lambda nm, sh, dt=F32: stack.enter_context(nc.sbuf_tensor(pre + nm, sh, dt))
        ps = lambda nm, sh, dt=F32: stack.enter_context(nc.psum_tensor(pre + nm, sh, dt))

        wq = sb("wq", [128, 8 * D], BF16)
        wk = sb("wk", [128, 8 * D], BF16)
        wv = sb("wv", [128, 8 * 512], BF16)
        wqv = sb("wqv", [128, 8 * 512], BF16)
        zT = sb("zT", [128, 8 * TC], BF16)
        qt_sb = sb("qt_sb", [128, 8 * TC], BF16)
        kt_sb = sb("kt_sb", [128, 8 * TC], BF16)
        v_sb = sb("v_sb", [128, NT * 512], BF16)
        qvs_sb = sb("qvs_sb", [128, NT * 16])
        qvw = sb("qvw", [128, 512])
        xt = [sb(f"xt{i}", [128, D]) for i in range(2)]
        zbf = [sb(f"zbf{i}", [128, D], BF16) for i in range(2)]
        zf = sb("zf", [128, D])
        scr = sb("scr", [128, D], BF16)
        cols = sb("cols", [128, 8])
        gbc = sb("gbc", [128, D])
        bbc = sb("bbc", [128, D])
        ws1bc = sb("ws1bc", [128, 512])
        bqc = sb("bqc_sb", [128, 8])
        bkc = sb("bkc_sb", [128, 8])
        bvrow = sb("bvrow", [1, 512], BF16)
        bqvrow = sb("bqvrow", [1, 512], BF16)
        bscol = sb("bscol", [128, 1])
        grow = sb("grow", [1, 2 * D])
        wsrow = sb("wsrow", [1, 512])
        bsrow = sb("bsrow", [1, 1])
        ones_f = sb("ones_f", [1, 128])
        ones_b = sb("ones_b", [1, 128], BF16)
        id_sb = sb("id_sb", [128, 128], BF16)
        kc = sb("kc", [128, 8])

        pst = ps("pst", [128, 1024], BF16)
        pbc = ps("pbc", [128, 512])
        pp = [ps(f"pp{i}", [128, 512]) for i in range(2)]

        ld = em.sem("ld")
        ldg = em.sem("ldg")
        ldx = [em.sem("ldx0"), em.sem("ldx1")]
        s_act = em.sem("act")
        s_dve = em.sem("dve")
        s_pe = em.sem("pe")
        s_ev = em.sem("ev")
        s_mm = em.sem("mm")
        s_st = em.sem("st")

        wqv_v = wq[:, :].rearrange("p (c n) -> p c n", c=8)
        wkv_v = wk[:, :].rearrange("p (c n) -> p c n", c=8)
        wvv_v = wv[:, :].rearrange("p (c n) -> p c n", c=8)
        wqvv_v = wqv[:, :].rearrange("p (c n) -> p c n", c=8)
        zTv = zT[:, :].rearrange("p (c t) -> p c t", c=8)
        qtv = qt_sb[:, :].rearrange("p (c t) -> p c t", c=8)
        ktv = kt_sb[:, :].rearrange("p (c t) -> p c t", c=8)
        vv = v_sb[:, :].rearrange("p (t n) -> p t n", t=NT)
        qvsv = qvs_sb[:, :].rearrange("p (t h) -> p t h", t=NT)
        xv_t = x_d.rearrange("(t p) d -> t p d", p=128)

        em.op("vector", "memset", kc[:, :], 0.0)
        em.op("vector", "memset", kc[:, 0:1], EPS)
        em.op("vector", "memset", ones_f[:, :], 1.0)
        em.op("vector", "memset", ones_b[:, :], 1.0, inc=s_dve)
        em.dma("sync", bqc[:, :], bq_d, ld)
        em.dma("sync", bkc[:, :], bk_d, ld)
        em.dma("sync", id_sb[:, :], id_d, ld)
        em.dma("sync", grow[0:1, 0:D], g1_d, ld)
        em.dma("sync", grow[0:1, D:2 * D], b1_d, ld)
        em.dma("sync", wsrow[:, :], ws1_d, ld)
        em.dma("sync", bsrow[:, :], bs_d, ld)
        em.dma("gpsimd", bvrow[:, :], bv_d, ldg)
        em.dma("gpsimd", bqvrow[:, :], bqv_d, ldg)
        em.dma("gpsimd", wqv_v, wqT_d.rearrange("(c p) n -> p c n", p=128), ldg)
        em.dma("gpsimd", wkv_v, wkT_d.rearrange("(c p) n -> p c n", p=128), ldg)
        em.dma("gpsimd", wvv_v, wvT_d.rearrange("(c p) n -> p c n", p=128), ldg)
        em.dma("gpsimd", wqvv_v, wqvT_d.rearrange("(c p) n -> p c n", p=128), ldg)
        ldv, ldgv = ld.n, ldg.n
        em.wait("tensor", ld, ldv)
        em.wait("tensor", ldg, ldgv)
        em.op("vector", "tensor_scalar_mul", bqc[:, :], bqc[:, :], 0.125,
              waits=((ld, ldv),), inc=s_dve)

        for dst, buf, n in ((gbc, grow[0:1, 0:D], 1024),
                            (bbc, grow[0:1, D:2 * D], 1024),
                            (ws1bc, wsrow[0:1, :], 512),
                            (bscol, bsrow[0:1, :], 1)):
            for j in range((n + 511) // 512):
                jn = min(512, n - 512 * j)
                em.op("tensor", "matmul", pbc[:, 0:jn], ones_f[0:1, :],
                      buf[0:1, 512 * j:512 * j + jn],
                      start=True, stop=True,
                      waits=((ld, ldv), (s_dve, s_dve.n), (s_ev, s_ev.n)),
                      inc=s_mm)
                em.op("vector", "tensor_copy", dst[:, 512 * j:512 * j + jn],
                      pbc[:, 0:jn], waits=(s_mm,), inc=s_ev)
        bc_ev = s_ev.n

        for t in range(NT):
            i = t % 2
            em.dma("sync", xt[i][:, :], xv_t[t], ldx[i],
                   waits=((s_dve, s_dve.n),))
            em.wait("scalar", ldx[i], ldx[i].n)
            emit_ln_tile(em, xt[i][:, :], None, scr[:, :], cols[:, :],
                         s_act, s_dve, kc[:, 0:1], zf32_ap=zf[:, :])
            emit_ln_tile(em, zf[:, :], zbf[i][:, :], scr[:, :], cols[:, :],
                         s_act, s_dve, kc[:, 0:1], gain_ap=gbc[:, :],
                         bias_ap=bbc[:, :], zf32_ap=xt[i][:, :],
                         out_waits=((s_pe, 8 * (t - 1)) if t >= 2 else (s_pe, 0),),
                         in_waits=((s_dve, s_dve.n), (s_ev, bc_ev)))
            for c in range(8):
                em.op("tensor", "transpose", pst[:, 0:128],
                      zbf[i][:, 128 * c:128 * c + 128], id_sb[:, :],
                      waits=((s_dve, s_dve.n), (s_ev, s_ev.n)), inc=s_pe)
                em.op("vector", "tensor_copy", zTv[:, c, 128 * t:128 * t + 128],
                      pst[:, 0:128], waits=(s_pe,), inc=s_ev)
        zt_ev = s_ev.n

        s_bank = [em.sem("bk0"), em.sem("bk1")]
        NQG = TC // 512
        for qg in range(NQG):
            for which, (wview, dstv, bias_col, scale) in enumerate(
                    ((wqv_v, qtv, bqc, 0.125), (wkv_v, ktv, bkc, 1.0))):
                for e in range(8):
                    bk_ = (qg * 16 + which * 8 + e) % 2
                    for c in range(8):
                        em.op("tensor", "matmul", pp[bk_][:, :],
                              wview[:, c, 128 * e:128 * e + 128],
                              zTv[:, c, 512 * qg:512 * qg + 512],
                              start=(c == 0), stop=(c == 7),
                              waits=((s_ev, zt_ev),
                                     (s_bank[bk_], s_bank[bk_].n))
                                    if c == 0 else (),
                              inc=(s_mm if c == 7 else None))
                    em.op("scalar", "activation",
                          dstv[:, e, 512 * qg:512 * qg + 512],
                          pp[bk_][:, :], AF.Identity, scale=scale,
                          bias=bias_col[:, e:e + 1],
                          waits=(s_mm,), inc=s_bank[bk_])
        for t in range(NT):
            for which, (wview, brow) in enumerate(((wvv_v, bvrow),
                                                   (wqvv_v, bqvrow))):
                bk_ = (t * 2 + which) % 2
                for c in range(8):
                    em.op("tensor", "matmul", pp[bk_][:, :],
                          zTv[:, c, 128 * t:128 * t + 128],
                          wview[:, c, 0:512],
                          start=(c == 0), stop=False,
                          waits=((s_ev, zt_ev),
                                 (s_bank[bk_], s_bank[bk_].n))
                                if c == 0 else ())
                em.op("tensor", "matmul", pp[bk_][:, :], ones_b[0:1, :],
                      brow[0:1, :], start=False, stop=True, inc=s_mm)
                if which == 0:
                    em.op("scalar", "activation", vv[:, t, :],
                          pp[bk_][:, :], AF.Identity, waits=(s_mm,),
                          inc=s_bank[bk_])
                else:
                    em.op("vector", "tensor_mul", qvw[:, :], pp[bk_][:, :],
                          ws1bc[:, :], waits=(s_mm,), inc=s_bank[bk_])
                    for h in range(16):
                        em.op("vector", "reduce_sum",
                              cols[:, 0:1], qvw[:, 32 * h:32 * h + 32],
                              axis=AX.X)
                        em.op("vector", "tensor_scalar",
                              qvsv[:, t, h:h + 1], cols[:, 0:1],
                              0.1767766952966369, bscol[:, 0:1],
                              ALU.mult, ALU.add,
                              inc=(s_dve if h == 15 else None))

        em.wait("sync", s_bank[0], s_bank[0].n)
        em.wait("sync", s_bank[1], s_bank[1].n)
        em.wait("sync", s_dve, s_dve.n)
        em.dma("sync", qt_d.rearrange("(c p) t -> p c t", p=128), qtv, s_st)
        em.dma("sync", kt_d.rearrange("(c p) t -> p c t", p=128), ktv, s_st)
        em.dma("sync", v_d.rearrange("(t p) n -> p t n", p=128), vv, s_st)
        em.dma("sync", qvs_d, qvs_sb[:, :], s_st)
        em.wait("gpsimd", s_st, s_st.n)

        if _own:
            with nc.Block() as block:
                em.finalize(block)
    return nc


def build_l0attn_flash(fx=None):
    H = 16
    KT_TILES = S // 128
    NPAIR = KT_TILES // 2
    NQG = TC // 512
    nc, _em0, pre, _own = _fx_open(fx)
    qt_d = _fio(fx, nc, "qt", [D, TC], BF16, "ExternalInput")
    kt_d = _fio(fx, nc, "kt_ag", [4 * D, TC], BF16, "ExternalInput")
    va_d = _fio(fx, nc, "v_ag", [S, 512], BF16, "ExternalInput")
    qvs_d = _fio(fx, nc, "qvs", [128, NT * 16], F32, "ExternalInput")
    ws2_d = _fio(fx, nc, "ws2t", [1, 512], F32, "ExternalInput")
    om_d = _fio(fx, nc, "om", [TC, D], BF16, "ExternalOutput")

    with ExitStack() as stack:
        em = _em0 if _em0 is not None else Em(nc, stack)
        sb = lambda nm, sh, dt=F32: stack.enter_context(nc.sbuf_tensor(pre + nm, sh, dt))
        ps = lambda nm, sh, dt=F32: stack.enter_context(nc.psum_tensor(pre + nm, sh, dt))

        qt_sb = sb("qt_sb", [64, 16 * TC], BF16)
        va = sb("va_sb", [128, KT_TILES * 512], BF16)
        kth = [sb(f"kth{i}", [64, S], BF16) for i in range(2)]
        pT = [sb(f"pT{i}", [128, 1024], BF16) for i in range(2)]
        om_h = sb("om_h", [128, 2 * NT * 64], BF16)
        osb = sb("osb", [128, 2 * 4 * 512], BF16)
        qvs_sb = sb("qvs_sb", [128, NT * 16])
        ws2bc = sb("ws2bc", [128, 512], BF16)
        wsrow = sb("wsrow", [1, 512], BF16)
        ones_bb = sb("ones_bb", [1, 128], BF16)
        cw = sb("cw", [128, 256], BF16)
        cr4 = sb("cr4", [128, 32])
        e8sb = sb("e8sb", [128, 32])
        mixc = sb("mixc", [128, 12])
        mcol = sb("mcol", [128, 8])
        recd = sb("recd", [128, 8])
        omacc = sb("omacc", [128, 64])
        onescol_b = sb("onescol_b", [128, 1], BF16)
        kc = sb("kc", [128, 2])

        pST = ps("pST", [128, 1024])
        pO = [ps(f"pO{i}", [128, 512]) for i in range(4)]
        pden = ps("pden", [128, 512])
        pbc = ps("pbc", [128, 512])

        ldg = em.sem("ldg")
        ldk = [em.sem("ldk0"), em.sem("ldk1")]
        s_act = em.sem("act")
        s_dve = em.sem("dve")
        s_sm = em.sem("sm")
        s_pv = em.sem("pv")
        s_mm = em.sem("mm")
        s_sto = [em.sem("sto0"), em.sem("sto1")]

        qtv = qt_sb[:, :].rearrange("p (h t) -> p h t", h=16)
        vav = va[:, :].rearrange("p (k n) -> p k n", k=KT_TILES)
        omhv = om_h[:, :].rearrange("p (b t d) -> p b t d", b=2, t=NT)
        omdv = om_d.rearrange("(t p) d -> p t d", p=128)
        qvsv = qvs_sb[:, :].rearrange("p (t h) -> p t h", t=NT)

        em.op("vector", "memset", kc[:, :], 0.0)
        em.op("vector", "memset", kc[:, 1:2], -30.0)
        em.op("vector", "memset", ones_bb[:, :], 1.0)
        em.op("vector", "memset", onescol_b[:, :], 1.0, inc=s_dve)
        em.dma("gpsimd", qtv, qt_d.rearrange("(h p) t -> p h t", p=64), ldg)
        em.dma("gpsimd", vav, va_d.rearrange("(k p) n -> p k n", p=128), ldg)
        em.dma("gpsimd", qvs_sb[:, :], qvs_d, ldg)
        em.dma("gpsimd", wsrow[:, :], ws2_d, ldg)
        ldgv = ldg.n
        em.wait("tensor", ldg, ldgv)
        em.op("tensor", "matmul", pbc[:, :], ones_bb[0:1, :], wsrow[0:1, :],
              start=True, stop=True,
              waits=((s_dve, s_dve.n),), inc=s_mm)
        em.op("vector", "tensor_copy", ws2bc[:, :], pbc[:, :],
              waits=(s_mm,), inc=s_dve)

        kld = {}

        def load_kth(h, w):
            i = h % 2
            for r in range(4):
                em.dma("sync", kth[i][:, 2048 * r:2048 * r + 2048],
                       kt_d[1024 * r + 64 * h:1024 * r + 64 * h + 64, :],
                       ldk[i], waits=w if r == 0 else ())
            kld[h] = ldk[i].n

        load_kth(0, ())
        load_kth(1, ())

        osb_done = [0]
        recd_done = [0]
        last_exp = [0]
        prev_e8_end = [0]

        def emit_mix_pre(p):
            hq, qgq, par = p["h"], p["qg"], p["par"]
            for qs in range(4):
                base = 2048 * par + 512 * qs
                w = ()
                if qs == 0:
                    w = ((s_act, max(p["evac_val"], prev_e8_end[0])),)
                for hh in range(2):
                    em.op("vector", "tensor_mul", cw[:, :],
                          osb[:, base + 256 * hh:base + 256 * hh + 256],
                          ws2bc[:, 256 * hh:256 * hh + 256],
                          waits=w if hh == 0 else ())
                    for r in range(4):
                        em.op("vector", "reduce_sum",
                              cr4[:, 8 * qs + 4 * hh + r:
                                  8 * qs + 4 * hh + r + 1],
                              cw[:, 64 * r:64 * r + 64], axis=AX.X)
                seg = cr4[:, 8 * qs:8 * qs + 8]
                em.op("vector", "tensor_scalar", seg, seg,
                      recd[:, 4 * par + qs:4 * par + qs + 1], None, ALU.mult)
                em.op("vector", "tensor_scalar_add", seg, seg,
                      qvsv[:, 4 * qgq + qs, hq:hq + 1])
                em.op("vector", "reduce_max", mixc[:, 8 + qs:9 + qs],
                      seg, axis=AX.X)
                em.op("vector", "tensor_scalar_mul", mixc[:, qs:qs + 1],
                      mixc[:, 8 + qs:9 + qs], -1.0, inc=s_dve)
                p["pre_done"][qs] = s_dve.n

        def emit_mix_e8(p):
            for qs in range(4):
                em.op("scalar", "activation", e8sb[:, 8 * qs:8 * qs + 8],
                      cr4[:, 8 * qs:8 * qs + 8], AF.Exp,
                      bias=mixc[:, qs:qs + 1],
                      accum_out=mixc[:, 4 + qs:5 + qs],
                      waits=((s_dve, p["pre_done"][qs]),), inc=s_act)
                p["e8_val"][qs] = s_act.n
            prev_e8_end[0] = p["e8_val"][3]

        sto_done = [0, 0]

        def emit_mix_post(p):
            hq, qgq, par = p["h"], p["qg"], p["par"]
            hpar = hq % 2
            for qs in range(4):
                base = 2048 * par + 512 * qs
                tidx = 4 * qgq + qs
                w = ((s_act, p["e8_val"][qs]),)
                if qs == 0:
                    w += ((s_sto[hpar], sto_done[hpar]),)
                em.op("vector", "reciprocal", mixc[:, 8 + qs:9 + qs],
                      mixc[:, 4 + qs:5 + qs], waits=w)
                em.op("vector", "tensor_mul", mixc[:, 8 + qs:9 + qs],
                      mixc[:, 8 + qs:9 + qs],
                      recd[:, 4 * par + qs:4 * par + qs + 1])
                em.op("vector", "tensor_scalar_mul", mcol[:, :],
                      e8sb[:, 8 * qs:8 * qs + 8], mixc[:, 8 + qs:9 + qs])
                for r in range(8):
                    if r == 0:
                        em.op("vector", "tensor_scalar_mul", omacc[:, :],
                              osb[:, base:base + 64], mcol[:, 0:1])
                    else:
                        em.op("vector", "scalar_tensor_tensor",
                              omacc[:, :],
                              osb[:, base + 64 * r:base + 64 * r + 64],
                              mcol[:, r:r + 1], omacc[:, :],
                              ALU.mult, ALU.add)
                em.op("vector", "tensor_copy",
                      omhv[:, hpar, tidx, :], omacc[:, :], inc=s_dve)
            if qgq == NQG - 1:
                em.dma("sync", omdv[:, :, 64 * hq:64 * hq + 64],
                       omhv[:, hpar], s_sto[hpar],
                       waits=((s_dve, s_dve.n),))
                sto_done[hpar] = s_sto[hpar].n

        pend = None
        for h in range(H):
            i = h % 2
            for qg in range(NQG):
                k = h * NQG + qg
                par = k % 2
                if pend is not None:
                    emit_mix_pre(pend)
                q0 = 512 * qg
                exp_hist = {}
                pv_hist = {}

                def emit_S(pr):
                    for half in range(2):
                        kt = 2 * pr + half
                        w = ()
                        if half == 0:
                            w = ((ldk[i], kld[h]),
                                 (s_act, exp_hist[pr - 1] if pr >= 1
                                  else last_exp[0]))
                        em.op("tensor", "matmul",
                              pST[:, 512 * half:512 * half + 512],
                              kth[i][:, 128 * kt:128 * kt + 128],
                              qtv[:, h, q0:q0 + 512],
                              start=True, stop=True, waits=w, inc=s_sm)

                def emit_exp(pr):
                    w = ((s_sm, s_sm.n),)
                    w += ((s_pv, pv_hist[pr - 2] if pr >= 2 else s_pv.n),)
                    em.op("scalar", "activation", pT[pr % 2][:, :],
                          pST[:, :], AF.Exp, bias=kc[:, 1:2],
                          waits=w, inc=s_act)
                    exp_hist[pr] = s_act.n

                def emit_pvden(pr):
                    pb = pr % 2
                    for half in range(2):
                        kt = 2 * pr + half
                        for qs in range(4):
                            w = ((s_act, exp_hist[pr]),)
                            if pr == 0 and half == 0:
                                w += ((s_act, osb_done[0]),
                                      (s_dve, recd_done[0]))
                            em.op("tensor", "matmul", pO[qs][:, :],
                                  pT[pb][:, 512 * half + 128 * qs:
                                         512 * half + 128 * qs + 128],
                                  vav[:, kt, :],
                                  start=(pr == 0 and half == 0),
                                  stop=(pr == NPAIR - 1 and half == 1),
                                  waits=w, inc=s_pv)
                        for qs in range(4):
                            em.op("tensor", "matmul", pden[:, qs:qs + 1],
                                  pT[pb][:, 512 * half + 128 * qs:
                                         512 * half + 128 * qs + 128],
                                  onescol_b[:, :],
                                  start=(pr == 0 and half == 0 and qs == 0),
                                  stop=(pr == NPAIR - 1 and half == 1
                                        and qs == 3),
                                  inc=s_pv)
                    pv_hist[pr] = s_pv.n

                for pr in range(NPAIR):
                    emit_S(pr)
                    emit_exp(pr)
                    if pr >= 1:
                        emit_pvden(pr - 1)
                emit_pvden(NPAIR - 1)
                last_exp[0] = exp_hist[NPAIR - 1]
                em.op("vector", "reciprocal", recd[:, 4 * par:4 * par + 4],
                      pden[:, 0:4], waits=(s_pv,), inc=s_dve)
                recd_done[0] = s_dve.n
                for qs in range(4):
                    em.op("scalar", "copy",
                          osb[:, 2048 * par + 512 * qs:
                              2048 * par + 512 * qs + 512],
                          pO[qs][:, :],
                          waits=((s_pv, s_pv.n), (s_dve, s_dve.n)),
                          inc=s_act)
                osb_done[0] = s_act.n
                if pend is not None:
                    emit_mix_e8(pend)
                    emit_mix_post(pend)
                pend = {"h": h, "qg": qg, "par": par,
                        "evac_val": osb_done[0],
                        "pre_done": [0] * 4, "e8_val": [0] * 4}
            if h + 2 < H:
                load_kth(h + 2, ((s_sm, s_sm.n),))
        emit_mix_pre(pend)
        emit_mix_e8(pend)
        emit_mix_post(pend)

        em.wait("gpsimd", s_sto[0], s_sto[0].n)
        em.wait("gpsimd", s_sto[1], s_sto[1].n)

        if _own:
            with nc.Block() as block:
                em.finalize(block)
    return nc


def build_l0attn_post(fx=None):
    nc, _em0, pre, _own = _fx_open(fx)
    x_d = _fio(fx, nc, "x", [TC, D], F32, "ExternalInput")
    om_d = _fio(fx, nc, "om", [TC, D], BF16, "ExternalInput")
    wfT_d = _fio(fx, nc, "wfT", [D, D], F32, "ExternalInput")
    bf_d = _fio(fx, nc, "bf", [1, D], F32, "ExternalInput")
    g2_d = _fio(fx, nc, "g2", [1, D], F32, "ExternalInput")
    b2_d = _fio(fx, nc, "b2", [1, D], F32, "ExternalInput")
    wr1T_d = _fio(fx, nc, "wr1T", [D, 2 * D], F32, "ExternalInput")
    br1_d = _fio(fx, nc, "br1c", [128, 16], F32, "ExternalInput")
    wr2T_d = _fio(fx, nc, "wr2T", [2 * D, D], F32, "ExternalInput")
    br2_d = _fio(fx, nc, "br2", [1, D], F32, "ExternalInput")
    id_d = _fio(fx, nc, "ident", [128, 128], BF16, "ExternalInput")
    xo_d = _fio(fx, nc, "xo", [TC, D], F32, "ExternalOutput")

    with ExitStack() as stack:
        em = _em0 if _em0 is not None else Em(nc, stack)
        sb = lambda nm, sh, dt=F32: stack.enter_context(nc.sbuf_tensor(pre + nm, sh, dt))
        ps = lambda nm, sh, dt=F32: stack.enter_context(nc.psum_tensor(pre + nm, sh, dt))

        wf = sb("wf", [128, 8 * D], BF16)
        wr1 = sb("wr1", [128, 8 * 2 * D], BF16)
        wr2 = sb("wr2", [128, 16 * D], BF16)
        xt = [sb(f"xt{i}", [128, D]) for i in range(2)]
        omt = [sb(f"omt{i}", [128, D], BF16) for i in range(2)]
        omT = sb("omT", [128, D], BF16)
        t1 = sb("t1", [128, D])
        onb = sb("onb", [128, D], BF16)
        onf = sb("onf", [128, D])
        onT = sb("onT", [128, D], BF16)
        h1T = sb("h1T", [128, 16 * 128], BF16)
        scr = sb("scr", [128, D], BF16)
        cols = sb("cols", [128, 8])
        cols2 = sb("cols2", [128, 8])
        gbc = sb("gbc", [128, D])
        bbc = sb("bbc", [128, D])
        br1c = sb("br1c_sb", [128, 16])
        grow = sb("grow", [1, 2 * D])
        bfrow = sb("bfrow", [1, D], BF16)
        br2row = sb("br2row", [1, D], BF16)
        ones_b = sb("ones_b", [1, 128], BF16)
        ones_f = sb("ones_f", [1, 128])
        id_sb = sb("id_sb", [128, 128], BF16)
        kc = sb("kc", [128, 8])

        pst = ps("pst", [128, 1024], BF16)
        pbc = ps("pbc", [128, 512])
        pm = [ps(f"pm{i}", [128, 512]) for i in range(2)]
        ph = [ps(f"ph{i}", [128, 512]) for i in range(2)]

        ld = em.sem("ld")
        ldg = em.sem("ldg")
        ldx = [em.sem("ldx0"), em.sem("ldx1")]
        ldo = [em.sem("ldo0"), em.sem("ldo1")]
        s_act = em.sem("act")
        s_dve = em.sem("dve")
        s_pe = em.sem("pe")
        s_ev = em.sem("ev")
        s_mm = em.sem("mm")
        s_stp = [em.sem("st0"), em.sem("st1")]
        s_upd = em.sem("upd")

        wfv = wf[:, :].rearrange("p (c n) -> p c n", c=8)
        wr1v = wr1[:, :].rearrange("p (c n) -> p c n", c=8)
        wr2v = wr2[:, :].rearrange("p (c n) -> p c n", c=16)
        xv_t = x_d.rearrange("(t p) d -> t p d", p=128)
        omv_t = om_d.rearrange("(t p) d -> t p d", p=128)
        xov_t = xo_d.rearrange("(t p) d -> t p d", p=128)

        em.op("vector", "memset", kc[:, :], 0.0)
        em.op("vector", "memset", kc[:, 0:1], EPS)
        em.op("vector", "memset", ones_f[:, :], 1.0)
        em.op("vector", "memset", ones_b[:, :], 1.0, inc=s_dve)
        em.dma("sync", grow[0:1, 0:D], g2_d, ld)
        em.dma("sync", grow[0:1, D:2 * D], b2_d, ld)
        em.dma("sync", br1c[:, :], br1_d, ld)
        em.dma("sync", id_sb[:, :], id_d, ld)
        em.dma("gpsimd", bfrow[:, :], bf_d, ldg)
        em.dma("gpsimd", br2row[:, :], br2_d, ldg)
        em.dma("gpsimd", wfv, wfT_d.rearrange("(c p) n -> p c n", p=128), ldg)
        em.dma("gpsimd", wr1v, wr1T_d.rearrange("(c p) n -> p c n", p=128), ldg)
        em.dma("gpsimd", wr2v, wr2T_d.rearrange("(c p) n -> p c n", p=128), ldg)
        ldv, ldgv = ld.n, ldg.n
        em.wait("tensor", ld, ldv)
        em.wait("tensor", ldg, ldgv)

        for dst, off in ((gbc, 0), (bbc, D)):
            for j in range(2):
                em.op("tensor", "matmul", pbc[:, :], ones_f[0:1, :],
                      grow[0:1, off + 512 * j:off + 512 * j + 512],
                      start=True, stop=True,
                      waits=((ld, ldv), (s_dve, s_dve.n), (s_ev, s_ev.n)),
                      inc=s_mm)
                em.op("vector", "tensor_copy", dst[:, 512 * j:512 * j + 512],
                      pbc[:, :], waits=(s_mm,), inc=s_ev)
        bc_ev = s_ev.n

        hp_hist = []
        for t in range(NT):
            i = t % 2
            em.dma("sync", xt[i][:, :], xv_t[t], ldx[i],
                   waits=((s_upd, s_upd.n), (s_stp[i], s_stp[i].n)))
            ldxv = ldx[i].n
            em.dma("sync", omt[i][:, :], omv_t[t], ldo[i],
                   waits=((s_pe, s_pe.n),))
            ldov = ldo[i].n
            em.wait("scalar", ldx[i], ldxv)
            emit_ln_tile(em, xt[i][:, :], None, scr[:, :], cols[:, :],
                         s_act, s_dve, kc[:, 0:1], zf32_ap=t1[:, :])
            em.wait("tensor", ldo[i], ldov)
            for c in range(8):
                em.op("tensor", "transpose", pst[:, 0:128],
                      omt[i][:, 128 * c:128 * c + 128], id_sb[:, :],
                      waits=((s_ev, s_ev.n),), inc=s_pe)
                em.op("vector", "tensor_copy", omT[:, 128 * c:128 * c + 128],
                      pst[:, 0:128], waits=(s_pe,), inc=s_ev)
            omT_ev = s_ev.n
            for j in range(2):
                for c in range(8):
                    em.op("tensor", "matmul", pm[j][:, :],
                          omT[:, 128 * c:128 * c + 128],
                          wfv[:, c, 512 * j:512 * j + 512],
                          start=(c == 0), stop=False,
                          waits=((s_ev, omT_ev), (s_dve, s_dve.n),
                                 (s_upd, s_upd.n))
                                if c == 0 else ())
                em.op("tensor", "matmul", pm[j][:, :], ones_b[0:1, :],
                      bfrow[0:1, 512 * j:512 * j + 512],
                      start=False, stop=True, inc=s_mm)
            mm1 = s_mm.n
            for j in range(2):
                em.op("vector", "tensor_add", t1[:, 512 * j:512 * j + 512],
                      t1[:, 512 * j:512 * j + 512], pm[j][:, :],
                      waits=((s_mm, mm1),))
            em.op("vector", "tensor_add", xt[i][:, :], xt[i][:, :], t1[:, :],
                  inc=s_dve)
            emit_ln_tile(em, t1[:, :], onb[:, :], scr[:, :], cols2[:, :],
                         s_act, s_dve, kc[:, 0:1], gain_ap=gbc[:, :],
                         bias_ap=bbc[:, :], zf32_ap=onf[:, :],
                         out_waits=((s_pe, s_pe.n),),
                         in_waits=((s_dve, s_dve.n), (s_ev, bc_ev)))
            for c in range(8):
                em.op("tensor", "transpose", pst[:, 0:128],
                      onb[:, 128 * c:128 * c + 128], id_sb[:, :],
                      waits=((s_dve, s_dve.n), (s_ev, s_ev.n)), inc=s_pe)
                em.op("vector", "tensor_copy", onT[:, 128 * c:128 * c + 128],
                      pst[:, 0:128], waits=(s_pe,), inc=s_ev)
            onT_ev = s_ev.n
            for p in range(4):
                bk = p % 2
                for u in range(4):
                    nb = 4 * p + u
                    for c in range(8):
                        first = (u == 0 and c == 0)
                        w = ()
                        if first:
                            w = ((s_ev, onT_ev),)
                            if len(hp_hist) >= 2:
                                w += ((s_act, hp_hist[-2]),)
                        em.op("tensor", "matmul",
                              ph[bk][:, 128 * u:128 * u + 128],
                              wr1v[:, c, 128 * nb:128 * nb + 128],
                              onT[:, 128 * c:128 * c + 128],
                              start=first, stop=(u == 3 and c == 7),
                              waits=w,
                              inc=(s_mm if (u == 3 and c == 7) else None))
                for u in range(4):
                    nb = 4 * p + u
                    em.op("scalar", "activation",
                          h1T[:, 128 * nb:128 * nb + 128],
                          ph[bk][:, 128 * u:128 * u + 128], AF.Relu,
                          bias=br1c[:, nb:nb + 1],
                          waits=(s_mm,) if u == 0 else (), inc=s_act)
                hp_hist.append(s_act.n)
            relu_done = s_act.n
            for j in range(2):
                for c in range(16):
                    em.op("tensor", "matmul", pm[j][:, :],
                          h1T[:, 128 * c:128 * c + 128],
                          wr2v[:, c, 512 * j:512 * j + 512],
                          start=(c == 0), stop=False,
                          waits=((s_act, relu_done), (s_dve, s_dve.n))
                                if c == 0 else ())
                em.op("tensor", "matmul", pm[j][:, :], ones_b[0:1, :],
                      br2row[0:1, 512 * j:512 * j + 512],
                      start=False, stop=True, inc=s_mm)
            mm3 = s_mm.n
            for j in range(2):
                em.op("vector", "tensor_add", xt[i][:, 512 * j:512 * j + 512],
                      xt[i][:, 512 * j:512 * j + 512], pm[j][:, :],
                      waits=((s_mm, mm3),),
                      inc=(s_upd if j == 1 else None))
            em.dma("sync", xov_t[t], xt[i][:, :], s_stp[i],
                   waits=((s_upd, s_upd.n),))
        em.wait("gpsimd", s_stp[0], s_stp[0].n)
        em.wait("gpsimd", s_stp[1], s_stp[1].n)

        if _own:
            with nc.Block() as block:
                em.finalize(block)
    return nc


def step_l0attn(x_parts, ins):
    common_p = {
        "wqT": np.ascontiguousarray(ins["L0_attn_Wq"].T),
        "wkT": np.ascontiguousarray(ins["L0_attn_Wk"].T),
        "wvT": np.ascontiguousarray(ins["L0_attn_Wv"].T),
        "wqvT": np.ascontiguousarray(ins["L0_attn_Wqv"].T),
        "bqc": np.ascontiguousarray(ins["L0_attn_bq"].reshape(8, 128).T),
        "bkc": np.ascontiguousarray(ins["L0_attn_bk"].reshape(8, 128).T),
        "bv": ins["L0_attn_bv"].reshape(1, 512),
        "bqv": ins["L0_attn_bqv"].reshape(1, 512),
        "ws1t": np.ascontiguousarray(np.tile(ins["L0_attn_Ws"][0, :32], 16)).reshape(1, 512),
        "bs": ins["L0_attn_bs"].reshape(1, 1),
        "g1": ins["L0_attn_g1"].reshape(1, D),
        "b1": ins["L0_attn_b1"].reshape(1, D),
        "ident": IDENT_BF,
    }
    res = _launch("l0attn_proj", build_l0attn_proj,
                  [dict(common_p, x=x_parts[c]) for c in range(NCORES)])
    qts = [np.asarray(res[c]["qt"]) for c in range(NCORES)]
    kts = [np.asarray(res[c]["kt"]) for c in range(NCORES)]
    vs = [np.asarray(res[c]["v"]) for c in range(NCORES)]
    qvss = [np.asarray(res[c]["qvs"]) for c in range(NCORES)]

    ws2t = np.ascontiguousarray(np.tile(ins["L0_attn_Ws"][0, 32:96], 8)).reshape(1, 512)
    maps = []
    for b in range(2):
        kt_ag = np.ascontiguousarray(np.concatenate(kts[4 * b:4 * b + 4], 0))
        v_ag = np.ascontiguousarray(np.concatenate(vs[4 * b:4 * b + 4], 0))
        for ci in range(4):
            c = 4 * b + ci
            maps.append({"qt": qts[c], "kt_ag": kt_ag, "v_ag": v_ag,
                         "qvs": qvss[c], "ws2t": ws2t})
    res = _launch("l0attn_flash", build_l0attn_flash, maps)
    oms = [np.asarray(res[c]["om"]) for c in range(NCORES)]

    common_q = {
        "wfT": np.ascontiguousarray(ins["L0_attn_Wf"].T),
        "bf": ins["L0_attn_bf"].reshape(1, D),
        "g2": ins["L0_attn_g2"].reshape(1, D),
        "b2": ins["L0_attn_b2"].reshape(1, D),
        "wr1T": np.ascontiguousarray(ins["L0_attn_Wr1"].T),
        "br1c": np.ascontiguousarray(ins["L0_attn_br1"].reshape(16, 128).T),
        "wr2T": np.ascontiguousarray(ins["L0_attn_Wr2"].T),
        "br2": ins["L0_attn_br2"].reshape(1, D),
        "ident": IDENT_BF,
    }
    res = _launch("l0attn_post", build_l0attn_post,
                  [dict(common_q, x=x_parts[c], om=oms[c])
                   for c in range(NCORES)])
    return [np.asarray(res[c]["xo"]) for c in range(NCORES)]


def build_l1ffn_part(fx=None):
    ES = 256
    nc, _em0, pre, _own = _fx_open(fx)
    x_d = _fio(fx, nc, "x", [TC, D], F32, "ExternalInput")
    mu_d = _fio(fx, nc, "mu", [D, ES], F32, "ExternalInput")
    sc_d = _fio(fx, nc, "scale", [1, 1], F32, "ExternalInput")
    g_d = _fio(fx, nc, "gamma", [1, D], F32, "ExternalInput")
    b_d = _fio(fx, nc, "beta", [1, D], F32, "ExternalInput")
    id_d = _fio(fx, nc, "ident", [128, 128], BF16, "ExternalInput")
    num_d = _fio(fx, nc, "num", [ES, D], F32, "ExternalOutput")
    den_d = _fio(fx, nc, "den", [128, 2], F32, "ExternalOutput")
    w_d = _fio(fx, nc, "w", [TC, ES], BF16, "ExternalOutput")

    with ExitStack() as stack:
        em = _em0 if _em0 is not None else Em(nc, stack)
        sb = lambda nm, sh, dt=F32: stack.enter_context(nc.sbuf_tensor(pre + nm, sh, dt))
        ps = lambda nm, sh, dt=F32: stack.enter_context(nc.psum_tensor(pre + nm, sh, dt))

        mu_sb = sb("mu_sb", [128, 8 * ES])
        mbf = sb("mbf", [128, 8 * ES], BF16)
        musq = sb("musq", [128, 8 * ES], BF16)
        rn = sb("rn", [1, ES])
        rnbc = sb("rnbc", [128, ES])
        gbc = sb("gbc", [128, D])
        bbc = sb("bbc", [128, D])
        grow = sb("grow", [1, 2 * D])
        screw = sb("screw", [1, 2])
        xt = [sb(f"xt{i}", [128, D]) for i in range(2)]
        zf = sb("zf", [128, D])
        xnf = sb("xnf", [128, D])
        xnb = sb("xnb", [128, D], BF16)
        xub = sb("xub", [128, D], BF16)
        xuT = sb("xuT", [128, D], BF16)
        wsb = sb("wsb", [128, NT * ES], BF16)
        numsb = sb("numsb", [128, 2 * D])
        densb = sb("densb", [128, 2])
        scr = sb("scr", [128, D], BF16)
        cols = sb("cols", [128, 8])
        ones_f = sb("ones_f", [1, 128])
        onescol_b = sb("onescol_b", [128, 1], BF16)
        id_sb = sb("id_sb", [128, 128], BF16)
        kc = sb("kc", [128, 8])

        pst = ps("pst", [128, 1024], BF16)
        pbc = ps("pbc", [128, 512])
        plg = ps("plg", [128, 256])
        pnum = [ps(f"pnum{i}", [128, 512]) for i in range(4)]
        pden = ps("pden", [128, 512])

        ld = em.sem("ld")
        ldx = [em.sem("ldx0"), em.sem("ldx1")]
        s_act = em.sem("act")
        s_dve = em.sem("dve")
        s_pe = em.sem("pe")
        s_ev = em.sem("ev")
        s_mm = em.sem("mm")
        s_nm = em.sem("nm")
        s_st = em.sem("st")

        muv = mu_sb[:, :].rearrange("p (c n) -> p c n", c=8)
        mbv = mbf[:, :].rearrange("p (c n) -> p c n", c=8)
        msv = musq[:, :].rearrange("p (c n) -> p c n", c=8)
        wv = wsb[:, :].rearrange("p (t n) -> p t n", t=NT)
        xv_t = x_d.rearrange("(t p) d -> t p d", p=128)

        em.op("vector", "memset", kc[:, :], 0.0)
        em.op("vector", "memset", kc[:, 0:1], EPS)
        em.op("vector", "memset", kc[:, 1:2], 1e-24)
        em.op("vector", "memset", ones_f[:, :], 1.0)
        em.op("vector", "memset", onescol_b[:, :], 1.0, inc=s_dve)
        em.dma("sync", mu_sb[:, :].rearrange("p (c n) -> p c n", c=8),
               mu_d.rearrange("(c p) n -> p c n", p=128), ld)
        em.dma("sync", grow[0:1, 0:D], g_d, ld)
        em.dma("sync", grow[0:1, D:2 * D], b_d, ld)
        em.dma("sync", screw[0:1, 0:1], sc_d, ld)
        em.dma("sync", id_sb[:, :], id_d, ld)
        ldv = ld.n
        em.wait("tensor", ld, ldv)
        em.wait("scalar", ld, ldv)

        for dst, off in ((gbc, 0), (bbc, D)):
            for j in range(2):
                em.op("tensor", "matmul", pbc[:, :], ones_f[0:1, :],
                      grow[0:1, off + 512 * j:off + 512 * j + 512],
                      start=True, stop=True,
                      waits=((ld, ldv), (s_dve, s_dve.n), (s_ev, s_ev.n)),
                      inc=s_mm)
                em.op("vector", "tensor_copy", dst[:, 512 * j:512 * j + 512],
                      pbc[:, :], waits=(s_mm,), inc=s_ev)

        em.op("scalar", "activation", musq[:, :], mu_sb[:, :], AF.Square,
              waits=((ld, ldv),), inc=s_act)
        for c in range(8):
            em.op("tensor", "matmul", plg[0:1, 0:ES],
                  onescol_b[:, :], msv[:, c, :],
                  start=(c == 0), stop=(c == 7),
                  waits=((s_act, s_act.n),) if c == 0 else (),
                  inc=(s_mm if c == 7 else None))
        em.op("vector", "tensor_scalar_max", rn[0:1, :], plg[0:1, 0:ES],
              1e-24, waits=(s_mm,))
        em.op("vector", "engine_nop", inc=s_dve)
        em.op("scalar", "activation", rn[0:1, :], rn[0:1, :], AF.Ln,
              waits=(s_dve,))
        em.op("scalar", "activation", rn[0:1, :], rn[0:1, :], AF.Exp,
              scale=-0.5, inc=s_act)
        em.op("vector", "tensor_scalar_mul", rn[0:1, :], rn[0:1, :],
              screw[0:1, 0:1], waits=(s_act,), inc=s_dve)
        em.op("tensor", "matmul", pbc[:, 0:ES], ones_f[0:1, :], rn[0:1, :],
              start=True, stop=True, waits=((s_dve, s_dve.n), (s_ev, s_ev.n)),
              inc=s_mm)
        em.op("vector", "tensor_copy", rnbc[:, :], pbc[:, 0:ES],
              waits=(s_mm,), inc=s_ev)
        for c in range(8):
            em.op("vector", "tensor_mul", mbv[:, c, :], muv[:, c, :],
                  rnbc[:, :], inc=(s_dve if c == 7 else None))
        mb_dve = s_dve.n

        for t in range(NT):
            i = t % 2
            em.dma("sync", xt[i][:, :], xv_t[t], ldx[i],
                   waits=((s_dve, s_dve.n),))
            em.wait("scalar", ldx[i], ldx[i].n)
            emit_ln_tile(em, xt[i][:, :], None, scr[:, :], cols[:, :],
                         s_act, s_dve, kc[:, 0:1], zf32_ap=zf[:, :])
            emit_ln_tile(em, zf[:, :], xnb[:, :], scr[:, :], cols[:, :],
                         s_act, s_dve, kc[:, 0:1], gain_ap=gbc[:, :],
                         bias_ap=bbc[:, :], zf32_ap=xnf[:, :],
                         in_waits=((s_dve, s_dve.n), (s_ev, s_ev.n)),
                         out_waits=((s_nm, s_nm.n),))
            em.op("scalar", "activation", scr[:, :], xnf[:, :], AF.Square,
                  accum_out=cols[:, 5:6], waits=((s_dve, s_dve.n),),
                  inc=s_act)
            em.op("vector", "tensor_scalar_max", cols[:, 6:7], cols[:, 5:6],
                  kc[:, 1:2], waits=((s_act, s_act.n),))
            em.op("vector", "engine_nop", inc=s_dve)
            em.op("scalar", "activation", cols[:, 6:7], cols[:, 6:7], AF.Ln,
                  waits=(s_dve,))
            em.op("scalar", "activation", cols[:, 6:7], cols[:, 6:7], AF.Exp,
                  scale=-0.5, inc=s_act)
            em.op("vector", "tensor_scalar_mul", xub[:, :], xnf[:, :],
                  cols[:, 6:7], waits=(s_act,), inc=s_dve)
            for c in range(8):
                em.op("tensor", "transpose", pst[:, 0:128],
                      xub[:, 128 * c:128 * c + 128], id_sb[:, :],
                      waits=((s_dve, s_dve.n), (s_ev, s_ev.n)), inc=s_pe)
                em.op("vector", "tensor_copy", xuT[:, 128 * c:128 * c + 128],
                      pst[:, 0:128], waits=(s_pe,), inc=s_ev)
            for c in range(8):
                em.op("tensor", "matmul", plg[:, :],
                      xuT[:, 128 * c:128 * c + 128], mbv[:, c, :],
                      start=(c == 0), stop=(c == 7),
                      waits=((s_ev, s_ev.n), (s_dve, mb_dve),
                             (s_act, s_act.n))
                            if c == 0 else (),
                      inc=(s_mm if c == 7 else None))
            em.op("scalar", "activation", wv[:, t, :], plg[:, :], AF.Exp,
                  waits=((s_mm, s_mm.n),), inc=s_act)
            for u in range(2):
                for j in range(2):
                    em.op("tensor", "matmul", pnum[2 * u + j][:, :],
                          wv[:, t, 128 * u:128 * u + 128],
                          xnb[:, 512 * j:512 * j + 512],
                          start=(t == 0), stop=(t == NT - 1),
                          waits=((s_act, s_act.n), (s_dve, s_dve.n))
                                if j == 0 else (),
                          inc=s_nm)
                em.op("tensor", "matmul", pden[:, u:u + 1],
                      wv[:, t, 128 * u:128 * u + 128], onescol_b[:, :],
                      start=(t == 0 and u == 0),
                      stop=(t == NT - 1 and u == 1), inc=s_nm)
        for u in range(2):
            for j in range(2):
                em.op("vector", "tensor_copy",
                      numsb[:, 1024 * u + 512 * j:1024 * u + 512 * j + 512],
                      pnum[2 * u + j][:, :], waits=((s_nm, s_nm.n),),
                      inc=s_dve)
        em.op("vector", "tensor_copy", densb[:, :], pden[:, 0:2],
              waits=((s_nm, s_nm.n),), inc=s_dve)
        em.wait("sync", s_dve, s_dve.n)
        em.wait("sync", s_act, s_act.n)
        em.dma("sync", num_d.rearrange("(u p) d -> p u d", p=128),
               numsb[:, :].rearrange("p (u d) -> p u d", u=2), s_st)
        em.dma("sync", den_d, densb[:, :], s_st)
        em.dma("sync", w_d.rearrange("(t p) n -> p t n", p=128), wv, s_st)
        em.wait("gpsimd", s_st, s_st.n)

        if _own:
            with nc.Block() as block:
                em.finalize(block)
    return nc


def build_l1ffn_comb(fx=None, nsum=4):
    ES, HH = 256, 256
    nc, _em0, pre, _own = _fx_open(fx)
    x_d = _fio(fx, nc, "x", [TC, D], F32, "ExternalInput")
    nums_d = _fio(fx, nc, "nums", [nsum, ES, D], F32, "ExternalInput")
    dens_d = _fio(fx, nc, "dens", [nsum, 128, 2], F32, "ExternalInput")
    w_dd = _fio(fx, nc, "w", [TC, ES], BF16, "ExternalInput")
    w1_d = _fio(fx, nc, "w1", [128, 8 * 16 * HH], F32, "ExternalInput")
    b1_d = _fio(fx, nc, "b1c", [128, 32], F32, "ExternalInput")
    w2_d = _fio(fx, nc, "w2", [128, 2 * 16 * D], F32, "ExternalInput")
    b2_d = _fio(fx, nc, "b2", [16, D], F32, "ExternalInput")
    sel_d = _fio(fx, nc, "sel", [16, ES], BF16, "ExternalInput")
    id_d = _fio(fx, nc, "ident", [128, 128], BF16, "ExternalInput")
    xo_d = _fio(fx, nc, "xo", [TC, D], F32, "ExternalOutput")

    with ExitStack() as stack:
        em = _em0 if _em0 is not None else Em(nc, stack)
        sb = lambda nm, sh, dt=F32: stack.enter_context(nc.sbuf_tensor(pre + nm, sh, dt))
        ps = lambda nm, sh, dt=F32: stack.enter_context(nc.psum_tensor(pre + nm, sh, dt))

        w1 = sb("w1sb", [128, 8 * 16 * HH], BF16)
        w2 = sb("w2sb", [128, 2 * 16 * D], BF16)
        b2r = sb("b2r", [16, D], BF16)
        selr = sb("selr", [16, ES], BF16)
        b1c = sb("b1c_sb", [128, 32])
        num = sb("num", [128, 2 * D])
        den = sb("den", [128, 2])
        sin_b = sb("sin_b", [128, 2 * D], BF16)
        sinT = sb("sinT", [128, 8 * ES], BF16)
        htb = [sb(f"htb{u}", [128, 16 * 128], BF16) for u in range(2)]
        soub = sb("soub", [128, 2 * D], BF16)
        wt = [sb(f"wt{i}", [128, ES], BF16) for i in range(2)]
        combT = sb("combT", [128, 2 * 128], BF16)
        xt = [sb(f"xt{i}", [128, D]) for i in range(2)]
        rs = sb("rs", [128, 4])
        id_sb = sb("id_sb", [128, 128], BF16)
        ones_b = sb("ones_b", [1, 128], BF16)

        pst = ps("pst", [128, 1024], BF16)
        ph = [ps(f"ph{i}", [128, 512]) for i in range(2)]
        po = [ps(f"po{i}", [128, 512]) for i in range(2)]

        ld = em.sem("ld")
        ldg = em.sem("ldg")
        ldx = [em.sem("ldx0"), em.sem("ldx1")]
        ldw = [em.sem("ldw0"), em.sem("ldw1")]
        s_act = em.sem("act")
        s_dve = em.sem("dve")
        s_pe = em.sem("pe")
        s_ev = em.sem("ev")
        s_mm = em.sem("mm")
        s_stp = [em.sem("st0"), em.sem("st1")]

        w1v = w1[:, :].rearrange("p (c e n) -> p c e n", c=8, e=16)
        w2v = w2[:, :].rearrange("p (c e n) -> p c e n", c=2, e=16)
        xv_t = x_d.rearrange("(t p) d -> t p d", p=128)
        wv_t = w_dd.rearrange("(t p) n -> t p n", p=128)
        xov_t = xo_d.rearrange("(t p) d -> t p d", p=128)

        em.op("vector", "memset", ones_b[:, :], 1.0)
        em.op("vector", "memset", htb[0][:, :], 0.0)
        em.op("vector", "memset", htb[1][:, :], 0.0, inc=s_dve)
        em.dma("sync", b1c[:, :], b1_d, ld)
        em.dma("sync", id_sb[:, :], id_d, ld)
        em.dma("gpsimd", b2r[:, :], b2_d, ldg)
        em.dma("gpsimd", selr[:, :], sel_d, ldg)
        em.dma("gpsimd", w1[:, :], w1_d, ldg)
        em.dma("gpsimd", w2[:, :], w2_d, ldg)
        em.dma("gpsimd", num[:, :].rearrange("p (u d) -> p u d", u=2),
               nums_d[0].rearrange("(u p) d -> p u d", p=128), ldg)
        em.dma("gpsimd", den[:, :], dens_d[0], ldg)
        for i in range(1, nsum):
            em.op("gpsimd", "dma_start",
                  out=num[:, :].rearrange("p (u d) -> p u d", u=2),
                  in_=nums_d[i].rearrange("(u p) d -> p u d", p=128),
                  accum_op=ALU.add, waits=((ldg, ldg.n),), inc=ldg, incv=16)
            em.op("gpsimd", "dma_start", out=den[:, :], in_=dens_d[i],
                  accum_op=ALU.add, waits=((ldg, ldg.n),), inc=ldg, incv=16)
        ldv, ldgv = ld.n, ldg.n
        em.wait("tensor", ld, ldv)
        em.wait("tensor", ldg, ldgv)
        em.wait("vector", ldg, ldgv)

        em.op("vector", "reciprocal", rs[:, 0:2], den[:, :], inc=s_dve)
        for u in range(2):
            em.op("vector", "tensor_scalar_mul",
                  sin_b[:, 1024 * u:1024 * u + 1024],
                  num[:, 1024 * u:1024 * u + 1024], rs[:, u:u + 1],
                  inc=s_dve)
        for u in range(2):
            for c in range(8):
                em.op("tensor", "transpose", pst[:, 0:128],
                      sin_b[:, 1024 * u + 128 * c:1024 * u + 128 * c + 128],
                      id_sb[:, :],
                      waits=((s_dve, s_dve.n), (s_ev, s_ev.n)), inc=s_pe)
                em.op("vector", "tensor_copy",
                      sinT[:, 256 * c + 128 * u:256 * c + 128 * u + 128],
                      pst[:, 0:128], waits=(s_pe,), inc=s_ev)
        sinT_ev = s_ev.n
        sinTv = sinT[:, :].rearrange("p (c n) -> p c n", c=8)

        for e in range(16):
            u_e, r = e // 8, e % 8
            for sub in range(2):
                bk = (2 * e + sub) % 2
                for c in range(8):
                    em.op("tensor", "matmul", ph[bk][:, 0:16],
                          w1v[:, c, e, 128 * sub:128 * sub + 128],
                          sinTv[:, c, 16 * e:16 * e + 16],
                          start=(c == 0), stop=(c == 7),
                          waits=((s_ev, sinT_ev),
                                 (s_act, max(0, s_act.n - 1)))
                                if c == 0 else (),
                          inc=(s_mm if c == 7 else None))
                kc_ = 2 * r + sub
                em.op("scalar", "activation",
                      htb[u_e][:, 128 * kc_ + 16 * r:128 * kc_ + 16 * r + 16],
                      ph[bk][:, 0:16], AF.Gelu,
                      bias=b1c[:, 2 * e + sub:2 * e + sub + 1],
                      waits=(s_mm,), inc=s_act)
        ht_act = s_act.n

        for u in range(2):
            for j in range(2):
                bk = (2 * u + j) % 2
                for kc_ in range(16):
                    r, sub = kc_ // 2, kc_ % 2
                    em.op("tensor", "matmul", po[bk][:, :],
                          htb[u][:, 128 * kc_:128 * kc_ + 128],
                          w2v[:, sub, 8 * u + r, 512 * j:512 * j + 512],
                          start=(kc_ == 0), stop=False,
                          waits=((s_act, ht_act), (s_dve, s_dve.n))
                                if kc_ == 0 else ())
                em.op("tensor", "matmul", po[bk][:, :],
                      selr[0:16, 128 * u:128 * u + 128],
                      b2r[0:16, 512 * j:512 * j + 512],
                      start=False, stop=True, inc=s_mm)
                em.op("vector", "tensor_copy",
                      soub[:, 1024 * u + 512 * j:1024 * u + 512 * j + 512],
                      po[bk][:, :], waits=(s_mm,), inc=s_dve)
        so_dve = s_dve.n

        for t in range(NT):
            i = t % 2
            em.dma("sync", xt[i][:, :], xv_t[t], ldx[i],
                   waits=((s_dve, s_dve.n), (s_stp[i], s_stp[i].n)))
            ldxv = ldx[i].n
            em.dma("sync", wt[i][:, :], wv_t[t], ldw[i],
                   waits=((s_pe, s_pe.n),))
            em.wait("vector", ldw[i], ldw[i].n)
            em.op("vector", "reduce_sum", rs[:, 2:3], wt[i][:, :], axis=AX.X)
            em.op("vector", "reciprocal", rs[:, 3:4], rs[:, 2:3])
            em.op("vector", "tensor_scalar", wt[i][:, :], wt[i][:, :],
                  rs[:, 3:4], 0.5, ALU.mult, ALU.mult, inc=s_dve)
            for u in range(2):
                em.op("tensor", "transpose", pst[:, 0:128],
                      wt[i][:, 128 * u:128 * u + 128], id_sb[:, :],
                      waits=((s_dve, s_dve.n), (s_ev, s_ev.n)), inc=s_pe)
                em.op("vector", "tensor_copy",
                      combT[:, 128 * u:128 * u + 128],
                      pst[:, 0:128], waits=(s_pe,), inc=s_ev)
            cT_ev = s_ev.n
            for j in range(2):
                bk = j
                for u in range(2):
                    em.op("tensor", "matmul", po[bk][:, :],
                          combT[:, 128 * u:128 * u + 128],
                          soub[:, 1024 * u + 512 * j:1024 * u + 512 * j + 512],
                          start=(u == 0), stop=(u == 1),
                          waits=((s_ev, cT_ev), (s_dve, max(so_dve, s_dve.n)))
                                if u == 0 else (),
                          inc=(s_mm if u == 1 else None))
            mmv = s_mm.n
            for j in range(2):
                em.op("vector", "tensor_add", xt[i][:, 512 * j:512 * j + 512],
                      xt[i][:, 512 * j:512 * j + 512], po[j][:, :],
                      waits=((s_mm, mmv), (ldx[i], ldxv)),
                      inc=(s_dve if j == 1 else None))
            em.dma("sync", xov_t[t], xt[i][:, :], s_stp[i],
                   waits=((s_dve, s_dve.n),))
        em.wait("gpsimd", s_stp[0], s_stp[0].n)
        em.wait("gpsimd", s_stp[1], s_stp[1].n)

        if _own:
            with nc.Block() as block:
                em.finalize(block)
    return nc


def step_l1ffn(x_parts, ins):
    common = {
        "mu": np.ascontiguousarray(ins["L1_ffn_mu"].reshape(D, 256)),
        "scale": ins["L1_ffn_scale"].reshape(1, 1),
        "gamma": ins["L1_ffn_ln_gamma"].reshape(1, D),
        "beta": ins["L1_ffn_ln_beta"].reshape(1, D),
        "ident": IDENT_BF,
    }
    res = _launch("l1ffn_part", build_l1ffn_part,
                  [dict(common, x=x_parts[c]) for c in range(NCORES)])
    nums = [np.asarray(res[c]["num"]) for c in range(NCORES)]
    dens = [np.asarray(res[c]["den"]) for c in range(NCORES)]
    ws = [np.asarray(res[c]["w"]) for c in range(NCORES)]

    sel = np.zeros((16, 256), dtype=BF)
    for e in range(16):
        sel[e, 16 * e:16 * e + 16] = 1
    b1c = np.ascontiguousarray(
        ins["L1_ffn_b1"].reshape(16, 2, 128).transpose(2, 0, 1).reshape(128, 32))
    w1l = np.ascontiguousarray(
        ins["L1_ffn_w1"].reshape(16, 8, 128, 256).transpose(2, 1, 0, 3)
        .reshape(128, 8 * 16 * 256))
    w2l = np.ascontiguousarray(
        ins["L1_ffn_w2"].reshape(16, 2, 128, 1024).transpose(2, 1, 0, 3)
        .reshape(128, 2 * 16 * 1024))
    maps = []
    for b in range(2):
        nstack = np.ascontiguousarray(np.stack(nums[4 * b:4 * b + 4]))
        dstack = np.ascontiguousarray(np.stack(dens[4 * b:4 * b + 4]))
        for ci in range(4):
            c = 4 * b + ci
            maps.append({"x": x_parts[c], "nums": nstack, "dens": dstack,
                         "w": ws[c], "w1": w1l, "b1c": b1c,
                         "w2": w2l, "b2": ins["L1_ffn_b2"],
                         "sel": sel, "ident": IDENT_BF})
    res = _launch("l1ffn_comb", build_l1ffn_comb, maps)
    return [np.asarray(res[c]["xo"]) for c in range(NCORES)]


def build_l3band(fx=None):
    H = 16
    nc, _em0, pre, _own = _fx_open(fx)
    x_d = _fio(fx, nc, "x", [TC, D], F32, "ExternalInput")
    qt_d = _fio(fx, nc, "qt", [D, TC], BF16, "ExternalInput")
    ktx_d = _fio(fx, nc, "ktx", [D, TC + 256], BF16, "ExternalInput")
    vbx_d = _fio(fx, nc, "vbx", [128, 18 * D], BF16, "ExternalInput")
    mk_d = _fio(fx, nc, "masks", [NT, 128, 384], F32, "ExternalInput")
    woT_d = _fio(fx, nc, "woT", [D, D], F32, "ExternalInput")
    id_d = _fio(fx, nc, "ident", [128, 128], BF16, "ExternalInput")
    xo_d = _fio(fx, nc, "xo", [TC, D], F32, "ExternalOutput")

    with ExitStack() as stack:
        em = _em0 if _em0 is not None else Em(nc, stack)
        sb = lambda nm, sh, dt=F32: stack.enter_context(nc.sbuf_tensor(pre + nm, sh, dt))
        ps = lambda nm, sh, dt=F32: stack.enter_context(nc.psum_tensor(pre + nm, sh, dt))

        qt_sb = sb("qt_sb", [64, 16 * TC], BF16)
        ktx = sb("ktx_sb", [64, 16 * (TC + 256)], BF16)
        vbx = sb("vbx_sb", [128, 18 * D], BF16)
        wo = sb("wo", [128, 8 * D], BF16)
        mk = [sb(f"mk{i}", [128, 384]) for i in range(2)]
        sm = sb("sm", [128, 384])
        pb = sb("pb", [128, 384], BF16)
        pTb = sb("pTb", [128, 3 * 128], BF16)
        ctx = sb("ctx", [128, D], BF16)
        ctxT = sb("ctxT", [128, D], BF16)
        xt = [sb(f"xt{i}", [128, D]) for i in range(2)]
        cls = sb("cls", [128, 8])
        id_sb = sb("id_sb", [128, 128], BF16)

        pst = ps("pst", [128, 1024], BF16)
        pS = ps("pS", [128, 512])
        pc = [ps(f"pc{i}", [128, 512]) for i in range(2)]
        pw = [ps(f"pw{i}", [128, 512]) for i in range(2)]

        ldg = em.sem("ldg")
        ldx = [em.sem("ldx0"), em.sem("ldx1")]
        ldm = [em.sem("ldm0"), em.sem("ldm1")]
        s_act = em.sem("act")
        s_dve = em.sem("dve")
        s_pe = em.sem("pe")
        s_ev = em.sem("ev")
        s_mm = em.sem("mm")
        s_stp = [em.sem("st0"), em.sem("st1")]
        s_upd = em.sem("upd")

        qtv = qt_sb[:, :].rearrange("p (h t) -> p h t", h=16)
        ktv = ktx[:, :].rearrange("p (h t) -> p h t", h=16)
        vbv = vbx[:, :].rearrange("p (t n) -> p t n", t=18)
        wov = wo[:, :].rearrange("p (c n) -> p c n", c=8)
        xv_t = x_d.rearrange("(t p) d -> t p d", p=128)
        xov_t = xo_d.rearrange("(t p) d -> t p d", p=128)

        em.dma("gpsimd", qtv, qt_d.rearrange("(h p) t -> p h t", p=64), ldg)
        em.dma("gpsimd", ktv, ktx_d.rearrange("(h p) t -> p h t", p=64), ldg)
        em.dma("gpsimd", vbx[:, :], vbx_d, ldg)
        em.dma("gpsimd", wov, woT_d.rearrange("(c p) n -> p c n", p=128), ldg)
        em.dma("sync", id_sb[:, :], id_d, ld2 := em.sem("ld2"))
        ld2v = ld2.n
        ldgv = ldg.n
        em.wait("tensor", ldg, ldgv)
        em.wait("tensor", ld2, ld2v)

        for t in range(NT):
            i = t % 2
            em.dma("sync", xt[i][:, :], xv_t[t], ldx[i],
                   waits=((s_upd, s_upd.n), (s_stp[i], s_stp[i].n)))
            ldxv = ldx[i].n
            em.dma("sync", mk[i][:, :], mk_d[t], ldm[i],
                   waits=((s_dve, s_dve.n),))
            ldmv = ldm[i].n
            for h in range(H):
                em.op("tensor", "matmul", pS[:, 0:384],
                      qtv[:, h, 128 * t:128 * t + 128],
                      ktv[:, h, 128 * t:128 * t + 384],
                      start=True, stop=True,
                      waits=((s_dve, s_dve.n), (ldg, ldgv)), inc=s_mm)
                em.op("vector", "tensor_add", sm[:, :], pS[:, 0:384],
                      mk[i][:, :], waits=((s_mm, s_mm.n), (ldm[i], ldmv)))
                em.op("vector", "reduce_max", cls[:, 0:1], sm[:, :],
                      axis=AX.X)
                em.op("vector", "tensor_scalar_mul", cls[:, 1:2], cls[:, 0:1],
                      -1.0, inc=s_dve)
                em.op("scalar", "activation", pb[:, :], sm[:, :], AF.Exp,
                      bias=cls[:, 1:2], accum_out=cls[:, 2:3],
                      waits=((s_dve, s_dve.n),), inc=s_act)
                em.op("vector", "reciprocal", cls[:, 3:4], cls[:, 2:3],
                      waits=(s_act,), inc=s_dve)
                for c in range(3):
                    em.op("tensor", "transpose", pst[:, 0:128],
                          pb[:, 128 * c:128 * c + 128], id_sb[:, :],
                          waits=((s_dve, s_dve.n), (s_ev, s_ev.n)), inc=s_pe)
                    em.op("vector", "tensor_copy",
                          pTb[:, 128 * c:128 * c + 128], pst[:, 0:128],
                          waits=(s_pe,), inc=s_ev)
                cb = pc[h % 2]
                for c in range(3):
                    em.op("tensor", "matmul", cb[:, 0:64],
                          pTb[:, 128 * c:128 * c + 128],
                          vbv[:, t + c, 64 * h:64 * h + 64],
                          start=(c == 0), stop=(c == 2),
                          waits=((s_ev, s_ev.n), (s_dve, s_dve.n))
                                if c == 0 else (),
                          inc=(s_mm if c == 2 else None))
                em.op("vector", "tensor_scalar_mul", ctx[:, 64 * h:64 * h + 64],
                      cb[:, 0:64], cls[:, 3:4], waits=(s_mm,), inc=s_dve)
            for c in range(8):
                em.op("tensor", "transpose", pst[:, 0:128],
                      ctx[:, 128 * c:128 * c + 128], id_sb[:, :],
                      waits=((s_dve, s_dve.n), (s_ev, s_ev.n)), inc=s_pe)
                em.op("vector", "tensor_copy", ctxT[:, 128 * c:128 * c + 128],
                      pst[:, 0:128], waits=(s_pe,), inc=s_ev)
            cT_ev = s_ev.n
            for j in range(2):
                for c in range(8):
                    em.op("tensor", "matmul", pw[j][:, :],
                          ctxT[:, 128 * c:128 * c + 128],
                          wov[:, c, 512 * j:512 * j + 512],
                          start=(c == 0), stop=(c == 7),
                          waits=((s_ev, cT_ev), (s_upd, s_upd.n))
                                if c == 0 else (),
                          inc=(s_mm if c == 7 else None))
            mmv = s_mm.n
            for j in range(2):
                em.op("vector", "tensor_add", xt[i][:, 512 * j:512 * j + 512],
                      xt[i][:, 512 * j:512 * j + 512], pw[j][:, :],
                      waits=((s_mm, mmv), (ldx[i], ldxv)),
                      inc=(s_upd if j == 1 else None))
            em.dma("sync", xov_t[t], xt[i][:, :], s_stp[i],
                   waits=((s_upd, s_upd.n),))
        em.wait("gpsimd", s_stp[0], s_stp[0].n)
        em.wait("gpsimd", s_stp[1], s_stp[1].n)

        if _own:
            with nc.Block() as block:
                em.finalize(block)
    return nc


def _attn_proj_maps(x_parts, wq, bq, wk, bk, wv, bv, scaleq=True):
    common = {"wqT": np.ascontiguousarray(wq.T),
              "wkT": np.ascontiguousarray(wk.T),
              "wvT": np.ascontiguousarray(wv.T),
              "bqc": np.ascontiguousarray(bq.reshape(8, 128).T),
              "bkc": np.ascontiguousarray(bk.reshape(8, 128).T),
              "bv": bv.reshape(1, D), "ident": IDENT_BF}
    return [dict(common, x=x_parts[c]) for c in range(NCORES)]


def _l3_masks(core):
    lp, rp = 8, 7
    t0 = (core % 4) * TC
    masks = np.full((NT, 128, 384), -1e30, dtype=np.float32)
    for t in range(NT):
        for qi in range(128):
            gq = t0 + 128 * t + qi
            for w in range(16):
                gk = gq - lp + w
                if 0 <= gk < S:
                    rel = gk - (t0 + 128 * t - 128)
                    if 0 <= rel < 384:
                        masks[t, qi, rel] = 0.0
    return masks


_L3_MASKS = None


def step_l3attn(x_parts, ins):
    maps = _attn_proj_maps(x_parts, ins["L3_attn_Wq"],
                           np.zeros(D, np.float32), ins["L3_attn_Wk"],
                           np.zeros(D, np.float32), ins["L3_attn_Wv"],
                           np.zeros(D, np.float32))
    res = _launch("l2attn_proj", build_l2attn_proj, maps)
    qts = [np.asarray(res[c]["qt"]) for c in range(NCORES)]
    kts = [np.asarray(res[c]["kt"]) for c in range(NCORES)]
    vs = [np.asarray(res[c]["v"]) for c in range(NCORES)]

    global _L3_MASKS
    if _L3_MASKS is None:
        _L3_MASKS = [_l3_masks(c) for c in range(NCORES)]
    woT = np.ascontiguousarray(ins["L3_attn_Wo"].T)
    zt_kt = np.zeros((D, 128), dtype=BF)
    zt_v = np.zeros((128, D), dtype=BF)
    maps = []
    for c in range(NCORES):
        left_k = kts[c - 1][:, -128:] if c % 4 != 0 else zt_kt
        right_k = kts[c + 1][:, :128] if c % 4 != 3 else zt_kt
        ktx = np.ascontiguousarray(
            np.concatenate([left_k, kts[c], right_k], axis=1))
        left_v = vs[c - 1][-128:] if c % 4 != 0 else zt_v
        right_v = vs[c + 1][:128] if c % 4 != 3 else zt_v
        vb = np.concatenate([left_v, vs[c], right_v], 0)
        vbx = np.ascontiguousarray(
            vb.reshape(18, 128, D).transpose(1, 0, 2).reshape(128, 18 * D))
        maps.append({"x": x_parts[c], "qt": qts[c], "ktx": ktx, "vbx": vbx,
                     "masks": _L3_MASKS[c], "woT": woT, "ident": IDENT_BF})
    res = _launch("l3band", build_l3band, maps)
    return [np.asarray(res[c]["xo"]) for c in range(NCORES)]


def build_l1band(fx=None):
    H = 16
    nc, _em0, pre, _own = _fx_open(fx)
    qt_d = _fio(fx, nc, "qt", [D, TC], BF16, "ExternalInput")
    ktx_d = _fio(fx, nc, "ktx", [D, TC + 256], BF16, "ExternalInput")
    k0_d = _fio(fx, nc, "k0col", [64, 16], BF16, "ExternalInput")
    q0_d = _fio(fx, nc, "q0col", [64, 16], BF16, "ExternalInput")
    vbx_d = _fio(fx, nc, "vbx", [128, 18 * D], BF16, "ExternalInput")
    v0_d = _fio(fx, nc, "v0row", [1, D], BF16, "ExternalInput")
    mk_d = _fio(fx, nc, "masks", [NT, 128, 385], F32, "ExternalInput")
    b0_d = _fio(fx, nc, "bias0", [128, 16], F32, "ExternalInput")
    id_d = _fio(fx, nc, "ident", [128, 128], BF16, "ExternalInput")
    ctx_d = _fio(fx, nc, "ctxu", [TC, D], BF16, "ExternalOutput")
    den_d = _fio(fx, nc, "dens", [128, NT * 16], F32, "ExternalOutput")
    r0_d = _fio(fx, nc, "r0p", [1, 1040], F32, "ExternalOutput")
    vs_d = _fio(fx, nc, "vsum", [1, D], F32, "ExternalOutput")

    with ExitStack() as stack:
        em = _em0 if _em0 is not None else Em(nc, stack)
        sb = lambda nm, sh, dt=F32: stack.enter_context(nc.sbuf_tensor(pre + nm, sh, dt))
        ps = lambda nm, sh, dt=F32: stack.enter_context(nc.psum_tensor(pre + nm, sh, dt))

        qt_sb = sb("qt_sb", [64, 16 * TC], BF16)
        ktx = sb("ktx_sb", [64, 16 * (TC + 256)], BF16)
        vbx = sb("vbx_sb", [128, 18 * D], BF16)
        k0c = sb("k0c", [64, 16], BF16)
        q0c = sb("q0c", [64, 16], BF16)
        v0r = sb("v0r", [1, D], BF16)
        mk = [sb(f"mk{i}", [128, 385]) for i in range(2)]
        b0c = sb("b0c", [128, 16])
        sm = sb("sm", [128, 385])
        m1 = sb("m1", [128, 385], BF16)
        ebuf = sb("ebuf", [128, 385], BF16)
        pbuf = sb("pbuf", [128, 385], BF16)
        pTb = sb("pTb", [128, 3 * 128], BF16)
        pT0 = sb("pT0", [1, 128], BF16)
        ctx = sb("ctx", [128, D], BF16)
        densb = sb("densb", [128, NT * 16])
        sm0 = sb("sm0", [128, 16])
        m10 = sb("m10", [128, 16], BF16)
        p0b = sb("p0b", [128, 16], BF16)
        p0s = sb("p0s", [128, 1])
        r0sb = sb("r0sb", [1, 1040])
        vssb = sb("vssb", [1, D])
        onescol_b = sb("onescol_b", [128, 1], BF16)
        onescol_f = sb("onescol_f", [128, 1])
        kc = sb("kc", [128, 8])
        id_sb = sb("id_sb", [128, 128], BF16)

        pst = ps("pst", [128, 1024], BF16)
        pS = ps("pS", [128, 512])
        pc = [ps(f"pc{i}", [128, 512]) for i in range(2)]

        ldg = em.sem("ldg")
        ld2 = em.sem("ld2")
        ldm = [em.sem("ldm0"), em.sem("ldm1")]
        s_act = em.sem("act")
        s_dve = em.sem("dve")
        s_pe = em.sem("pe")
        s_ev = em.sem("ev")
        s_mm = em.sem("mm")
        s_st = em.sem("st")

        qtv = qt_sb[:, :].rearrange("p (h t) -> p h t", h=16)
        ktv = ktx[:, :].rearrange("p (h t) -> p h t", h=16)
        vbv = vbx[:, :].rearrange("p (t n) -> p t n", t=18)
        ctxv_d = ctx_d.rearrange("(t p) d -> t p d", p=128)

        em.op("vector", "memset", kc[:, :], 0.0)
        em.op("vector", "memset", kc[:, 1:2], -30.0)
        em.op("vector", "memset", onescol_b[:, :], 1.0)
        em.op("vector", "memset", onescol_f[:, :], 1.0, inc=s_dve)
        em.dma("gpsimd", qtv, qt_d.rearrange("(h p) t -> p h t", p=64), ldg)
        em.dma("gpsimd", ktv, ktx_d.rearrange("(h p) t -> p h t", p=64), ldg)
        em.dma("gpsimd", vbx[:, :], vbx_d, ldg)
        em.dma("sync", k0c[:, :], k0_d, ld2)
        em.dma("sync", q0c[:, :], q0_d, ld2)
        em.dma("sync", v0r[:, :], v0_d, ld2)
        em.dma("sync", b0c[:, :], b0_d, ld2)
        em.dma("sync", id_sb[:, :], id_d, ld2)
        ldgv, ld2v = ldg.n, ld2.n
        em.wait("tensor", ldg, ldgv)
        em.wait("tensor", ld2, ld2v)
        em.wait("scalar", ld2, ld2v)

        for j in range(2):
            for tt in range(NT):
                em.op("tensor", "matmul", pc[j][0:1, :], onescol_b[:, :],
                      vbv[:, 1 + tt, 512 * j:512 * j + 512],
                      start=(tt == 0), stop=(tt == NT - 1),
                      waits=((s_dve, s_dve.n),) if tt == 0 else (),
                      inc=(s_mm if tt == NT - 1 else None))
            em.op("vector", "tensor_copy", vssb[0:1, 512 * j:512 * j + 512],
                  pc[j][0:1, :], waits=(s_mm,), inc=s_dve)
        em.wait("sync", s_dve, s_dve.n)
        em.dma("sync", vs_d, vssb[:, :], s_st)

        for h in range(H):
            for c in range(NT):
                em.op("tensor", "matmul", pS[:, c:c + 1],
                      ktv[:, h, 128 + 128 * c:256 + 128 * c],
                      q0c[:, h:h + 1],
                      start=(c == 0), stop=(c == NT - 1),
                      waits=((s_dve, s_dve.n), (s_act, s_act.n))
                            if c == 0 else (),
                      inc=(s_mm if c == NT - 1 else None))
            em.op("vector", "tensor_add", sm0[:, :], pS[:, 0:16], b0c[:, :],
                  waits=(s_mm,))
            em.op("vector", "tensor_scalar", m10[:, :], sm0[:, :], 0.0, None,
                  ALU.is_ge, inc=s_dve)
            em.op("scalar", "activation", p0b[:, :], sm0[:, :], AF.Exp,
                  bias=kc[:, 1:2], waits=(s_dve,), inc=s_act)
            em.op("vector", "scalar_tensor_tensor", p0b[:, :], p0b[:, :],
                  1.0, m10[:, :], ALU.mult, ALU.mult,
                  accum_out=p0s[:, 0:1], waits=(s_act,), inc=s_dve)
            em.op("tensor", "matmul", pc[0][0:1, 0:1], p0s[:, 0:1],
                  onescol_f[:, :], start=True, stop=True,
                  waits=((s_dve, s_dve.n), (s_act, s_act.n)), inc=s_mm)
            em.op("scalar", "copy", r0sb[0:1, 1024 + h:1025 + h],
                  pc[0][0:1, 0:1], waits=(s_mm,), inc=s_act)
            for c in range(NT):
                em.op("tensor", "matmul", pc[1][0:1, 0:64],
                      p0b[:, c:c + 1], vbv[:, 1 + c, 64 * h:64 * h + 64],
                      start=(c == 0), stop=(c == NT - 1),
                      waits=((s_dve, s_dve.n), (s_act, s_act.n))
                            if c == 0 else (),
                      inc=(s_mm if c == NT - 1 else None))
            em.op("scalar", "copy", r0sb[0:1, 64 * h:64 * h + 64],
                  pc[1][0:1, 0:64], waits=(s_mm,), inc=s_act)
        em.wait("sync", s_act, s_act.n)
        em.dma("sync", r0_d, r0sb[:, :], s_st)

        for t in range(NT):
            i = t % 2
            em.dma("sync", mk[i][:, :], mk_d[t], ldm[i],
                   waits=((s_dve, s_dve.n),))
            ldmv = ldm[i].n
            for h in range(H):
                em.op("tensor", "matmul", pS[:, 0:384],
                      qtv[:, h, 128 * t:128 * t + 128],
                      ktv[:, h, 128 * t:128 * t + 384],
                      start=True, stop=False,
                      waits=((s_dve, s_dve.n), (s_act, s_act.n)))
                em.op("tensor", "matmul", pS[:, 384:385],
                      qtv[:, h, 128 * t:128 * t + 128],
                      k0c[:, h:h + 1],
                      start=False, stop=True, inc=s_mm)
                em.op("vector", "tensor_add", sm[:, :], pS[:, 0:385],
                      mk[i][:, :], waits=((s_mm, s_mm.n), (ldm[i], ldmv)))
                em.op("vector", "tensor_scalar", m1[:, :], sm[:, :], 0.0,
                      None, ALU.is_ge, inc=s_dve)
                em.op("scalar", "activation", ebuf[:, :], sm[:, :], AF.Exp,
                      bias=kc[:, 1:2], waits=(s_dve,), inc=s_act)
                em.op("vector", "scalar_tensor_tensor", pbuf[:, :],
                      ebuf[:, :], 1.0, m1[:, :], ALU.mult, ALU.mult,
                      accum_out=densb[:, 16 * t + h:16 * t + h + 1],
                      waits=(s_act,), inc=s_dve)
                for c in range(3):
                    em.op("tensor", "transpose", pst[:, 0:128],
                          pbuf[:, 128 * c:128 * c + 128], id_sb[:, :],
                          waits=((s_dve, s_dve.n), (s_ev, s_ev.n)), inc=s_pe)
                    em.op("vector", "tensor_copy",
                          pTb[:, 128 * c:128 * c + 128], pst[:, 0:128],
                          waits=(s_pe,), inc=s_ev)
                em.op("tensor", "transpose", pst[0:1, 0:128],
                      pbuf[:, 384:385], id_sb[:, :],
                      waits=((s_ev, s_ev.n),), inc=s_pe)
                em.op("vector", "tensor_copy", pT0[0:1, :], pst[0:1, 0:128],
                      waits=(s_pe,), inc=s_ev)
                pt_ev = s_ev.n
                cb = pc[h % 2]
                for c in range(3):
                    em.op("tensor", "matmul", cb[:, 0:64],
                          pTb[:, 128 * c:128 * c + 128],
                          vbv[:, t + c, 64 * h:64 * h + 64],
                          start=(c == 0), stop=False,
                          waits=((s_ev, pt_ev), (s_act, s_act.n))
                                if c == 0 else ())
                em.op("tensor", "matmul", cb[:, 0:64], pT0[0:1, :],
                      v0r[0:1, 64 * h:64 * h + 64],
                      start=False, stop=True, inc=s_mm)
                em.op("scalar", "copy", ctx[:, 64 * h:64 * h + 64],
                      cb[:, 0:64],
                      waits=((s_mm, s_mm.n), (s_st, s_st.n))
                            if h == 0 else ((s_mm, s_mm.n),),
                      inc=s_act)
            em.wait("sync", s_act, s_act.n)
            em.dma("sync", ctxv_d[t], ctx[:, :], s_st,
                   waits=((s_act, s_act.n),))
        em.wait("vector", s_st, s_st.n)
        em.dma("sync", den_d, densb[:, :], s_st2 := em.sem("st2"),
               waits=((s_dve, s_dve.n),))
        em.wait("gpsimd", s_st, s_st.n)
        em.wait("gpsimd", s_st2, s_st2.n)

        if _own:
            with nc.Block() as block:
                em.finalize(block)
    return nc


def build_l1fin(fx=None, nsum=4):
    nc, _em0, pre, _own = _fx_open(fx)
    x_d = _fio(fx, nc, "x", [TC, D], F32, "ExternalInput")
    ctx_d = _fio(fx, nc, "ctxu", [TC, D], BF16, "ExternalInput")
    den_d = _fio(fx, nc, "dens", [128, NT * 16], F32, "ExternalInput")
    r0_d = _fio(fx, nc, "r0ps", [1, nsum * 1040], F32, "ExternalInput")
    vs_d = _fio(fx, nc, "vsums", [1, nsum * D], F32, "ExternalInput")
    bo1_d = _fio(fx, nc, "blockones", [16, D], F32, "ExternalInput")
    rf_d = _fio(fx, nc, "r0flag", [1, 2], F32, "ExternalInput")
    woT_d = _fio(fx, nc, "woT", [D, D], F32, "ExternalInput")
    bo_d = _fio(fx, nc, "bo", [1, D], F32, "ExternalInput")
    id_d = _fio(fx, nc, "ident", [128, 128], BF16, "ExternalInput")
    xo_d = _fio(fx, nc, "xo", [TC, D], F32, "ExternalOutput")

    with ExitStack() as stack:
        em = _em0 if _em0 is not None else Em(nc, stack)
        sb = lambda nm, sh, dt=F32: stack.enter_context(nc.sbuf_tensor(pre + nm, sh, dt))
        ps = lambda nm, sh, dt=F32: stack.enter_context(nc.psum_tensor(pre + nm, sh, dt))

        wo = sb("wo", [128, 8 * D], BF16)
        borow = sb("borow", [1, D], BF16)
        dens = sb("dens_sb", [128, NT * 16])
        r0p = sb("r0p", [1, nsum * 1040])
        vssb = sb("vssb", [1, nsum * D])
        bo1 = sb("bo1", [16, D])
        rf = sb("rf", [1, 2])
        vmb = sb("vmb", [128, D])
        recc = sb("recc", [128, 16])
        coef = sb("coef", [128, 16])
        finv = sb("finv", [128, 16])
        ctxu = [sb(f"ctxu{i}", [128, D], BF16) for i in range(2)]
        ctxn = sb("ctxn", [128, D])
        ctxb = sb("ctxb", [128, D], BF16)
        ctxT = sb("ctxT", [128, D], BF16)
        xt = [sb(f"xt{i}", [128, D]) for i in range(2)]
        r0f = sb("r0f", [1, D])
        r0tmp = sb("r0tmp", [1, D])
        rec0 = sb("rec0", [1, 16])
        rec0T = sb("rec0T", [16, 1])
        fl0 = sb("fl0", [1, 16])
        fl0T = sb("fl0T", [16, 1])
        kc = sb("kc", [128, 8])
        ones_f = sb("ones_f", [1, 128])
        ones_b = sb("ones_b", [1, 128], BF16)
        id_sb = sb("id_sb", [128, 128], BF16)

        pst = ps("pst", [128, 1024], BF16)
        pbc = ps("pbc", [128, 512])
        pw = [ps(f"pw{i}", [128, 512]) for i in range(2)]

        ld = em.sem("ld")
        ldg = em.sem("ldg")
        ldx = [em.sem("ldx0"), em.sem("ldx1")]
        ldc = [em.sem("ldc0"), em.sem("ldc1")]
        s_act = em.sem("act")
        s_dve = em.sem("dve")
        s_pe = em.sem("pe")
        s_ev = em.sem("ev")
        s_mm = em.sem("mm")
        s_stp = [em.sem("st0"), em.sem("st1")]
        s_upd = em.sem("upd")

        wov = wo[:, :].rearrange("p (c n) -> p c n", c=8)
        xv_t = x_d.rearrange("(t p) d -> t p d", p=128)
        cv_t = ctx_d.rearrange("(t p) d -> t p d", p=128)
        xov_t = xo_d.rearrange("(t p) d -> t p d", p=128)

        em.op("vector", "memset", kc[:, :], 0.0)
        em.op("vector", "memset", kc[:, 1:2], 1e-30)
        em.op("vector", "memset", ones_f[:, :], 1.0)
        em.op("vector", "memset", ones_b[:, :], 1.0, inc=s_dve)
        em.dma("sync", dens[:, :], den_d, ld)
        em.dma("sync", r0p[:, :], r0_d, ld)
        em.dma("sync", vssb[:, :], vs_d, ld)
        em.dma("sync", bo1[:, :], bo1_d, ld)
        em.dma("sync", rf[:, :], rf_d, ld)
        em.dma("sync", id_sb[:, :], id_d, ld)
        em.dma("gpsimd", wov, woT_d.rearrange("(c p) n -> p c n", p=128), ldg)
        em.dma("gpsimd", borow[:, :], bo_d, ldg)
        ldv, ldgv = ld.n, ldg.n
        em.wait("tensor", ld, ldv)
        em.wait("tensor", ldg, ldgv)
        em.wait("vector", ld, ldv)

        for i in range(1, nsum):
            em.op("vector", "tensor_add", vssb[0:1, 0:D], vssb[0:1, 0:D],
                  vssb[0:1, D * i:D * i + D])
            em.op("vector", "tensor_add", r0p[0:1, 0:1040], r0p[0:1, 0:1040],
                  r0p[0:1, 1040 * i:1040 * i + 1040])
        em.op("vector", "tensor_scalar_mul", vssb[0:1, 0:D], vssb[0:1, 0:D],
              1.0 / S, inc=s_dve)
        for j in range(2):
            em.op("tensor", "matmul", pbc[:, :], ones_f[0:1, :],
                  vssb[0:1, 512 * j:512 * j + 512], start=True, stop=True,
                  waits=((s_dve, s_dve.n), (s_ev, s_ev.n)), inc=s_mm)
            em.op("vector", "tensor_copy", vmb[:, 512 * j:512 * j + 512],
                  pbc[:, :], waits=(s_mm,), inc=s_ev)
        em.op("vector", "tensor_scalar_max", rec0[0:1, :],
              r0p[0:1, 1024:1040], 1e-30, waits=((s_dve, s_dve.n),))
        em.op("vector", "reciprocal", rec0[0:1, :], rec0[0:1, :])
        em.op("vector", "tensor_scalar", fl0[0:1, :], r0p[0:1, 1024:1040],
              0.0, None, ALU.is_gt, inc=s_dve)
        rec0_dram = nc.dram_tensor(pre + "rec0_scr", [1, 16], F32)
        fl0_dram = nc.dram_tensor(pre + "fl0_scr", [1, 16], F32)
        em.dma("sync", rec0_dram.ap(), rec0[0:1, :], sc1 := em.sem("sc1"),
               waits=((s_dve, s_dve.n),))
        em.dma("sync", fl0_dram.ap(), fl0[0:1, :], sc1)
        em.dma("sync", rec0T[:, :],
               rec0_dram.ap().rearrange("o (p u) -> (o p) u", u=1),
               sc2 := em.sem("sc2"), waits=((sc1, sc1.n),))
        em.dma("sync", fl0T[:, :],
               fl0_dram.ap().rearrange("o (p u) -> (o p) u", u=1), sc2)
        for j in range(2):
            em.op("tensor", "matmul", pbc[0:1, 0:512],
                  rec0T[:, :], bo1[0:16, 512 * j:512 * j + 512],
                  start=True, stop=True,
                  waits=((sc2, sc2.n), (s_ev, s_ev.n)), inc=s_mm)
            em.op("vector", "tensor_copy", r0tmp[0:1, 512 * j:512 * j + 512],
                  pbc[0:1, 0:512], waits=(s_mm,), inc=s_ev)
            em.op("tensor", "matmul", pbc[0:1, 0:512],
                  fl0T[:, :], bo1[0:16, 512 * j:512 * j + 512],
                  start=True, stop=True, waits=((s_ev, s_ev.n),), inc=s_mm)
            em.op("vector", "tensor_copy", r0f[0:1, 512 * j:512 * j + 512],
                  pbc[0:1, 0:512], waits=(s_mm,), inc=s_ev)
        em.op("vector", "tensor_mul", r0tmp[0:1, :], r0tmp[0:1, :],
              r0p[0:1, 0:1024], waits=((s_ev, s_ev.n),))
        em.op("vector", "tensor_mul", r0tmp[0:1, :], r0tmp[0:1, :],
              r0f[0:1, :])
        em.op("vector", "tensor_scalar", r0f[0:1, :], r0f[0:1, :], -1.0, 1.0,
              ALU.mult, ALU.add)
        em.op("vector", "tensor_mul", r0f[0:1, :], r0f[0:1, :], vmb[0:1, :])
        em.op("vector", "tensor_add", r0tmp[0:1, :], r0tmp[0:1, :],
              r0f[0:1, :], inc=s_dve)
        r0_dve = s_dve.n

        for t in range(NT):
            i = t % 2
            em.dma("sync", xt[i][:, :], xv_t[t], ldx[i],
                   waits=((s_upd, s_upd.n), (s_stp[i], s_stp[i].n)))
            ldxv = ldx[i].n
            em.dma("sync", ctxu[i][:, :], cv_t[t], ldc[i],
                   waits=((s_dve, s_dve.n),))
            ldcv = ldc[i].n
            em.op("vector", "tensor_scalar_max", recc[:, :],
                  dens[:, 16 * t:16 * t + 16], kc[:, 1:2],
                  waits=((ldc[i], ldcv),))
            em.op("vector", "reciprocal", recc[:, :], recc[:, :])
            em.op("vector", "tensor_scalar", coef[:, :],
                  dens[:, 16 * t:16 * t + 16], 0.0, None, ALU.is_gt)
            em.op("vector", "tensor_scalar", finv[:, :], coef[:, :], -1.0,
                  1.0, ALU.mult, ALU.add)
            em.op("vector", "tensor_mul", coef[:, :], coef[:, :], recc[:, :])
            for h in range(16):
                em.op("vector", "tensor_scalar_mul",
                      ctxn[:, 64 * h:64 * h + 64],
                      ctxu[i][:, 64 * h:64 * h + 64], coef[:, h:h + 1])
                em.op("vector", "scalar_tensor_tensor",
                      ctxn[:, 64 * h:64 * h + 64], vmb[:, 64 * h:64 * h + 64],
                      finv[:, h:h + 1], ctxn[:, 64 * h:64 * h + 64],
                      ALU.mult, ALU.add)
            if t == 0:
                em.op("vector", "tensor_scalar_mul", r0f[0:1, :],
                      r0tmp[0:1, :], rf[0:1, 0:1],
                      waits=((s_dve, r0_dve),))
                em.op("vector", "tensor_scalar", ctxn[0:1, :], ctxn[0:1, :],
                      rf[0:1, 1:2], None, ALU.mult)
                em.op("vector", "tensor_add", ctxn[0:1, :], ctxn[0:1, :],
                      r0f[0:1, :])
            em.op("vector", "tensor_copy", ctxb[:, :], ctxn[:, :],
                  inc=s_dve)
            for c in range(8):
                em.op("tensor", "transpose", pst[:, 0:128],
                      ctxb[:, 128 * c:128 * c + 128], id_sb[:, :],
                      waits=((s_dve, s_dve.n), (s_ev, s_ev.n)), inc=s_pe)
                em.op("vector", "tensor_copy", ctxT[:, 128 * c:128 * c + 128],
                      pst[:, 0:128], waits=(s_pe,), inc=s_ev)
            cT_ev = s_ev.n
            for j in range(2):
                for c in range(8):
                    em.op("tensor", "matmul", pw[j][:, :],
                          ctxT[:, 128 * c:128 * c + 128],
                          wov[:, c, 512 * j:512 * j + 512],
                          start=(c == 0), stop=False,
                          waits=((s_ev, cT_ev), (s_upd, s_upd.n))
                                if c == 0 else ())
                em.op("tensor", "matmul", pw[j][:, :],
                      ones_b[0:1, :], borow[0:1, 512 * j:512 * j + 512],
                      start=False, stop=True, inc=s_mm)
            mmv = s_mm.n
            for j in range(2):
                em.op("vector", "tensor_add", xt[i][:, 512 * j:512 * j + 512],
                      xt[i][:, 512 * j:512 * j + 512], pw[j][:, :],
                      waits=((s_mm, mmv), (ldx[i], ldxv)),
                      inc=(s_upd if j == 1 else None))
            em.dma("sync", xov_t[t], xt[i][:, :], s_stp[i],
                   waits=((s_upd, s_upd.n),))
        em.wait("gpsimd", s_stp[0], s_stp[0].n)
        em.wait("gpsimd", s_stp[1], s_stp[1].n)

        if _own:
            with nc.Block() as block:
                em.finalize(block)
    return nc


POS_NEG = -1000000000.0


def _l1_pos_bias(d):
    return (np.exp(-0.5 * (d / 2.0) ** 2)
            + np.cos(2.0 * np.pi * d / 401.0)).astype(np.float32)


def _l1_masks(core):
    t0 = (core % 4) * TC
    masks = np.full((NT, 128, 385), POS_NEG, dtype=np.float32)
    d = np.arange(-2, 3)
    pb = _l1_pos_bias(d.astype(np.float64))
    for t in range(NT):
        for qi in range(128):
            gq = t0 + 128 * t + qi
            for w in range(5):
                gk = gq + d[w]
                if 0 <= gk < S:
                    rel = gk - (t0 + 128 * t - 128)
                    if 0 <= rel < 384:
                        masks[t, qi, rel] = pb[w]
            if gq > 2:
                masks[t, qi, 384] = _l1_pos_bias(np.float64(gq))
    return masks


_L1_MASKS = None


def step_l1attn(x_parts, ins):
    maps = _attn_proj_maps(x_parts, ins["L1_attn_wq"], ins["L1_attn_bq"],
                           ins["L1_attn_wk"], ins["L1_attn_bk"],
                           ins["L1_attn_wv"], ins["L1_attn_bv"])
    res = _launch("l2attn_proj", build_l2attn_proj, maps)
    qts = [np.asarray(res[c]["qt"]) for c in range(NCORES)]
    kts = [np.asarray(res[c]["kt"]) for c in range(NCORES)]
    vs = [np.asarray(res[c]["v"]) for c in range(NCORES)]

    global _L1_MASKS
    if _L1_MASKS is None:
        _L1_MASKS = [_l1_masks(c) for c in range(NCORES)]
    zt_kt = np.zeros((D, 128), dtype=BF)
    zt_v = np.zeros((128, D), dtype=BF)
    bias0s = []
    for c in range(NCORES):
        t0 = (c % 4) * TC
        j = (t0 + np.arange(TC)).astype(np.float64)
        b0 = _l1_pos_bias(j).reshape(NT, 128).T
        bias0s.append(np.ascontiguousarray(b0))
    maps = []
    for c in range(NCORES):
        b = c // 4
        left_k = kts[c - 1][:, -128:] if c % 4 != 0 else zt_kt
        right_k = kts[c + 1][:, :128] if c % 4 != 3 else zt_kt
        ktxa = np.ascontiguousarray(
            np.concatenate([left_k, kts[c], right_k], axis=1))
        left_v = vs[c - 1][-128:] if c % 4 != 0 else zt_v
        right_v = vs[c + 1][:128] if c % 4 != 3 else zt_v
        vb = np.concatenate([left_v, vs[c], right_v], 0)
        vbx = np.ascontiguousarray(
            vb.reshape(18, 128, D).transpose(1, 0, 2).reshape(128, 18 * D))
        k0col = np.ascontiguousarray(kts[4 * b][:, 0].reshape(16, 64).T)
        q0col = np.ascontiguousarray(qts[4 * b][:, 0].reshape(16, 64).T)
        v0row = np.ascontiguousarray(vs[4 * b][0]).reshape(1, D)
        maps.append({"qt": qts[c], "ktx": ktxa, "k0col": k0col,
                     "q0col": q0col, "vbx": vbx, "v0row": v0row,
                     "masks": _L1_MASKS[c], "bias0": bias0s[c],
                     "ident": IDENT_BF})
    res = _launch("l1band", build_l1band, maps)
    ctxus = [np.asarray(res[c]["ctxu"]) for c in range(NCORES)]
    denss = [np.asarray(res[c]["dens"]) for c in range(NCORES)]
    r0ps = [np.asarray(res[c]["r0p"]) for c in range(NCORES)]
    vsums = [np.asarray(res[c]["vsum"]) for c in range(NCORES)]

    bo1 = np.zeros((16, D), np.float32)
    for h in range(16):
        bo1[h, 64 * h:64 * h + 64] = 1
    woT = np.ascontiguousarray(ins["L1_attn_wo"].T)
    maps = []
    for c in range(NCORES):
        b = c // 4
        r0stack = np.concatenate([r0ps[4 * b + k] for k in range(4)],
                                 axis=1)
        vsstack = np.concatenate([vsums[4 * b + k] for k in range(4)],
                                 axis=1)
        rf = np.array([[1.0, 0.0]] if c % 4 == 0 else [[0.0, 1.0]],
                      np.float32)
        maps.append({"x": x_parts[c], "ctxu": ctxus[c], "dens": denss[c],
                     "r0ps": r0stack, "vsums": vsstack, "blockones": bo1,
                     "r0flag": rf, "woT": woT,
                     "bo": ins["L1_attn_bo"].reshape(1, D),
                     "ident": IDENT_BF})
    res = _launch("l1fin", build_l1fin, maps)
    return [np.asarray(res[c]["xo"]) for c in range(NCORES)]


def _shard_tokens(x):
    xf = np.ascontiguousarray(np.asarray(x).reshape(T, D))
    return [xf[c * TC:(c + 1) * TC] for c in range(NCORES)]


def _gather_tokens(parts):
    return np.concatenate(parts, 0).reshape(B, S, D)


def step_ffn0(x_parts, mh):
    res = _launch("ffn0", build_ffn0,
                  [{"x": x_parts[c], "mh": mh, "ident": IDENT_BF}
                   for c in range(NCORES)])
    return [np.asarray(res[c]["xo"]) for c in range(NCORES)]


def make_mh(w1, w2):
    w1 = np.asarray(w1)
    w2 = np.asarray(w2)
    res = _launch("mbuild", build_mbuild,
                  [{"w1": np.ascontiguousarray(w1[c]),
                    "w2": np.ascontiguousarray(w2[c]),
                    "ident": IDENT_BF} for c in range(NCORES)])
    mall = np.stack([np.asarray(res[c]["m"]) for c in range(NCORES)], 0)
    res2 = _launch("msum", build_msum, [{"mall": mall}] * NCORES)
    return np.asarray(res2[0]["mh"])


G4 = [[0, 1, 2, 3], [4, 5, 6, 7]]
G8 = [list(range(8))]
GP_EVEN = [[0, 1], [2, 3], [4, 5], [6, 7]]
GP_ODD = [[1, 2], [3, 4], [5, 6], [0, 7]]


class _Done(Exception):
    pass


_PHASE_CAP = 999


def build_fused():
    fx = _FX()
    nc, em = fx.nc, fx.em
    _p = [0]
    _ccn = [0]
    cap = _PHASE_CAP

    def phase(builder, alias, **kw):
        if _p[0] + 1 > cap:
            raise _Done
        _p[0] += 1
        fx.prefix = f"p{_p[0]}_"
        fx.alias = alias
        builder(fx=fx, **kw)
        em.barrier()

    def cc(kind, op, groups, src_ap, dst_ap):
        if _p[0] >= cap:
            raise _Done
        _ccn[0] += 1
        s = em.sem(f"cc{_ccn[0]}")
        em.op("gpsimd", "collective_compute", kind, op,
              replica_groups=groups, ins=[src_ap], outs=[dst_ap],
              inc=s, incv=1)
        em.wait("gpsimd", s, 1)

    def it(name, shape, dtype):
        if name not in fx.io:
            fx.io[name] = nc.dram_tensor(name, shape, dtype).ap()
        return fx.io[name]

    fx.io["xA"] = nc.dram_tensor("xA", [TC, D], F32,
                                 kind="ExternalInput").ap()
    xB = it("xB", [TC, D], F32)
    out_d = nc.dram_tensor("out", [TC, D], F32, kind="ExternalOutput").ap()
    fx.io["out"] = out_d

    try:
        _run_fused_body(fx, nc, em, _p, cap, phase, cc, it, xB)
    except _Done:
        pass
    return fx.close()


def _run_fused_body(fx, nc, em, _p, cap, phase, cc, it, xB):
    m_part = it("m_part", [D, D], F32)
    m_sum = it("m_sum", [D, D], F32)
    mh_d = it("mh_d", [D, D], BF16)
    phase(build_mbuild, {"w1": "mb_w1", "w2": "mb_w2", "m": "m_part",
                         "ident": "identb"})
    cc("AllReduce", ALU.add, G8, m_part, m_sum)
    em.barrier()
    _p[0] += 1
    fx.prefix = f"p{_p[0]}_"
    with ExitStack() as stack:
        msb = stack.enter_context(nc.sbuf_tensor(fx.prefix + "msb",
                                                 [128, 8 * D], F32))
        mbf = stack.enter_context(nc.sbuf_tensor(fx.prefix + "mbf",
                                                 [128, 8 * D], BF16))
        s_ld = em.sem("ld")
        s_dve = em.sem("dve")
        s_st = em.sem("st")
        em.dma("sync", msb[:, :].rearrange("p (c n) -> p c n", c=8),
               m_sum.rearrange("(c p) n -> p c n", p=128), s_ld)
        em.op("vector", "tensor_scalar_mul", mbf[:, :], msb[:, :], 0.5,
              waits=((s_ld, s_ld.n),), inc=s_dve)
        em.dma("sync", mh_d.rearrange("(c p) n -> p c n", p=128),
               mbf[:, :].rearrange("p (c n) -> p c n", c=8), s_st,
               waits=((s_dve, s_dve.n),))
    em.barrier()

    def ffn0(xin, xout):
        phase(build_ffn0, {"x": xin, "mh": "mh_d", "xo": xout,
                           "ident": "identb"})

    ffn0("xA", "xB")
    it("l0_qt", [D, TC], BF16)
    it("l0_kt", [D, TC], BF16)
    it("l0_v", [TC, 512], BF16)
    it("l0_qvs", [128, NT * 16], F32)
    it("l0_ktag", [4 * D, TC], BF16)
    it("l0_vag", [S, 512], BF16)
    it("l0_om", [TC, D], BF16)
    phase(build_l0attn_proj,
          {"x": "xB", "wqT": "l0p_wqT", "wkT": "l0p_wkT", "wvT": "l0p_wvT",
           "wqvT": "l0p_wqvT", "bqc": "l0p_bqc", "bkc": "l0p_bkc",
           "bv": "l0p_bv", "bqv": "l0p_bqv", "ws1t": "l0p_ws1t",
           "bs": "l0p_bs", "g1": "l0p_g1", "b1": "l0p_b1",
           "ident": "identb", "qt": "l0_qt", "kt": "l0_kt", "v": "l0_v",
           "qvs": "l0_qvs"})
    cc("AllGather", ALU.bypass, G4, fx.io["l0_kt"], fx.io["l0_ktag"])
    cc("AllGather", ALU.bypass, G4, fx.io["l0_v"], fx.io["l0_vag"])
    em.barrier()
    phase(build_l0attn_flash,
          {"qt": "l0_qt", "kt_ag": "l0_ktag", "v_ag": "l0_vag",
           "qvs": "l0_qvs", "ws2t": "l0p_ws2t", "om": "l0_om"})
    phase(build_l0attn_post,
          {"x": "xB", "om": "l0_om", "wfT": "l0q_wfT", "bf": "l0q_bf",
           "g2": "l0q_g2", "b2": "l0q_b2", "wr1T": "l0q_wr1T",
           "br1c": "l0q_br1c", "wr2T": "l0q_wr2T", "br2": "l0q_br2",
           "ident": "identb", "xo": "xA"})
    ffn0("xA", "xB")

    def l1ffn(xin, xout, tag):
        it(f"{tag}_num", [256, D], F32)
        it(f"{tag}_den", [128, 2], F32)
        it(f"{tag}_nums", [1, 256, D], F32)
        it(f"{tag}_dens", [1, 128, 2], F32)
        it(f"{tag}_w", [TC, 256], BF16)
        phase(build_l1ffn_part,
              {"x": xin, "mu": "l1f_mu", "scale": "l1f_scale",
               "gamma": "l1f_gamma", "beta": "l1f_beta", "ident": "identb",
               "num": f"{tag}_num", "den": f"{tag}_den", "w": f"{tag}_w"})
        cc("AllReduce", ALU.add, G4, fx.io[f"{tag}_num"],
           fx.io[f"{tag}_nums"].rearrange("o e d -> (o e) d"))
        cc("AllReduce", ALU.add, G4, fx.io[f"{tag}_den"],
           fx.io[f"{tag}_dens"].rearrange("o e d -> (o e) d"))
        em.barrier()
        phase(build_l1ffn_comb,
              {"x": xin, "nums": f"{tag}_nums", "dens": f"{tag}_dens",
               "w": f"{tag}_w", "w1": "l1f_w1l", "b1c": "l1f_b1c",
               "w2": "l1f_w2l", "b2": "l1f_b2", "sel": "l1f_sel",
               "ident": "identb", "xo": xout}, nsum=1)

    l1ffn("xB", "xA", "p1a")

    def attn_proj(xin, tag, wpfx, hm=False):
        it(f"{tag}_qt", [D, TC], BF16)
        it(f"{tag}_kt", [D, TC], BF16)
        it(f"{tag}_v", [128, NT * D] if hm else [TC, D], BF16)
        phase((lambda fx=None: build_l2attn_proj(fx=fx, v_hm=hm))
              if hm else build_l2attn_proj,
              {"x": xin, "wqT": f"{wpfx}_wqT", "wkT": f"{wpfx}_wkT",
               "wvT": f"{wpfx}_wvT", "bqc": f"{wpfx}_bqc",
               "bkc": f"{wpfx}_bkc", "bv": f"{wpfx}_bv", "ident": "identb",
               "qt": f"{tag}_qt", "kt": f"{tag}_kt", "v": f"{tag}_v"})

    def band_assembly(tag, with_k0):
        kt_i = fx.io[f"{tag}_kt"]
        v_i = fx.io[f"{tag}_v"]
        ktx = it(f"{tag}_ktx", [D, TC + 256], BF16)
        vbx = it(f"{tag}_vbx", [128, 18 * D], BF16)
        ktp = it(f"{tag}_ktp", [D, 256], BF16)
        vp = it(f"{tag}_vp", [256, D], BF16)
        ktp_ag = it(f"{tag}_ktpag", [4 * D, 256], BF16)
        vp_ag = it(f"{tag}_vpag", [1024, D], BF16)
        s_cp = em.sem(tag + "_pk")
        em.dma("sync", ktp[:, 0:128], kt_i[:, 0:128], s_cp)
        em.dma("sync", ktp[:, 128:256], kt_i[:, TC - 128:TC], s_cp)
        em.dma("sync", vp[0:128, :], v_i[0:128, :], s_cp)
        em.dma("sync", vp[128:256, :], v_i[TC - 128:TC, :], s_cp)
        em.dma("sync", ktx[:, 128:128 + TC], kt_i, s_cp)
        em.dma("sync",
               vbx[:, :].rearrange("p (t n) -> p t n", t=18)[:, 1:17, :],
               v_i.rearrange("(t p) n -> p t n", p=128), s_cp)
        if with_k0:
            pk4 = it(f"{tag}_pk4", [64, 48], BF16)
            pk4_ag = it(f"{tag}_pk4ag", [256, 48], BF16)
            em.op("sync", "dma_start", out=pk4[:, 0:16],
                  in_=kt_i[:, 0:1].rearrange("(c p) o -> p (c o)", p=64),
                  allow_slow_non_contiguous=True, inc=s_cp, incv=16)
            em.op("sync", "dma_start", out=pk4[:, 16:32],
                  in_=fx.io[f"{tag}_qt"][:, 0:1].rearrange(
                      "(c p) o -> p (c o)", p=64),
                  allow_slow_non_contiguous=True, inc=s_cp, incv=16)
            em.dma("sync", pk4[:, 32:48],
                   v_i[0:1, :].rearrange("o (p c) -> (o p) c", p=64), s_cp)
        em.wait("gpsimd", s_cp, s_cp.n)
        cc("AllGather", ALU.bypass, G4, ktp, ktp_ag)
        cc("AllGather", ALU.bypass, G4, vp, vp_ag)
        if with_k0:
            cc("AllGather", ALU.bypass, G4, pk4, pk4_ag)
        em.barrier()
        _p[0] += 1
        fx.prefix = f"p{_p[0]}_"
        with ExitStack() as stack:
            sbn = lambda nm, sh, dt: stack.enter_context(
                nc.sbuf_tensor(fx.prefix + nm, sh, dt))
            cnd = sbn("cnd", [128, 8 * 128], BF16)
            acc = sbn("acc", [128, 8 * 128], BF16)
            vcnd = sbn("vcnd", [128, D], BF16)
            vacc = sbn("vacc", [128, D], BF16)
            fs = sbn("fs", [128, 8], F32)
            s_ld = em.sem("ld")
            s_dve = em.sem("dve")
            s_cp2 = em.sem(tag + "_bl")
            fsel = fx.get("fsel4", [128, 8], F32, "ExternalInput")
            em.dma("sync", fs[:, :], fsel, s_ld)
            for side in range(2):
                kcol = 128 if side == 0 else 0
                vrow = 128 if side == 0 else 0
                kdst = (ktx[:, 0:128] if side == 0
                        else ktx[:, 128 + TC:256 + TC])
                vdst = vbx[:, :].rearrange("p (t n) -> p t n", t=18)[
                    :, 0 if side == 0 else 17, :]
                for r in range(4):
                    em.dma("sync",
                           cnd[:, :].rearrange("p (c n) -> p c n", c=8),
                           ktp_ag[1024 * r:1024 * r + 1024,
                                  kcol:kcol + 128].rearrange(
                               "(c p) n -> p c n", p=128),
                           s_ld, waits=((s_dve, s_dve.n),))
                    em.dma("sync", vcnd[:, :],
                           vp_ag[256 * r + vrow:256 * r + vrow + 128, :],
                           s_ld)
                    col = fs[:, 4 * side + r:4 * side + r + 1]
                    if r == 0:
                        em.op("vector", "tensor_scalar_mul", acc[:, :],
                              cnd[:, :], col,
                              waits=((s_ld, s_ld.n), (s_cp2, s_cp2.n)))
                        em.op("vector", "tensor_scalar_mul", vacc[:, :],
                              vcnd[:, :], col, inc=s_dve)
                    else:
                        em.op("vector", "scalar_tensor_tensor", acc[:, :],
                              cnd[:, :], col, acc[:, :], ALU.mult, ALU.add,
                              waits=((s_ld, s_ld.n),))
                        em.op("vector", "scalar_tensor_tensor", vacc[:, :],
                              vcnd[:, :], col, vacc[:, :], ALU.mult,
                              ALU.add, inc=s_dve)
                em.dma("sync", kdst.rearrange("(c p) n -> p c n", p=128),
                       acc[:, :].rearrange("p (c n) -> p c n", c=8), s_cp2,
                       waits=((s_dve, s_dve.n),))
                em.dma("sync", vdst, vacc[:, :], s_cp2)
            if with_k0:
                k0c = it(f"{tag}_k0", [64, 16], BF16)
                q0c = it(f"{tag}_q0", [64, 16], BF16)
                v0r = it(f"{tag}_v0", [1, D], BF16)
                em.dma("sync", k0c, pk4_ag[0:64, 0:16], s_cp2)
                em.dma("sync", q0c, pk4_ag[0:64, 16:32], s_cp2)
                em.dma("sync",
                       v0r.rearrange("o (p c) -> (o p) c", p=64),
                       pk4_ag[0:64, 32:48], s_cp2)
            em.wait("gpsimd", s_cp2, s_cp2.n)
        em.barrier()

    attn_proj("xA", "l1", "l1a")
    band_assembly("l1", with_k0=True)
    it("l1_ctxu", [TC, D], BF16)
    it("l1_dens", [128, NT * 16], F32)
    it("l1_r0p", [1, 1040], F32)
    it("l1_vsum", [1, D], F32)
    it("l1_r0ps", [1, 1040], F32)
    it("l1_vsums", [1, D], F32)
    phase(build_l1band,
          {"qt": "l1_qt", "ktx": "l1_ktx", "k0col": "l1_k0",
           "q0col": "l1_q0", "vbx": "l1_vbx", "v0row": "l1_v0",
           "masks": "l1_masks", "bias0": "l1_bias0", "ident": "identb",
           "ctxu": "l1_ctxu", "dens": "l1_dens", "r0p": "l1_r0p",
           "vsum": "l1_vsum"})
    cc("AllReduce", ALU.add, G4, fx.io["l1_r0p"], fx.io["l1_r0ps"])
    cc("AllReduce", ALU.add, G4, fx.io["l1_vsum"], fx.io["l1_vsums"])
    em.barrier()
    phase(build_l1fin,
          {"x": "xA", "ctxu": "l1_ctxu", "dens": "l1_dens",
           "r0ps": "l1_r0ps", "vsums": "l1_vsums",
           "blockones": "l1_blockones", "r0flag": "l1_r0flag",
           "woT": "l1a_woT", "bo": "l1a_bo", "ident": "identb",
           "xo": "xB"}, nsum=1)
    l1ffn("xB", "xA", "p1b")

    def l2ffn(xin, xout):
        phase(build_l2ffn,
              {"x": xin, "gamma": "l2f_gamma", "beta": "l2f_beta",
               "q1": "l2f_q1", "s1": "l2f_s1", "b1": "l2f_b1",
               "q2": "l2f_q2", "s2": "l2f_s2", "b2": "l2f_b2",
               "ident": "identb", "xo": xout})

    l2ffn("xA", "xB")
    attn_proj("xB", "l2", "l2a", hm=True)
    it("l2_ktag", [4 * D, TC], BF16)
    it("l2_vag", [512, NT * D], BF16)
    cc("AllGather", ALU.bypass, G4, fx.io["l2_kt"], fx.io["l2_ktag"])
    cc("AllGather", ALU.bypass, G4, fx.io["l2_v"], fx.io["l2_vag"])
    em.barrier()
    phase(build_l2attn_flash,
          {"x": "xB", "qt": "l2_qt", "kt_ag": "l2_ktag", "v_ag": "l2_vag",
           "woT": "l2a_woT", "bo": "l2a_bo", "xo": "xA"})
    l2ffn("xA", "xB")

    def l3ffn(xin, xout):
        phase(build_l3ffn,
              {"x": xin, "rw": "l3f_rw", "rb": "l3f_rb", "we": "l3f_we",
               "be": "l3f_be", "cols8m": "l3f_cols8m",
               "identf": "identf", "xo": xout})

    l3ffn("xB", "xA")
    attn_proj("xA", "l3", "l3a")
    band_assembly("l3", with_k0=False)
    phase(build_l3band,
          {"x": "xA", "qt": "l3_qt", "ktx": "l3_ktx", "vbx": "l3_vbx",
           "masks": "l3_masks", "woT": "l3a_woT", "ident": "identb",
           "xo": "xB"})
    l3ffn("xB", "out")


def kernel_fused(**ins):
    x_parts = _shard_tokens(ins["L0_attn_x"])
    common = {
        "identb": IDENT_BF,
        "identf": np.eye(128, dtype=np.float32),
        "l0p_wqT": np.ascontiguousarray(ins["L0_attn_Wq"].T),
        "l0p_wkT": np.ascontiguousarray(ins["L0_attn_Wk"].T),
        "l0p_wvT": np.ascontiguousarray(ins["L0_attn_Wv"].T),
        "l0p_wqvT": np.ascontiguousarray(ins["L0_attn_Wqv"].T),
        "l0p_bqc": np.ascontiguousarray(ins["L0_attn_bq"].reshape(8, 128).T),
        "l0p_bkc": np.ascontiguousarray(ins["L0_attn_bk"].reshape(8, 128).T),
        "l0p_bv": ins["L0_attn_bv"].reshape(1, 512),
        "l0p_bqv": ins["L0_attn_bqv"].reshape(1, 512),
        "l0p_ws1t": np.ascontiguousarray(
            np.tile(ins["L0_attn_Ws"][0, :32], 16)).reshape(1, 512),
        "l0p_bs": ins["L0_attn_bs"].reshape(1, 1),
        "l0p_g1": ins["L0_attn_g1"].reshape(1, D),
        "l0p_b1": ins["L0_attn_b1"].reshape(1, D),
        "l0p_ws2t": np.ascontiguousarray(
            np.tile(ins["L0_attn_Ws"][0, 32:96], 8)).reshape(1, 512),
        "l0q_wfT": np.ascontiguousarray(ins["L0_attn_Wf"].T),
        "l0q_bf": ins["L0_attn_bf"].reshape(1, D),
        "l0q_g2": ins["L0_attn_g2"].reshape(1, D),
        "l0q_b2": ins["L0_attn_b2"].reshape(1, D),
        "l0q_wr1T": np.ascontiguousarray(ins["L0_attn_Wr1"].T),
        "l0q_br1c": np.ascontiguousarray(
            ins["L0_attn_br1"].reshape(16, 128).T),
        "l0q_wr2T": np.ascontiguousarray(ins["L0_attn_Wr2"].T),
        "l0q_br2": ins["L0_attn_br2"].reshape(1, D),
        "l1f_mu": np.ascontiguousarray(ins["L1_ffn_mu"].reshape(D, 256)),
        "l1f_scale": ins["L1_ffn_scale"].reshape(1, 1),
        "l1f_gamma": ins["L1_ffn_ln_gamma"].reshape(1, D),
        "l1f_beta": ins["L1_ffn_ln_beta"].reshape(1, D),
        "l1f_b1c": np.ascontiguousarray(
            ins["L1_ffn_b1"].reshape(16, 2, 128).transpose(2, 0, 1)
            .reshape(128, 32)),
        "l1f_w1l": np.ascontiguousarray(
            ins["L1_ffn_w1"].reshape(16, 8, 128, 256).transpose(2, 1, 0, 3)
            .reshape(128, 8 * 16 * 256)),
        "l1f_w2l": np.ascontiguousarray(
            ins["L1_ffn_w2"].reshape(16, 2, 128, 1024).transpose(2, 1, 0, 3)
            .reshape(128, 2 * 16 * 1024)),
        "l1f_b2": ins["L1_ffn_b2"],
        "l1f_sel": None,
        "l1a_wqT": np.ascontiguousarray(ins["L1_attn_wq"].T),
        "l1a_wkT": np.ascontiguousarray(ins["L1_attn_wk"].T),
        "l1a_wvT": np.ascontiguousarray(ins["L1_attn_wv"].T),
        "l1a_bqc": np.ascontiguousarray(ins["L1_attn_bq"].reshape(8, 128).T),
        "l1a_bkc": np.ascontiguousarray(ins["L1_attn_bk"].reshape(8, 128).T),
        "l1a_bv": ins["L1_attn_bv"].reshape(1, D),
        "l1a_woT": np.ascontiguousarray(ins["L1_attn_wo"].T),
        "l1a_bo": ins["L1_attn_bo"].reshape(1, D),
        "l1_blockones": None,
        "l2f_gamma": ins["L2_ffn_ln_gamma"].reshape(1, D),
        "l2f_beta": ins["L2_ffn_ln_beta"].reshape(1, D),
        "l2f_q1": ins["L2_ffn_fc1_q"],
        "l2f_s1": ins["L2_ffn_fc1_scales"],
        "l2f_b1": np.ascontiguousarray(
            ins["L2_ffn_fc1_bias"].reshape(32, 128).T),
        "l2f_q2": ins["L2_ffn_fc2_q"],
        "l2f_s2": ins["L2_ffn_fc2_scales"],
        "l2f_b2": ins["L2_ffn_fc2_bias"].reshape(1, D),
        "l2a_wqT": np.ascontiguousarray(ins["L2_attn_wq"].T),
        "l2a_wkT": np.ascontiguousarray(ins["L2_attn_wk"].T),
        "l2a_wvT": np.ascontiguousarray(ins["L2_attn_wv"].T),
        "l2a_bqc": np.ascontiguousarray(ins["L2_attn_bq"].reshape(8, 128).T),
        "l2a_bkc": np.ascontiguousarray(ins["L2_attn_bk"].reshape(8, 128).T),
        "l2a_bv": ins["L2_attn_bv"].reshape(1, D),
        "l2a_woT": np.ascontiguousarray(ins["L2_attn_wo"].T),
        "l2a_bo": ins["L2_attn_bo"].reshape(1, D),
        "l3f_rw": ins["L3_ffn_router_w"],
        "l3f_rb": ins["L3_ffn_router_b"].reshape(1, 8),
        "l3f_we": ins["L3_ffn_expert_w"],
        "l3f_be": ins["L3_ffn_expert_b"].reshape(1, 8 * D),
        "l3f_cols8m": np.ascontiguousarray(np.broadcast_to(
            (np.arange(8) - 100.0).astype(np.float32), (128, 8))),
        "l3a_wqT": np.ascontiguousarray(ins["L3_attn_Wq"].T),
        "l3a_wkT": np.ascontiguousarray(ins["L3_attn_Wk"].T),
        "l3a_wvT": np.ascontiguousarray(ins["L3_attn_Wv"].T),
        "l3a_bqc": np.zeros((128, 8), np.float32),
        "l3a_bkc": np.zeros((128, 8), np.float32),
        "l3a_bv": np.zeros((1, D), np.float32),
        "l3a_woT": np.ascontiguousarray(ins["L3_attn_Wo"].T),
    }
    bo1 = np.zeros((16, D), np.float32)
    for h in range(16):
        bo1[h, 64 * h:64 * h + 64] = 1
    common["l1_blockones"] = bo1
    sel = np.zeros((16, 256), dtype=BF)
    for e in range(16):
        sel[e, 16 * e:16 * e + 16] = 1
    common["l1f_sel"] = sel
    global _L1_MASKS, _L3_MASKS
    if _L1_MASKS is None:
        _L1_MASKS = [_l1_masks(c) for c in range(NCORES)]
    if _L3_MASKS is None:
        _L3_MASKS = [_l3_masks(c) for c in range(NCORES)]
    bias0s = []
    for c in range(NCORES):
        t0 = (c % 4) * TC
        j = (t0 + np.arange(TC)).astype(np.float64)
        bias0s.append(np.ascontiguousarray(
            _l1_pos_bias(j).reshape(NT, 128).T))

    w1 = np.asarray(ins["L0_ffn_w1"])
    w2 = np.asarray(ins["L0_ffn_w2"])
    maps = []
    for c in range(NCORES):
        m = dict(common)
        m["xA"] = x_parts[c]
        m["mb_w1"] = np.ascontiguousarray(w1[c])
        m["mb_w2"] = np.ascontiguousarray(w2[c])
        r = c % 4
        fs = np.zeros(8, np.float32)
        if r > 0:
            fs[r - 1] = 1.0
        if r < 3:
            fs[4 + r + 1] = 1.0
        m["fsel4"] = np.ascontiguousarray(
            np.broadcast_to(fs, (128, 8)).copy())
        m["l1_masks"] = _L1_MASKS[c]
        m["l1_bias0"] = bias0s[c]
        m["l1_r0flag"] = np.array(
            [[1.0, 0.0]] if c % 4 == 0 else [[0.0, 1.0]], np.float32)
        m["l3_masks"] = _L3_MASKS[c]
        maps.append(m)
    res = _launch("fused", build_fused, maps)
    return _gather_tokens(
        [np.asarray(res[c]["out"]) for c in range(NCORES)]).astype(np.float32)


def kernel(**inputs):
    return kernel_fused(**inputs)


def kernel_unfused(**inputs):
    ins = inputs
    x_parts = _shard_tokens(ins["L0_attn_x"])
    mh = make_mh(ins["L0_ffn_w1"], ins["L0_ffn_w2"])
    x_parts = step_ffn0(x_parts, mh)
    x_parts = step_l0attn(x_parts, ins)
    x_parts = step_ffn0(x_parts, mh)
    x_parts = step_l1ffn(x_parts, ins)
    x_parts = step_l1attn(x_parts, ins)
    x_parts = step_l1ffn(x_parts, ins)
    x_parts = step_l2ffn(x_parts, ins)
    x_parts = step_l2attn(x_parts, ins)
    x_parts = step_l2ffn(x_parts, ins)
    x_parts = step_l3ffn(x_parts, ins)
    x_parts = step_l3attn(x_parts, ins)
    x_parts = step_l3ffn(x_parts, ins)
    return _gather_tokens(x_parts).astype(np.float32)
